# version 05
# baseline (speedup 1.0000x reference)
import numpy as np
from contextlib import ExitStack
import concourse.bass as bass
import concourse.mybir as mybir
from concourse.bass import IndirectOffsetOnAxis
from concourse.bass_utils import run_bass_kernel_spmd
import ml_dtypes

F32 = mybir.dt.float32
BF16 = mybir.dt.bfloat16
U32 = mybir.dt.uint32
AF = mybir.ActivationFunctionType
ALU = mybir.AluOpType
AX = mybir.AxisListType
BF = ml_dtypes.bfloat16

RMAGIC = 1.5 * 2 ** 23

B, T, C, H, NH = 4, 4096, 1024, 16, 64
NCORE = 8
LCH = 128
SC = 512
EPS_LN = 1e-5
EPS_GN = 1e-5 * 64


class Sched:
    ENGS = ("sync", "tensor", "vector", "scalar", "gpsimd")

    def __init__(self, nc, ctx):
        self.nc = nc
        self.ctx = ctx
        self.q = {e: [] for e in self.ENGS}
        self.sems = {}
        self.cnt = {}

    def _sem(self, name):
        if name not in self.sems:
            self.sems[name] = self.ctx.enter_context(self.nc.semaphore(name))
            self.cnt[name] = 0
        return self.sems[name]

    ENGSEM = {"tensor": "mm", "vector": "vv", "scalar": "ss", "gpsimd": "gg", "sync": None}
    ALIAS = {"mmod": "ss", "gc": "gg"}

    def _canon(self, name):
        return self.ALIAS.get(name, name)

    FINAL_WAIT = ("ldr",)

    def _mkwl(self, names):
        wl = []
        for w in names:
            if self.cnt.get(w, 0) <= 0 and w not in self.FINAL_WAIT:
                continue
            self._sem(w)
            if w in self.FINAL_WAIT:
                wl.append((w, None))
            else:
                wl.append((w, self.cnt[w]))
        return wl

    def do(self, eng, fn, waits=(), inc=None, dma=False):
        names = [self._canon(w) for w in waits]
        own = self.ENGSEM[eng]
        if own is not None and own not in names:
            names.append(own)
        wl = self._mkwl(names)
        amt = 16 if dma else 1
        ih = None
        if inc is not None:
            inc = self._canon(inc) if not dma else inc
            if not dma and self.ENGSEM[eng] is not None:
                inc = self.ENGSEM[eng]
            ih = self._sem(inc)
            self.cnt[inc] += amt

        def run(e, wl=wl, fn=fn, ih=ih, amt=amt, S=self):
            for nm, v in wl:
                vv = S.cnt[nm] if v is None else v
                if vv > 0:
                    e.wait_ge(S.sems[nm], vv)
            inst = fn(e)
            if ih is not None:
                inst.then_inc(ih, amt)
        self.q[eng].append(run)

    def wait(self, eng, *names):
        names = [self._canon(w) for w in names]
        wl = self._mkwl(names)

        def run(e, wl=wl, S=self):
            for nm, v in wl:
                vv = S.cnt[nm] if v is None else v
                if vv > 0:
                    e.wait_ge(S.sems[nm], vv)
        self.q[eng].append(run)

    def finalize(self):
        with self.nc.Block() as block:
            for eng in self.ENGS:
                ops = list(self.q[eng])

                def make(ops):
                    def body(e):
                        for op in ops:
                            op(e)
                    return body
                getattr(block, eng)(make(ops))


def _mk(nc, ctx):
    sb = lambda n, s, d: ctx.enter_context(nc.sbuf_tensor(n, list(s), d))
    ps = lambda n, s, d: ctx.enter_context(nc.psum_tensor(n, list(s), d))
    return sb, ps


def host_consts():
    i = np.arange(LCH)
    return dict(
        tril_incl=(i[:, None] <= i[None, :]).astype(np.float32),
        mask_su=(i[:, None] < i[None, :]).astype(np.float32),
        mask_sui=(i[:, None] <= i[None, :]).astype(np.float32),
        idn128=np.eye(128, dtype=np.float32),
        idn64b=np.eye(64, dtype=np.float32).astype(BF),
        ones128=np.ones((128, 1), dtype=np.float32),
        ones64=np.ones((64, 1), dtype=np.float32),
        onesr=np.ones((1, 128), dtype=np.float32),
        ones11=np.ones((1, 1), dtype=np.float32),
    )


def build_A(dbg=False):
    nc = bass.Bass("TRN2", target_bir_lowering=False, debug=False)
    dram = lambda n, s, d, k: nc.dram_tensor(n, list(s), d, kind=k).ap()
    NSC = T // SC
    NCH = SC // LCH
    xT = dram("xT", (8, 128, T), F32, "ExternalInput")
    wmod = dram("wmod", (1024, 6144), F32, "ExternalInput")
    c_col = dram("c_col", (128, 8), F32, "ExternalInput")
    tm_rows = dram("tm_rows", (1, 5120), F32, "ExternalInput")
    tmx_col = dram("tmx_col", (128, 8), F32, "ExternalInput")
    maa_w1 = dram("maa_w1", (8, 128, 160), F32, "ExternalInput")
    maa_w2 = dram("maa_w2", (5, 32, 1024), F32, "ExternalInput")
    td_w1 = dram("td_w1", (8, 128, 64), F32, "ExternalInput")
    td_w2 = dram("td_w2", (64, 512), F32, "ExternalInput")
    tdec = dram("tdec", (128, 4), F32, "ExternalInput")
    Wr = dram("Wr", (8, 128, 512), F32, "ExternalInput")
    Wk = dram("Wk", (8, 128, 512), F32, "ExternalInput")
    Wv = dram("Wv", (8, 128, 512), F32, "ExternalInput")
    Wg = dram("Wg", (8, 128, 512), F32, "ExternalInput")
    u_col = dram("u_col", (128, 8), F32, "ExternalInput")
    lnw_r = dram("lnw_r", (1, 512), F32, "ExternalInput")
    lnb_r = dram("lnb_r", (1, 512), F32, "ExternalInput")
    ctr = dram("ctrl", (128, 128), F32, "ExternalInput")
    cmask = dram("cmask", (128, 128), F32, "ExternalInput")
    cidn = dram("cidn", (128, 128), F32, "ExternalInput")
    cidnb = dram("cidnb", (128, 64), BF16, "ExternalInput")
    cones = dram("cones", (128, 1), F32, "ExternalInput")
    conesr = dram("conesr", (1, 128), F32, "ExternalInput")
    cones64 = dram("cones64", (64, 1), F32, "ExternalInput")
    cones11 = dram("cones11", (1, 1), F32, "ExternalInput")
    yg_o = dram("yg_o", (4, 128, T), BF16, "ExternalOutput")
    mods_o = dram("mods_o", (128, 48), F32, "ExternalOutput")
    dbg_map = {}
    if dbg:
        dbg_h = dram("dbg_h", (128, 8 * (SC + 1)), BF16, "ExternalOutput")
        dbg_r = dram("dbg_r", (128, 4 * SC), BF16, "ExternalOutput")
        dbg_k = dram("dbg_k", (128, 4 * SC), BF16, "ExternalOutput")
        dbg_v = dram("dbg_v", (128, 4 * SC), BF16, "ExternalOutput")
        dbg_g = dram("dbg_g", (128, 4 * SC), BF16, "ExternalOutput")
        dbg_ew = dram("dbg_ew", (128, 4 * SC), F32, "ExternalOutput")
        dbg_xk = dram("dbg_xk", (128, 5 * 8 * SC), BF16, "ExternalOutput")
        dbg_Y = dram("dbg_Y", (128, 512), F32, "ExternalOutput")
        dbg_qt = dram("dbg_qt", (128, 512), BF16, "ExternalOutput")
        dbg_kt = dram("dbg_kt", (128, 512), BF16, "ExternalOutput")
        dbg_li = dram("dbg_li", (128, 512), F32, "ExternalOutput")
        dbg_ktT = dram("dbg_ktT", (128, 512), BF16, "ExternalOutput")
        dbg_vT = dram("dbg_vT", (128, 512), BF16, "ExternalOutput")
        dbg_S = dram("dbg_S", (128, 512), F32, "ExternalOutput")
        dbg_yn = dram("dbg_yn", (128, 512), F32, "ExternalOutput")
        dbg_gn = dram("dbg_gn", (128, 8), F32, "ExternalOutput")
        dbg_gn2 = dram("dbg_gn2", (128, 8), F32, "ExternalOutput")
        dbg_gnA = dram("dbg_gnA", (128, 8), F32, "ExternalOutput")
        dbg_ysq = dram("dbg_ysq", (128, 512), F32, "ExternalOutput")
        dbg_gnB = dram("dbg_gnB", (128, 8), F32, "ExternalOutput")
        dbg_map.update(dbg_h=dbg_h, dbg_r=dbg_r, dbg_k=dbg_k, dbg_v=dbg_v,
                       dbg_g=dbg_g, dbg_ew=dbg_ew, dbg_xk=dbg_xk, dbg_Y=dbg_Y,
                       dbg_qt=dbg_qt, dbg_kt=dbg_kt, dbg_li=dbg_li,
                       dbg_ktT=dbg_ktT, dbg_vT=dbg_vT, dbg_S=dbg_S,
                       dbg_yn=dbg_yn, dbg_gn=dbg_gn, dbg_gn2=dbg_gn2,
                       dbg_gnA=dbg_gnA, dbg_gnB=dbg_gnB, dbg_ysq=dbg_ysq)

    ctx = ExitStack()
    sb, ps = _mk(nc, ctx)
    s = Sched(nc, ctx)

    t_wr = sb("t_wr", (128, 8 * 512), BF16)
    t_wk = sb("t_wk", (128, 8 * 512), BF16)
    t_wv = sb("t_wv", (128, 8 * 512), BF16)
    t_wg = sb("t_wg", (128, 8 * 512), BF16)
    t_w1 = sb("t_w1", (128, 8 * 160), BF16)
    t_w2 = sb("t_w2", (32, 5 * 1024), BF16)
    t_tdw1 = sb("t_tdw1", (128, 8 * 64), BF16)
    t_tdw2 = sb("t_tdw2", (64, 512), BF16)
    t_ccol = sb("t_ccol", (128, 8), F32)
    t_tm = sb("t_tm", (1, 5120), BF16)
    t_tmxc = sb("t_tmxc", (128, 8), F32)
    t_tdec = sb("t_tdec", (128, 4), F32)
    t_ucol = sb("t_ucol", (128, 8), F32)
    t_mods = sb("t_mods", (128, 48), F32)
    t_onepca = sb("t_onepca", (128, 8), F32)
    t_tril = sb("t_tril", (128, 128), F32)
    t_masku = sb("t_masku", (128, 128), F32)
    t_idn = sb("t_idn", (128, 128), F32)
    t_idnb = sb("t_idnb", (128, 64), BF16)
    t_ones = sb("t_ones", (128, 1), F32)
    t_onesr = sb("t_onesr", (1, 128), F32)
    t_ones64 = sb("t_ones64", (64, 1), F32)
    t_ones11 = sb("t_ones11", (1, 1), F32)
    t_onesrow = sb("t_onesrow", (1, SC), BF16)
    t_epsln = sb("t_epsln", (128, 1), F32)
    t_epsgn = sb("t_epsgn", (128, 1), F32)
    t_lnr = sb("t_lnr", (1, 512), F32)
    t_lbr = sb("t_lbr", (1, 512), F32)
    t_lnwb = sb("t_lnwb", (128, 512), F32)
    t_lnbb = sb("t_lnbb", (128, 512), F32)
    t_x = sb("t_x", (128, 8 * SC), F32)
    t_sq = sb("t_sq", (128, SC), F32)
    t_sq2 = sb("t_sq2", (128, SC), F32)
    t_stats = sb("t_stats", (1, SC), F32)
    t_mub = sb("t_mub", (128, SC), F32)
    t_rsb = sb("t_rsb", (128, SC), F32)
    t_h = sb("t_h", (128, 8 * (SC + 1)), BF16)
    t_z = sb("t_z", (128, 8 * SC), BF16)
    t_t5 = sb("t_t5", (32, 5 * SC), BF16)
    t_t6b = sb("t_t6b", (64, SC), BF16)
    t_xk5 = sb("t_xk5", (128, 5 * 8 * SC), BF16)
    if 5 * 8 * SC >= 20480:
        t_wmodrow = t_xk5[:, 0:12288].bitcast(F32)
    else:
        t_wmodrow = sb("t_wmodalt", (128, 6144), F32)[:]
    t_xxx = t_xk5[:, 0:8 * SC]
    t_r = sb("t_r", (128, 4 * SC), BF16)
    t_k = sb("t_k", (128, 4 * SC), BF16)
    t_v = sb("t_v", (128, 4 * SC), BF16)
    t_gs = sb("t_gs", (128, 4 * SC), BF16)
    t_ew = sb("t_ew", (128, 4 * SC), F32)
    t_ewT = sb("t_ewT", (128, 512), F32)
    t_liT = sb("t_liT", (128, 512), F32)
    t_liC = sb("t_liC", (128, 512), F32)
    t_ysq = t_liC
    t_pq = sb("t_pq", (128, 512), F32)
    t_ek = sb("t_ek", (128, 512), F32)
    t_al = sb("t_al", (128, 4), F32)
    t_qt = sb("t_qt", (128, 512), BF16)
    t_kt = sb("t_kt", (128, 512), BF16)
    t_ktT = sb("t_ktT", (128, 8 * 64), BF16)
    t_vT = sb("t_vT", (128, 8 * 64), BF16)
    t_gm = sb("t_gm", (128, 128), BF16)
    t_ruk = sb("t_ruk", (128, 128), F32)
    t_drow = sb("t_drow", (1, 128), F32)
    t_dcol = sb("t_dcol", (128, 1), F32)
    t_S = sb("t_S", (128, 8 * 64), F32)
    t_Sb = sb("t_Sb", (128, 8 * 64), BF16)
    t_Y = sb("t_Y", (128, 512), F32)
    t_gn = sb("t_gn", (128, 8), F32)
    t_gn2 = sb("t_gn2", (128, 8), F32)
    t_gn3 = sb("t_gn3", (128, 8), F32)
    t_yn = sb("t_yn", (128, 512), F32)
    t_yg = sb("t_yg", (128, 4 * SC), BF16)

    p0 = ps("p0", (128, 512), F32)
    p1 = ps("p1", (128, 512), F32)
    p2 = ps("p2", (128, 512), F32)
    p3 = ps("p3", (128, 512), F32)
    p4 = ps("p4", (128, 512), F32)
    p5 = ps("p5", (128, 512), F32)
    p6b = ps("p6b", (128, 1024), BF16)
    p7 = ps("p7", (128, 512), F32)

    def ld(dst, src):
        s.do("sync", lambda e, d=dst, sr=src: e.dma_start(out=d, in_=sr), inc="ldr", dma=True)

    def gld(dst, src):
        s.do("gpsimd", lambda e, d=dst, sr=src: e.dma_start(out=d, in_=sr), inc="gld", dma=True)

    ld(t_tril[:], ctr); ld(t_masku[:], cmask); ld(t_idn[:], cidn); ld(t_idnb[:], cidnb)
    ld(t_ones[:], cones); ld(t_onesr[:], conesr); ld(t_ones64[:], cones64); ld(t_ones11[:], cones11)
    ld(t_ccol[:], c_col); ld(t_tmxc[:], tmx_col)
    ld(t_tdec[:], tdec)
    ld(t_ucol[:], u_col)
    ld(t_lnr[:], lnw_r); ld(t_lbr[:], lnb_r)
    gld(t_wr[:].rearrange("p (k m) -> p k m", k=8), Wr.rearrange("k p m -> p k m"))
    gld(t_wk[:].rearrange("p (k m) -> p k m", k=8), Wk.rearrange("k p m -> p k m"))
    gld(t_wv[:].rearrange("p (k m) -> p k m", k=8), Wv.rearrange("k p m -> p k m"))
    gld(t_wg[:].rearrange("p (k m) -> p k m", k=8), Wg.rearrange("k p m -> p k m"))
    gld(t_w1[:].rearrange("p (k m) -> p k m", k=8), maa_w1.rearrange("k p m -> p k m"))
    gld(t_w2[:].rearrange("e (f m) -> e f m", f=5), maa_w2.rearrange("f e m -> e f m"))
    gld(t_tdw1[:].rearrange("p (k m) -> p k m", k=8), td_w1.rearrange("k p m -> p k m"))
    gld(t_tdw2[:], td_w2)
    gld(t_tm[:], tm_rows)

    s.do("vector", lambda e: e.memset(t_S[:], 0.0), inc="vv")
    s.do("vector", lambda e: e.memset(t_Sb[:], 0.0), inc="vv")
    s.do("vector", lambda e: e.memset(t_h[:], 0.0), inc="vv")
    s.do("vector", lambda e: e.memset(t_onesrow[:], 1.0), inc="vv")
    s.do("vector", lambda e: e.memset(t_epsln[:], EPS_LN), inc="vv")
    s.do("vector", lambda e: e.memset(t_epsgn[:], EPS_GN), inc="vv")

    s.do("scalar", lambda e: e.activation(out=t_ccol[:], in_=t_ccol[:], func=AF.Silu),
         waits=("ldr",), inc="ss")
    if 5 * 8 * SC >= 20480:
        t_wm = t_xk5[:, 12288:12288 + 8192].bitcast(F32)
        t_modrow = t_xk5[:, 0:12288].bitcast(F32)[0:1, 0:6144]
    else:
        t_wm = sb("t_wmalt", (128, 4096), F32)[:]
        t_modrow = t_wmodrow[0:1, 0:6144]
    for cchunk in range(12):
        s.do("sync", lambda e, cc=cchunk: e.dma_start(
            out=t_wm[:].rearrange("p (kk m) -> p kk m", kk=8),
            in_=wmod[:, cc * 512:(cc + 1) * 512].rearrange("(kk p) m -> p kk m", p=128)),
            waits=("mmod",) if cchunk > 0 else (), inc="ldw", dma=True)
        s.wait("tensor", "ldw", "ss")
        for kk in range(8):
            s.do("tensor", lambda e, kk=kk: e.matmul(
                out=p1[0:1, 0:512], lhsT=t_ccol[:, kk:kk + 1],
                rhs=t_wm[:, kk * 512:(kk + 1) * 512], start=(kk == 0), stop=(kk == 7)),
                inc="mm" if kk == 7 else None)
        s.do("scalar", lambda e, cc=cchunk: e.copy(
            out=t_modrow[0:1, cc * 512:(cc + 1) * 512], in_=p1[0:1, 0:512]),
            waits=("mm",), inc="mmod")
    s.wait("tensor", "mmod")
    for col in range(48):
        s.do("tensor", lambda e, col=col: e.matmul(
            out=p1[:, col:col + 1], lhsT=t_modrow[0:1, col * 128:(col + 1) * 128],
            rhs=t_ones11[:], start=True, stop=True), inc="mm" if col == 47 else None)
    s.do("scalar", lambda e: e.copy(out=t_mods[:], in_=p1[:, 0:48]), waits=("mm",), inc="ss")
    for c0 in (16, 40):
        s.do("scalar", lambda e, c0=c0: e.activation(
            out=t_sq[:, 0:8], in_=t_mods[:, c0:c0 + 8], func=AF.Abs), waits=("ss", "vv"), inc="ss")
        s.do("vector", lambda e, c0=c0: e.tensor_scalar(
            out=t_sq[:, 0:8], in0=t_sq[:, 0:8], scalar1=0.25, scalar2=None,
            op0=ALU.max), waits=("ss",), inc="vv")
        s.do("scalar", lambda e, c0=c0: e.activation(
            out=t_sq[:, 8:16], in_=t_mods[:, c0:c0 + 8], func=AF.Sign), waits=("vv",), inc="ss")
        s.do("vector", lambda e, c0=c0: e.tensor_tensor(
            out=t_mods[:, c0:c0 + 8], in0=t_sq[:, 0:8], in1=t_sq[:, 8:16],
            op=ALU.mult), waits=("ss",), inc="vv")
    s.do("gpsimd", lambda e: e.dma_start(out=mods_o, in_=t_mods[:]),
         waits=("vv",), inc="gst", dma=True)
    s.do("vector", lambda e: e.tensor_scalar(
        out=t_onepca[:], in0=t_mods[:, 8:16], scalar1=1.0, scalar2=None, op0=ALU.add),
        waits=("vv",), inc="vv")
    s.wait("tensor", "ldr", "ss")
    s.do("tensor", lambda e: e.matmul(out=p2[:], lhsT=t_onesr[:], rhs=t_lnr[:],
                                      start=True, stop=True), inc="mm")
    s.do("scalar", lambda e: e.copy(out=t_lnwb[:], in_=p2[:]), waits=("mm",), inc="ss")
    s.do("tensor", lambda e: e.matmul(out=p3[:], lhsT=t_onesr[:], rhs=t_lbr[:],
                                      start=True, stop=True), inc="mm")
    s.do("scalar", lambda e: e.copy(out=t_lnbb[:], in_=p3[:]), waits=("mm",), inc="ss")

    s.wait("tensor", "gld", "vv", "ss")

    for sc in range(NSC):
        s.do("sync", lambda e, sc=sc: e.dma_start(
            out=t_x[:].rearrange("p (a t) -> p a t", a=8), in_=xT[:, :, sc * SC:(sc + 1) * SC].rearrange("a p t -> p a t")),
            waits=("vv", "ss", "mm", "gg", "gst"), inc="ldx", dma=True)
        s.wait("tensor", "ldx")
        s.wait("scalar", "ldx")
        for a in range(8):
            s.do("tensor", lambda e, a=a: e.matmul(
                out=p0[0:1, 0:SC], lhsT=t_ones[:], rhs=t_x[:, a * SC:(a + 1) * SC],
                start=(a == 0), stop=(a == 7)), inc="mm" if a == 7 else None)
        for a in range(8):
            buf = t_sq if a % 2 == 0 else t_sq2
            s.do("scalar", lambda e, a=a, buf=buf: e.activation(
                out=buf[:], in_=t_x[:, a * SC:(a + 1) * SC], func=AF.Square),
                waits=("mm",), inc="ss")
            s.wait("tensor", "ss")
            s.do("tensor", lambda e, a=a, buf=buf: e.matmul(
                out=p1[0:1, 0:SC], lhsT=t_ones[:], rhs=buf[:],
                start=(a == 0), stop=(a == 7)), inc="mm")
        s.do("scalar", lambda e: e.activation(out=t_stats[0:1, 0:SC], in_=p0[0:1, 0:SC],
                                              func=AF.Copy, scale=1.0 / 1024),
             waits=("mm",), inc="ss")
        s.do("scalar", lambda e: e.activation(out=t_lnr[0:1, 0:SC], in_=p1[0:1, 0:SC],
                                              func=AF.Copy, scale=1.0 / 1024), inc="ss")
        s.do("vector", lambda e: e.tensor_tensor(
            out=t_lbr[0:1, 0:SC], in0=t_stats[0:1, 0:SC], in1=t_stats[0:1, 0:SC],
            op=ALU.mult), waits=("ss",), inc="vv")
        s.do("vector", lambda e: e.tensor_tensor(
            out=t_lnr[0:1, 0:SC], in0=t_lnr[0:1, 0:SC],
            in1=t_lbr[0:1, 0:SC], op=ALU.subtract), inc="vv")
        s.do("scalar", lambda e: e.activation(out=t_lbr[0:1, 0:SC],
                                              in_=t_lnr[0:1, 0:SC], func=AF.Sqrt,
                                              bias=t_epsln[0:1, :]), waits=("vv",), inc="ss")
        s.do("vector", lambda e: e.reciprocal(out=t_lnr[0:1, 0:SC],
                                              in_=t_lbr[0:1, 0:SC]),
             waits=("ss",), inc="vv")
        s.wait("tensor", "vv")
        s.do("tensor", lambda e: e.matmul(out=p2[:, 0:SC], lhsT=t_onesr[:],
                                          rhs=t_stats[0:1, 0:SC], start=True, stop=True), inc="mm")
        s.do("tensor", lambda e: e.matmul(out=p3[:, 0:SC], lhsT=t_onesr[:],
                                          rhs=t_lnr[0:1, 0:SC], start=True, stop=True), inc="mm")
        s.do("scalar", lambda e: e.copy(out=t_mub[:], in_=p2[:, 0:SC]), waits=("mm",), inc="ss")
        s.do("scalar", lambda e: e.copy(out=t_rsb[:], in_=p3[:, 0:SC]), inc="ss")
        for a in range(8):
            s.do("vector", lambda e, a=a: e.tensor_copy(
                out=t_h[:, a * (SC + 1):a * (SC + 1) + 1],
                in_=t_h[:, a * (SC + 1) + SC:a * (SC + 1) + SC + 1]),
                waits=("ss",) if a == 0 else (), inc="vv")
        for a in range(8):
            buf = t_sq if a % 2 == 0 else t_sq2
            s.do("vector", lambda e, a=a, buf=buf: e.tensor_tensor(
                out=buf[:], in0=t_x[:, a * SC:(a + 1) * SC], in1=t_mub[:],
                op=ALU.subtract), waits=("ss",), inc="vv")
            s.do("vector", lambda e, a=a, buf=buf: e.tensor_tensor(
                out=buf[:], in0=buf[:], in1=t_rsb[:], op=ALU.mult), inc="vv")
            s.do("vector", lambda e, a=a, buf=buf: e.tensor_scalar(
                out=t_h[:, a * (SC + 1) + 1:a * (SC + 1) + 1 + SC], in0=buf[:],
                scalar1=t_onepca[:, a:a + 1], scalar2=t_mods[:, a:a + 1],
                op0=ALU.mult, op1=ALU.add), inc="vv")
        for a in range(8):
            hc = t_h[:, a * (SC + 1) + 1:a * (SC + 1) + 1 + SC]
            hs = t_h[:, a * (SC + 1):a * (SC + 1) + SC]
            s.do("vector", lambda e, hc=hc, hs=hs, a=a: e.tensor_tensor(
                out=t_z[:, a * SC:(a + 1) * SC], in0=hs, in1=hc, op=ALU.subtract),
                waits=("ss",), inc="vv")
            s.do("vector", lambda e, hc=hc, a=a: e.scalar_tensor_tensor(
                out=t_xxx[:, a * SC:(a + 1) * SC], in0=t_z[:, a * SC:(a + 1) * SC],
                scalar=t_tmxc[:, a:a + 1], in1=hc, op0=ALU.mult, op1=ALU.add), inc="vv")
        s.wait("tensor", "vv")
        for (m0, m1, pp) in ((0, 96, p4), (96, 160, p5)):
            for kk in range(8):
                s.do("tensor", lambda e, kk=kk, m0=m0, m1=m1, pp=pp: e.matmul(
                    out=pp[0:m1 - m0, 0:SC], lhsT=t_w1[:, kk * 160 + m0:kk * 160 + m1],
                    rhs=t_xxx[:, kk * SC:(kk + 1) * SC],
                    start=(kk == 0), stop=(kk == 7)), inc="mm" if kk == 7 else None)
        for f in range(5):
            pp, r0 = (p4, f * 32) if f < 3 else (p5, (f - 3) * 32)
            s.do("scalar", lambda e, f=f, pp=pp, r0=r0: e.activation(
                out=t_t5[:, f * SC:(f + 1) * SC], in_=pp[r0:r0 + 32, 0:SC],
                func=AF.Tanh), waits=("mm",), inc="ss")
        s.wait("tensor", "ss")
        for f in range(5):
            for a in range(8):
                s.do("tensor", lambda e, f=f, a=a: e.matmul(
                    out=p2[:, 0:SC], lhsT=t_tm[0:1, (f * 8 + a) * 128:(f * 8 + a + 1) * 128],
                    rhs=t_onesrow[:], start=True, stop=False),
                    waits=("vv", "gg")),
                s.do("tensor", lambda e, f=f, a=a: e.matmul(
                    out=p2[:, 0:SC], lhsT=t_w2[:, f * 1024 + a * 128:f * 1024 + (a + 1) * 128],
                    rhs=t_t5[:, f * SC:(f + 1) * SC], start=False, stop=True), inc="mm")
                buf = t_sq if (f * 8 + a) % 2 == 0 else t_sq2
                s.do("vector", lambda e, f=f, a=a, buf=buf: e.scalar_tensor_tensor(
                    out=buf[:], in0=t_z[:, a * SC:(a + 1) * SC], scalar=0.0,
                    in1=p2[:, 0:SC], op0=ALU.add, op1=ALU.mult), waits=("mm", "gg"), inc="vv")
                eng = "gpsimd" if (a % 2 == 1) else "vector"
                hc = t_h[:, a * (SC + 1) + 1:a * (SC + 1) + 1 + SC]
                s.do(eng, lambda e, f=f, a=a, hc=hc, buf=buf: e.tensor_tensor(
                    out=t_xk5[:, (f * 8 + a) * SC:(f * 8 + a + 1) * SC],
                    in0=buf[:], in1=hc, op=ALU.add),
                    waits=("vv",), inc="gg" if eng == "gpsimd" else "vv")
        proj = [(t_wr, 3, t_r, None), (t_wk, 1, t_k, None),
                (t_wv, 2, t_v, None), (t_wg, 4, t_gs, AF.Silu)]
        s.wait("tensor", "vv", "gg")
        for wt, slot, dst, act in proj:
            for mt in range(4):
                pp = (p4, p5)[mt % 2]
                for kk in range(8):
                    s.do("tensor", lambda e, wt=wt, slot=slot, mt=mt, kk=kk, pp=pp: e.matmul(
                        out=pp[:, 0:SC],
                        lhsT=wt[:, kk * 512 + mt * 128:kk * 512 + mt * 128 + 128],
                        rhs=t_xk5[:, (slot * 8 + kk) * SC:(slot * 8 + kk + 1) * SC],
                        start=(kk == 0), stop=(kk == 7)),
                        waits=("ss",) if kk == 0 else (), inc="mm" if kk == 7 else None)
                s.do("scalar", lambda e, dst=dst, mt=mt, act=act, pp=pp: e.activation(
                    out=dst[:, mt * SC:(mt + 1) * SC], in_=pp[:, 0:SC],
                    func=act if act else AF.Copy), waits=("mm",), inc="ss")
        for kk in range(8):
            s.do("tensor", lambda e, kk=kk: e.matmul(
                out=p2[0:64, 0:SC], lhsT=t_tdw1[:, kk * 64:(kk + 1) * 64],
                rhs=t_xk5[:, kk * SC:(kk + 1) * SC], start=(kk == 0), stop=(kk == 7)),
                waits=("vv",) if kk == 0 else (), inc="mm" if kk == 7 else None)
        s.do("scalar", lambda e: e.activation(out=t_t6b[:], in_=p2[0:64, 0:SC],
                                              func=AF.Tanh), waits=("mm",), inc="ss")
        s.wait("tensor", "ss")
        for mt in range(4):
            s.do("tensor", lambda e, mt=mt: e.matmul(
                out=p3[:, 0:SC], lhsT=t_tdw2[:, mt * 128:(mt + 1) * 128],
                rhs=t_t6b[:], start=True, stop=True), waits=("ss",), inc="mm")
            s.do("scalar", lambda e, mt=mt: e.activation(
                out=t_ew[:, mt * SC:(mt + 1) * SC], in_=p3[:, 0:SC], func=AF.Exp,
                bias=t_tdec[:, mt:mt + 1]), waits=("mm",), inc="ss")
        if dbg and sc == 0:
            for nm, tt in (("dbg_h", t_h), ("dbg_r", t_r), ("dbg_k", t_k), ("dbg_v", t_v),
                           ("dbg_g", t_gs), ("dbg_ew", t_ew), ("dbg_xk", t_xk5)):
                s.do("gpsimd", lambda e, nm=nm, tt=tt: e.dma_start(
                    out=dbg_map[nm], in_=tt[:]),
                    waits=("ss", "vv", "gg"), inc="gst", dma=True)
        for ch in range(NCH):
            tof = ch * LCH
            for a in range(4):
                s.do("tensor", lambda e, a=a, tof=tof: e.transpose(
                    out=p2[:, 0:128], in_=t_ew[:, a * SC + tof:a * SC + tof + LCH],
                    identity=t_idn[:]), waits=("ss", "vv"), inc="mm")
                s.do("scalar", lambda e, a=a: e.copy(
                    out=t_ewT[:, a * 128:(a + 1) * 128], in_=p2[:, 0:128]),
                    waits=("mm",), inc="ss")
            s.do("tensor", lambda e: e.matmul(out=p3[:, 0:512], lhsT=t_tril[:],
                                              rhs=t_ewT[:], start=True, stop=True),
                 waits=("ss",), inc="mm")
            s.do("scalar", lambda e: e.copy(out=t_liT[:], in_=p3[:, 0:512]),
                 waits=("mm",), inc="ss")
            for a in range(4):
                s.do("tensor", lambda e, a=a: e.transpose(
                    out=p2[:, 0:128], in_=t_liT[:, a * 128:(a + 1) * 128],
                    identity=t_idn[:]), waits=("ss",), inc="mm")
                s.do("scalar", lambda e, a=a: e.copy(
                    out=t_liC[:, a * 128:(a + 1) * 128], in_=p2[:, 0:128]),
                    waits=("mm",), inc="ss")
            for a in range(4):
                s.do("vector", lambda e, a=a, tof=tof: e.tensor_tensor(
                    out=t_pq[:, a * 128:(a + 1) * 128], in0=t_liC[:, a * 128:(a + 1) * 128],
                    in1=t_ew[:, a * SC + tof:a * SC + tof + LCH], op=ALU.subtract),
                    waits=("ss",), inc="vv")
                s.do("scalar", lambda e, a=a: e.activation(
                    out=t_pq[:, a * 128:(a + 1) * 128], in_=t_pq[:, a * 128:(a + 1) * 128],
                    func=AF.Exp, scale=-1.0), waits=("vv",), inc="ss")
                s.do("scalar", lambda e, a=a: e.activation(
                    out=t_ek[:, a * 128:(a + 1) * 128], in_=t_liC[:, a * 128:(a + 1) * 128],
                    func=AF.Exp), inc="ss")
                s.do("scalar", lambda e, a=a: e.activation(
                    out=t_al[:, a:a + 1], in_=t_liC[:, a * 128 + 127:a * 128 + 128],
                    func=AF.Exp, scale=-1.0), inc="ss")
            for a in range(4):
                s.do("vector", lambda e, a=a, tof=tof: e.tensor_tensor(
                    out=t_qt[:, a * 128:(a + 1) * 128],
                    in0=t_r[:, a * SC + tof:a * SC + tof + LCH],
                    in1=t_pq[:, a * 128:(a + 1) * 128], op=ALU.mult),
                    waits=("ss",), inc="vv")
                s.do("vector", lambda e, a=a, tof=tof: e.tensor_tensor(
                    out=t_kt[:, a * 128:(a + 1) * 128],
                    in0=t_k[:, a * SC + tof:a * SC + tof + LCH],
                    in1=t_ek[:, a * 128:(a + 1) * 128], op=ALU.mult), inc="vv")
            for h in range(8):
                a, base = h // 2, (h % 2) * 64
                s.do("tensor", lambda e, a=a, base=base: e.transpose(
                    out=p6b[:, 0:64], in_=t_kt[base:base + 64, a * 128:(a + 1) * 128],
                    identity=t_idnb[base:base + 64, :]), waits=("vv",), inc="mm")
                s.do("vector", lambda e, h=h: e.tensor_copy(
                    out=t_ktT[:, h * 64:(h + 1) * 64], in_=p6b[:, 0:64]),
                    waits=("mm",), inc="vv")
                s.do("tensor", lambda e, a=a, base=base, tof=tof: e.transpose(
                    out=p6b[:, 64:128], in_=t_v[base:base + 64, a * SC + tof:a * SC + tof + LCH],
                    identity=t_idnb[base:base + 64, :]), waits=("vv",), inc="mm")
                s.do("vector", lambda e, h=h: e.tensor_copy(
                    out=t_vT[:, h * 64:(h + 1) * 64], in_=p6b[:, 64:128]),
                    waits=("mm",), inc="vv")
            for h in range(8):
                a, base = h // 2, (h % 2) * 64
                qh = t_qt[base:base + 64, a * 128:(a + 1) * 128]
                kh = t_kt[base:base + 64, a * 128:(a + 1) * 128]
                rh = t_r[base:base + 64, a * SC + tof:a * SC + tof + LCH]
                kfh = t_k[base:base + 64, a * SC + tof:a * SC + tof + LCH]
                s.do("tensor", lambda e, kh=kh, qh=qh: e.matmul(
                    out=p2[:, 0:128], lhsT=kh, rhs=qh, start=True, stop=True),
                    waits=("vv",), inc="mm")
                s.do("vector", lambda e: e.tensor_tensor(
                    out=t_gm[:], in0=p2[:, 0:128], in1=t_masku[:], op=ALU.mult),
                    waits=("mm",), inc="vv")
                s.do("vector", lambda e, rh=rh, kfh=kfh, base=base: e.tensor_tensor(
                    out=t_ruk[base:base + 64, :], in0=rh, in1=kfh, op=ALU.mult), inc="vv")
                s.do("vector", lambda e, h=h, base=base: e.tensor_scalar(
                    out=t_ruk[base:base + 64, :], in0=t_ruk[base:base + 64, :],
                    scalar1=t_ucol[base:base + 64, h:h + 1], scalar2=None,
                    op0=ALU.mult), inc="vv")
                s.do("tensor", lambda e, base=base: e.matmul(
                    out=p3[0:1, 0:128], lhsT=t_ones[base:base + 64, :],
                    rhs=t_ruk[base:base + 64, :],
                    start=True, stop=True), waits=("vv",), inc="mm")
                s.do("scalar", lambda e: e.copy(out=t_drow[:], in_=p3[0:1, 0:128]),
                     waits=("mm",), inc="ss")
                s.do("tensor", lambda e: e.matmul(
                    out=p3[:, 128:129], lhsT=t_drow[:], rhs=t_ones11[:],
                    start=True, stop=True), waits=("ss",), inc="mm")
                s.do("scalar", lambda e: e.copy(out=t_dcol[:], in_=p3[:, 128:129]),
                     waits=("mm",), inc="ss")
                s.do("tensor", lambda e, h=h: e.matmul(
                    out=p5[:, 0:64], lhsT=t_gm[:], rhs=t_vT[:, h * 64:(h + 1) * 64],
                    start=True, stop=False), waits=("vv", "ss")),
                s.do("tensor", lambda e, qh=qh, h=h, base=base: e.matmul(
                    out=p5[:, 0:64], lhsT=qh, rhs=t_Sb[base:base + 64, h * 64:(h + 1) * 64],
                    start=False, stop=True), inc="mm")
                s.do("tensor", lambda e, h=h, base=base: e.matmul(
                    out=p7[base:base + 64, 0:64], lhsT=t_ktT[:, h * 64:(h + 1) * 64],
                    rhs=t_vT[:, h * 64:(h + 1) * 64], start=True, stop=True,
                    tile_position=(0, base)), inc="mm")
                s.do("vector", lambda e, h=h: e.scalar_tensor_tensor(
                    out=t_Y[:, h * 64:(h + 1) * 64], in0=t_vT[:, h * 64:(h + 1) * 64],
                    scalar=t_dcol[:], in1=p5[:, 0:64], op0=ALU.mult, op1=ALU.add),
                    waits=("mm", "ss"), inc="vv")
                alcol = t_al[base:base + 64, a:a + 1]
                sh_ = (slice(base, base + 64), slice(h * 64, (h + 1) * 64))
                s.do("vector", lambda e, sh_=sh_, base=base: e.tensor_tensor(
                    out=t_S[sh_[0], sh_[1]], in0=t_S[sh_[0], sh_[1]],
                    in1=p7[base:base + 64, 0:64], op=ALU.add), inc="vv")
                s.do("vector", lambda e, sh_=sh_, alcol=alcol: e.tensor_scalar(
                    out=t_S[sh_[0], sh_[1]], in0=t_S[sh_[0], sh_[1]],
                    scalar1=alcol, scalar2=None, op0=ALU.mult), inc="vv")
                s.do("vector", lambda e, sh_=sh_: e.tensor_copy(
                    out=t_Sb[sh_[0], sh_[1]], in_=t_S[sh_[0], sh_[1]]),
                    inc="vv")
            if dbg and sc == 0 and ch == 0:
                s.do("gpsimd", lambda e: e.dma_start(out=dbg_map["dbg_Y"], in_=t_Y[:]),
                     waits=("vv",), inc="gst", dma=True)
                for nm, tt in (("dbg_qt", t_qt), ("dbg_kt", t_kt), ("dbg_li", t_liC),
                               ("dbg_ktT", t_ktT), ("dbg_vT", t_vT), ("dbg_S", t_S)):
                    s.do("gpsimd", lambda e, nm=nm, tt=tt: e.dma_start(
                        out=dbg_map[nm], in_=tt[:]), waits=("vv", "ss"), inc="gst", dma=True)
                s.wait("vector", "gst")
                s.wait("tensor", "gst")
                s.wait("scalar", "gst")
            s.do("vector", lambda e: e.tensor_copy(out=t_ysq[:], in_=t_Y[:]),
                 waits=("vv",), inc="vv")
            s.do("vector", lambda e: e.tensor_tensor(
                out=t_ysq[:], in0=t_ysq[:], in1=t_Y[:], op=ALU.mult), inc="vv")
            s.do("vector", lambda e: e.tensor_reduce(
                out=t_gn[:], in_=t_Y[:].rearrange("p (h n) -> p h n", n=64),
                axis=AX.X, op=ALU.add), inc="vv")
            s.do("vector", lambda e: e.tensor_reduce(
                out=t_gn2[:], in_=t_ysq[:].rearrange("p (h n) -> p h n", n=64),
                axis=AX.X, op=ALU.add), inc="vv")
            s.do("vector", lambda e: e.tensor_scalar(
                out=t_gn[:], in0=t_gn[:], scalar1=1.0 / 64, scalar2=None, op0=ALU.mult), inc="vv")
            s.do("vector", lambda e: e.tensor_copy(out=t_gn3[:], in_=t_gn[:]), inc="vv")
            s.do("vector", lambda e: e.tensor_tensor(
                out=t_gn3[:], in0=t_gn3[:], in1=t_gn[:], op=ALU.mult), inc="vv")
            s.do("vector", lambda e: e.tensor_scalar(
                out=t_gn2[:], in0=t_gn2[:], scalar1=1.0 / 64, scalar2=None, op0=ALU.mult), inc="vv")
            s.do("vector", lambda e: e.tensor_tensor(
                out=t_gn2[:], in0=t_gn2[:], in1=t_gn3[:], op=ALU.subtract), inc="vv")
            if dbg and sc == 0 and ch == 0:
                s.do("gpsimd", lambda e: e.dma_start(out=dbg_map["dbg_gnB"], in_=t_gn2[:]),
                     waits=("vv",), inc="gst", dma=True)
                s.wait("vector", "gst")
                s.wait("scalar", "gst")
            s.do("scalar", lambda e: e.activation(
                out=t_gn2[:], in_=t_gn2[:], func=AF.Sqrt, bias=t_epsgn[:]), waits=("vv",), inc="ss")
            s.do("vector", lambda e: e.reciprocal(out=t_gn2[:], in_=t_gn2[:]),
                 waits=("ss",), inc="vv")
            s.do("vector", lambda e: e.tensor_tensor(
                out=t_yn[:].rearrange("p (h n) -> p h n", n=64),
                in0=t_Y[:].rearrange("p (h n) -> p h n", n=64),
                in1=t_gn[:].rearrange("p (h o) -> p h o", o=1).broadcast_to((128, 8, 64)),
                op=ALU.subtract), inc="vv")
            s.do("vector", lambda e: e.tensor_tensor(
                out=t_yn[:].rearrange("p (h n) -> p h n", n=64),
                in0=t_yn[:].rearrange("p (h n) -> p h n", n=64),
                in1=t_gn2[:].rearrange("p (h o) -> p h o", o=1).broadcast_to((128, 8, 64)),
                op=ALU.mult), inc="vv")
            s.do("vector", lambda e: e.tensor_tensor(
                out=t_yn[:], in0=t_yn[:], in1=t_lnwb[:], op=ALU.mult), inc="vv")
            s.do("vector", lambda e: e.tensor_tensor(
                out=t_yn[:], in0=t_yn[:], in1=t_lnbb[:], op=ALU.add), inc="vv")
            if dbg and sc == 0 and ch == 0:
                for nm, tt in (("dbg_yn", t_yn), ("dbg_gn", t_gn), ("dbg_gn2", t_gn2)):
                    s.do("gpsimd", lambda e, nm=nm, tt=tt: e.dma_start(
                        out=dbg_map[nm], in_=tt[:]), waits=("vv", "ss"), inc="gst", dma=True)
                s.wait("vector", "gst")
            for a in range(4):
                s.do("tensor", lambda e, a=a: e.transpose(
                    out=p2[:, 0:128], in_=t_yn[:, a * 128:(a + 1) * 128],
                    identity=t_idn[:]), waits=("vv",), inc="mm")
                s.do("vector", lambda e, a=a, tof=tof: e.scalar_tensor_tensor(
                    out=t_yg[:, a * SC + tof:a * SC + tof + LCH],
                    in0=t_gs[:, a * SC + tof:a * SC + tof + LCH], scalar=0.0,
                    in1=p2[:, 0:128], op0=ALU.add, op1=ALU.mult),
                    waits=("mm",), inc="vv")
        for a in range(4):
            s.do("gpsimd", lambda e, sc=sc, a=a: e.dma_start(
                out=yg_o[a, :, sc * SC:(sc + 1) * SC],
                in_=t_yg[:, a * SC:(a + 1) * SC]), waits=("vv",), inc="gst", dma=True)

    s.wait("gpsimd", "gst")
    s.finalize()
    ctx.close()
    return nc


def prep_A(d):
    cst = host_consts()
    tm_names = ["time_maa_x", "time_maa_w", "time_maa_k", "time_maa_v", "time_maa_r", "time_maa_g"]
    tm = np.stack([np.asarray(d["L0_attn_" + n]).reshape(1024) for n in tm_names])
    in_maps = []
    for c in range(NCORE):
        b, hg = c // 2, c % 2
        SL = slice(hg * 512, hg * 512 + 512)
        x = np.asarray(d["L0_attn_x"][b])
        m = {
            "xT": np.ascontiguousarray(x.T).reshape(8, 128, x.shape[0]),
            "wmod": d["L0_wmod"],
            "c_col": np.ascontiguousarray(np.asarray(d["L0_c"][b]).reshape(8, 128).T),
            "tm_rows": tm[1:].reshape(1, 5120),
            "tmx_col": np.ascontiguousarray(tm[0].reshape(8, 128).T),
            "maa_w1": np.asarray(d["L0_attn_time_maa_w1"]).reshape(8, 128, 160),
            "maa_w2": d["L0_attn_time_maa_w2"],
            "td_w1": np.asarray(d["L0_attn_time_decay_w1"]).reshape(8, 128, 64),
            "td_w2": np.ascontiguousarray(np.asarray(d["L0_attn_time_decay_w2"])[:, SL]),
            "tdec": np.ascontiguousarray(np.asarray(d["L0_attn_time_decay"]).reshape(1024)[SL].reshape(4, 128).T),
            "Wr": np.ascontiguousarray(np.asarray(d["L0_attn_W_r"])[:, SL]).reshape(8, 128, 512),
            "Wk": np.ascontiguousarray(np.asarray(d["L0_attn_W_k"])[:, SL]).reshape(8, 128, 512),
            "Wv": np.ascontiguousarray(np.asarray(d["L0_attn_W_v"])[:, SL]).reshape(8, 128, 512),
            "Wg": np.ascontiguousarray(np.asarray(d["L0_attn_W_g"])[:, SL]).reshape(8, 128, 512),
            "u_col": np.ascontiguousarray(np.vstack([np.asarray(d["L0_attn_time_faaaa"])[hg * 8:hg * 8 + 8].reshape(8, 64).T] * 2)),
            "lnw_r": np.asarray(d["L0_attn_ln_w"])[SL].reshape(1, 512),
            "lnb_r": np.asarray(d["L0_attn_ln_b"])[SL].reshape(1, 512),
            "ctrl": cst["tril_incl"], "cmask": cst["mask_su"],
            "cidn": cst["idn128"], "cidnb": np.vstack([cst["idn64b"], cst["idn64b"]]),
            "cones": cst["ones128"], "conesr": cst["onesr"],
            "cones64": cst["ones64"], "cones11": cst["ones11"],
        }
        in_maps.append({k: np.ascontiguousarray(v) for k, v in m.items()})
    return in_maps



def build_B():
    nc = bass.Bass("TRN2", target_bir_lowering=False, debug=False)
    dram = lambda n, s, d, k: nc.dram_tensor(n, list(s), d, kind=k).ap()
    NTOK = T // 2
    NSC = NTOK // SC
    yg = dram("yg", (8, 128, NTOK), BF16, "ExternalInput")
    xT = dram("xT", (8, 128, NTOK), F32, "ExternalInput")
    Wo = dram("Wo", (8, 128, 1024), F32, "ExternalInput")
    mods = dram("mods", (128, 48), F32, "ExternalInput")
    wg_sl = dram("wg_sl", (3, 128, 1024), F32, "ExternalInput")
    wu_sl = dram("wu_sl", (3, 128, 1024), F32, "ExternalInput")
    wd_sl = dram("wd_sl", (128, 3072), F32, "ExternalInput")
    cones = dram("cones", (128, 1), F32, "ExternalInput")
    conesr = dram("conesr", (1, 128), F32, "ExternalInput")
    cidn = dram("cidn", (128, 128), F32, "ExternalInput")
    x1_o = dram("x1_o", (8, 128, NTOK), F32, "ExternalOutput")
    h2_o = dram("h2_o", (8, 128, NTOK), F32, "ExternalOutput")
    max1_o = dram("max1_o", (1, 1), F32, "ExternalOutput")
    wqg_o = dram("wqg_o", (3, 128, 1024), BF16, "ExternalOutput")
    wqu_o = dram("wqu_o", (3, 128, 1024), BF16, "ExternalOutput")
    wqd_o = dram("wqd_o", (128, 3072), BF16, "ExternalOutput")

    ctx = ExitStack()
    sb, ps = _mk(nc, ctx)
    s = Sched(nc, ctx)
    t_wo = sb("t_wo", (128, 8 * 1024), BF16)
    t_yg = sb("t_yg", (128, 8 * SC), BF16)
    t_x = sb("t_x", (128, 8 * SC), F32)
    t_x1 = sb("t_x1", (128, 8 * SC), F32)
    t_h2 = sb("t_h2", (128, SC), F32)
    t_mods = sb("t_mods", (128, 48), F32)
    t_onepcb = sb("t_onepcb", (128, 8), F32)
    t_w3 = sb("t_w3", (128, 3 * 1024), F32)
    t_wd = sb("t_wd", (128, 3072), F32)
    t_wq = sb("t_wq", (128, 3 * 1024), BF16)
    t_wqd = sb("t_wqd", (128, 3072), BF16)
    t_bs = sb("t_bs", (128, 224), F32)
    t_sq = sb("t_sq", (128, SC), F32)
    t_sq2 = sb("t_sq2", (128, SC), F32)
    t_row1 = sb("t_row1", (1, SC), F32)
    t_row2 = sb("t_row2", (1, SC), F32)
    t_row3 = sb("t_row3", (1, SC), F32)
    t_mub = sb("t_mub", (128, SC), F32)
    t_rsb = sb("t_rsb", (128, SC), F32)
    t_macc = sb("t_macc", (128, 1), F32)
    t_mrow = sb("t_mrow", (1, 128), F32)
    t_eps = sb("t_eps", (128, 1), F32)
    t_ones = sb("t_ones", (128, 1), F32)
    t_onesr = sb("t_onesr", (1, 128), F32)
    t_idn = sb("t_idn", (128, 128), F32)
    p0 = ps("p0", (128, 512), F32)
    p1 = ps("p1", (128, 512), F32)
    p2 = ps("p2", (128, 512), F32)
    p3 = ps("p3", (128, 512), F32)

    def ld(dst, src):
        s.do("sync", lambda e, d=dst, sr=src: e.dma_start(out=d, in_=sr), inc="ldr", dma=True)

    def gld(dst, src):
        s.do("gpsimd", lambda e, d=dst, sr=src: e.dma_start(out=d, in_=sr), inc="gld", dma=True)

    ld(t_mods[:], mods); ld(t_ones[:], cones); ld(t_onesr[:], conesr); ld(t_idn[:], cidn)
    gld(t_wo[:].rearrange("p (k m) -> p k m", k=8), Wo.rearrange("k p m -> p k m"))
    s.do("vector", lambda e: e.memset(t_eps[:], EPS_LN), inc="vv")
    s.do("vector", lambda e: e.memset(t_macc[:], 0.0), inc="vv")
    s.do("vector", lambda e: e.tensor_scalar(
        out=t_onepcb[:], in0=t_mods[:, 32:40], scalar1=1.0, scalar2=None, op0=ALU.add),
        waits=("ldr",), inc="vv")

    t_sq_wd = sb("t_sq_wd", (128, 3072), F32)
    def dequant(src_ap, nt, wtile, wout, bf_out, blocks_per_row):
        if nt > 1:
            s.do("sync", lambda e, w=wtile, sr=src_ap, nt=nt: e.dma_start(
                out=w[:, 0:nt * 1024].rearrange("p (a m) -> p a m", a=nt),
                in_=sr.rearrange("a p m -> p a m")), waits=("gst", "vv"), inc="ldq", dma=True)
        else:
            s.do("sync", lambda e, w=wtile, sr=src_ap: e.dma_start(
                out=w[:], in_=sr), waits=("gst", "vv"), inc="ldq", dma=True)
        for a in range(nt):
            F = 1024 if nt > 1 else 3072
            NG = F // 32
            w_ap = wtile[:, a * F:(a + 1) * F]
            for g in range(NG):
                s.do("vector", lambda e, w_ap=w_ap, g=g: e.tensor_reduce(
                    out=t_bs[:, g:g + 1], in_=w_ap[:, g * 32:(g + 1) * 32],
                    axis=AX.X, op=ALU.max, apply_absolute_value=True),
                    waits=("ldq", "gst") if g == 0 else (), inc="vv")
            s.do("vector", lambda e, NG=NG: e.tensor_scalar(
                out=t_bs[:, 0:NG], in0=t_bs[:, 0:NG], scalar1=1.0 / 7, scalar2=1e-12,
                op0=ALU.mult, op1=ALU.max), inc="vv")
            s.do("vector", lambda e, NG=NG: e.reciprocal(
                out=t_bs[:, 128:128 + NG], in_=t_bs[:, 0:NG]), inc="vv")
            s.do("vector", lambda e, w_ap=w_ap, NG=NG, F=F: e.tensor_tensor(
                out=t_sq_wd[:, 0:F].rearrange("p (g n) -> p g n", n=32),
                in0=w_ap.rearrange("p (g n) -> p g n", n=32),
                in1=t_bs[:, 128:128 + NG].rearrange("p (g o) -> p g o", o=1).broadcast_to((128, NG, 32)),
                op=ALU.mult), inc="vv")
            tgt = t_sq_wd[:, 0:F]
            s.do("vector", lambda e, tgt=tgt: e.tensor_scalar(
                out=tgt, in0=tgt, scalar1=RMAGIC, scalar2=RMAGIC,
                op0=ALU.add, op1=ALU.subtract), inc="vv")
            s.do("vector", lambda e, tgt=tgt: e.tensor_scalar(
                out=tgt, in0=tgt, scalar1=-8.0, scalar2=7.0,
                op0=ALU.max, op1=ALU.min), inc="vv")
            s.do("vector", lambda e, tgt=tgt, bf_out=bf_out, a=a, NG=NG, F=F: e.tensor_tensor(
                out=bf_out[:, a * F:(a + 1) * F].rearrange("p (g n) -> p g n", n=32),
                in0=tgt.rearrange("p (g n) -> p g n", n=32),
                in1=t_bs[:, 0:NG].rearrange("p (g o) -> p g o", o=1).broadcast_to((128, NG, 32)),
                op=ALU.mult), inc="vv")
        for a in range(nt):
            s.do("gpsimd", lambda e, wout=wout, bf_out=bf_out, a=a, nt=nt: e.dma_start(
                out=wout[a] if nt > 1 else wout,
                in_=bf_out[:, a * 1024:(a + 1) * 1024] if nt > 1 else bf_out[:]),
                waits=("vv",), inc="gst", dma=True)

    dequant(wg_sl, 3, t_w3, wqg_o, t_wq, 32)
    s.wait("sync", "gst")
    dequant(wu_sl, 3, t_w3, wqu_o, t_wq, 32)
    dequant(wd_sl, 1, t_wd, wqd_o, t_wqd, 96)

    s.wait("tensor", "gld", "ldr", "vv")
    for sc in range(NSC):
        s.do("sync", lambda e, sc=sc: e.dma_start(
            out=t_yg[:].rearrange("p (a t) -> p a t", a=8),
            in_=yg[:, :, sc * SC:(sc + 1) * SC].rearrange("a p t -> p a t")),
            waits=("vv", "mm", "gst"), inc="ldx", dma=True)
        s.do("sync", lambda e, sc=sc: e.dma_start(
            out=t_x[:].rearrange("p (a t) -> p a t", a=8),
            in_=xT[:, :, sc * SC:(sc + 1) * SC].rearrange("a p t -> p a t")),
            inc="ldx", dma=True)
        s.wait("tensor", "ldx")
        s.wait("scalar", "ldx")
        s.wait("vector", "ldx")
        for mt in range(8):
            pp = (p0, p1)[mt % 2]
            for kk in range(8):
                s.do("tensor", lambda e, mt=mt, kk=kk, pp=pp: e.matmul(
                    out=pp[:, 0:SC], lhsT=t_wo[:, kk * 1024 + mt * 128:kk * 1024 + mt * 128 + 128],
                    rhs=t_yg[:, kk * SC:(kk + 1) * SC], start=(kk == 0), stop=(kk == 7)),
                    waits=("vv",) if kk == 0 else (), inc="mm" if kk == 7 else None)
            s.do("vector", lambda e, mt=mt, pp=pp: e.scalar_tensor_tensor(
                out=t_x1[:, mt * SC:(mt + 1) * SC], in0=pp[:, 0:SC],
                scalar=t_mods[:, 16 + mt:17 + mt], in1=t_x[:, mt * SC:(mt + 1) * SC],
                op0=ALU.mult, op1=ALU.add), waits=("mm",), inc="vv")
            s.do("gpsimd", lambda e, mt=mt, sc=sc: e.dma_start(
                out=x1_o[mt, :, sc * SC:(sc + 1) * SC], in_=t_x1[:, mt * SC:(mt + 1) * SC]),
                waits=("vv",), inc="gst", dma=True)
        for a in range(8):
            s.do("tensor", lambda e, a=a: e.matmul(
                out=p2[0:1, 0:SC], lhsT=t_ones[:], rhs=t_x1[:, a * SC:(a + 1) * SC],
                start=(a == 0), stop=(a == 7)), waits=("vv",) if a == 0 else (),
                inc="mm" if a == 7 else None)
        for a in range(8):
            buf = t_sq if a % 2 == 0 else t_sq2
            s.do("scalar", lambda e, a=a, buf=buf: e.activation(
                out=buf[:], in_=t_x1[:, a * SC:(a + 1) * SC], func=AF.Square),
                waits=("mm", "vv"), inc="ss")
            s.wait("tensor", "ss")
            s.do("tensor", lambda e, a=a, buf=buf: e.matmul(
                out=p3[0:1, 0:SC], lhsT=t_ones[:], rhs=buf[:],
                start=(a == 0), stop=(a == 7)), inc="mm")
        s.do("scalar", lambda e: e.activation(out=t_row1[:], in_=p2[0:1, 0:SC],
                                              func=AF.Copy, scale=1.0 / 1024), waits=("mm",), inc="ss")
        s.do("scalar", lambda e: e.activation(out=t_row2[:], in_=p3[0:1, 0:SC],
                                              func=AF.Copy, scale=1.0 / 1024), inc="ss")
        s.do("vector", lambda e: e.tensor_copy(out=t_row3[:], in_=t_row1[:]), waits=("ss",), inc="vv")
        s.do("vector", lambda e: e.tensor_tensor(
            out=t_row3[:], in0=t_row3[:], in1=t_row1[:], op=ALU.mult), inc="vv")
        s.do("vector", lambda e: e.tensor_tensor(
            out=t_row2[:], in0=t_row2[:], in1=t_row3[:], op=ALU.subtract), inc="vv")
        s.do("scalar", lambda e: e.activation(out=t_row3[:], in_=t_row2[:], func=AF.Sqrt,
                                              bias=t_eps[0:1, :]), waits=("vv",), inc="ss")
        s.do("vector", lambda e: e.reciprocal(out=t_row2[:], in_=t_row3[:]), waits=("ss",), inc="vv")
        s.wait("tensor", "vv")
        s.do("tensor", lambda e: e.matmul(out=p2[:, 0:SC], lhsT=t_onesr[:],
                                          rhs=t_row1[:], start=True, stop=True), inc="mm")
        s.do("tensor", lambda e: e.matmul(out=p3[:, 0:SC], lhsT=t_onesr[:],
                                          rhs=t_row2[:], start=True, stop=True), inc="mm")
        s.do("scalar", lambda e: e.copy(out=t_mub[:], in_=p2[:, 0:SC]), waits=("mm",), inc="ss")
        s.do("scalar", lambda e: e.copy(out=t_rsb[:], in_=p3[:, 0:SC]), inc="ss")
        for a in range(8):
            buf = t_sq if a % 2 == 0 else t_sq2
            s.do("vector", lambda e, a=a, buf=buf: e.tensor_tensor(
                out=buf[:], in0=t_x1[:, a * SC:(a + 1) * SC], in1=t_mub[:],
                op=ALU.subtract), waits=("ss",), inc="vv")
            s.do("vector", lambda e, a=a, buf=buf: e.tensor_tensor(
                out=buf[:], in0=buf[:], in1=t_rsb[:], op=ALU.mult), inc="vv")
            s.do("vector", lambda e, a=a, buf=buf: e.tensor_scalar(
                out=t_h2[:], in0=buf[:],
                scalar1=t_onepcb[:, a:a + 1], scalar2=t_mods[:, 24 + a:25 + a],
                op0=ALU.mult, op1=ALU.add), inc="vv")
            s.do("gpsimd", lambda e, a=a, sc=sc: e.dma_start(
                out=h2_o[a, :, sc * SC:(sc + 1) * SC], in_=t_h2[:]),
                waits=("vv",), inc="gst", dma=True)
            s.do("vector", lambda e: e.tensor_reduce(
                out=t_sq[:, 0:1], in_=t_h2[:], axis=AX.X, op=ALU.max,
                apply_absolute_value=True), waits=("gst",), inc="vv")
            s.do("vector", lambda e: e.tensor_tensor(
                out=t_macc[:], in0=t_macc[:], in1=t_sq[:, 0:1], op=ALU.max), inc="vv")
    s.wait("tensor", "vv")
    s.do("tensor", lambda e: e.transpose(out=p2[0:1, 0:128], in_=t_macc[:],
                                         identity=t_idn[:]), inc="mm")
    s.do("scalar", lambda e: e.copy(out=t_mrow[:], in_=p2[0:1, 0:128]), waits=("mm",), inc="ss")
    s.do("vector", lambda e: e.tensor_reduce(
        out=t_macc[0:1, 0:1], in_=t_mrow[:], axis=AX.X, op=ALU.max), waits=("ss",), inc="vv")
    s.do("gpsimd", lambda e: e.dma_start(out=max1_o, in_=t_macc[0:1, 0:1]),
         waits=("vv",), inc="gst", dma=True)
    s.wait("gpsimd", "gst")
    s.finalize()
    ctx.close()
    return nc


def prep_B(d, ygs, modss):
    cst = host_consts()
    in_maps = []
    x = np.asarray(d["L0_attn_x"])
    NTOK = x.shape[1] // 2
    for c in range(NCORE):
        b, half = c // 2, c % 2
        tsl = slice(half * NTOK, (half + 1) * NTOK)
        yg_full = np.concatenate([ygs[2 * b], ygs[2 * b + 1]], axis=0)
        m = {
            "yg": np.ascontiguousarray(yg_full[:, :, tsl]),
            "xT": np.ascontiguousarray(x[b].T).reshape(8, 128, x.shape[1])[:, :, tsl],
            "Wo": np.asarray(d["L0_attn_W_o"]).reshape(8, 128, 1024),
            "mods": modss[2 * b],
            "wg_sl": np.asarray(d["L0_ffn_w_gate"])[c * 384:(c + 1) * 384].reshape(3, 128, 1024),
            "wu_sl": np.asarray(d["L0_ffn_w_up"])[c * 384:(c + 1) * 384].reshape(3, 128, 1024),
            "wd_sl": np.asarray(d["L0_ffn_w_down"])[c * 128:(c + 1) * 128],
            "cones": cst["ones128"], "conesr": cst["onesr"], "cidn": cst["idn128"],
        }
        in_maps.append({k: np.ascontiguousarray(v) for k, v in m.items()})
    return in_maps



def build_C():
    nc = bass.Bass("TRN2", target_bir_lowering=False, debug=False)
    dram = lambda n, s, d, k: nc.dram_tensor(n, list(s), d, kind=k).ap()
    NTOK = T // 2
    NSC = NTOK // SC
    h2 = dram("h2", (8, 128, NTOK), F32, "ExternalInput")
    maxes = dram("maxes", (1, 8), F32, "ExternalInput")
    wquT = dram("wquT", (8, 128, 3072), BF16, "ExternalInput")
    wqgT = dram("wqgT", (8, 128, 3072), BF16, "ExternalInput")
    conesr = dram("conesr", (1, 128), F32, "ExternalInput")
    cidn = dram("cidn", (128, 128), F32, "ExternalInput")
    up_o = dram("up_o", (24, 128, NTOK), BF16, "ExternalOutput")
    gate_o = dram("gate_o", (24, 128, NTOK), BF16, "ExternalOutput")
    maxu_o = dram("maxu_o", (1, 1), F32, "ExternalOutput")
    maxg_o = dram("maxg_o", (1, 1), F32, "ExternalOutput")

    ctx = ExitStack()
    sb, ps = _mk(nc, ctx)
    s = Sched(nc, ctx)
    t_wu = sb("t_wu", (128, 8 * 3072), BF16)
    t_wg = sb("t_wg", (128, 8 * 3072), BF16)
    t_h2 = sb("t_h2", (128, 8 * SC), F32)
    t_xi = sb("t_xi", (128, 8 * SC), BF16)
    t_up = sb("t_up", (128, SC), BF16)
    t_mx = sb("t_mx", (1, 8), F32)
    t_s = sb("t_s", (128, 8), F32)
    t_mu = sb("t_mu", (128, 1), F32)
    t_mg = sb("t_mg", (128, 1), F32)
    t_mrow = sb("t_mrow", (1, 128), F32)
    t_onesr = sb("t_onesr", (1, 128), F32)
    t_idn = sb("t_idn", (128, 128), F32)
    t_sq = sb("t_sq", (128, SC), F32)
    p0 = ps("p0", (128, 512), F32)
    p1 = ps("p1", (128, 512), F32)
    p2 = ps("p2", (128, 512), F32)

    def ld(dst, src):
        s.do("sync", lambda e, d=dst, sr=src: e.dma_start(out=d, in_=sr), inc="ldr", dma=True)
    ld(t_mx[:], maxes); ld(t_onesr[:], conesr); ld(t_idn[:], cidn)
    s.do("gpsimd", lambda e: e.dma_start(
        out=t_wu[:].rearrange("p (k m) -> p k m", k=8), in_=wquT.rearrange("k p m -> p k m")),
        inc="gld", dma=True)
    s.do("gpsimd", lambda e: e.dma_start(
        out=t_wg[:].rearrange("p (k m) -> p k m", k=8), in_=wqgT.rearrange("k p m -> p k m")),
        inc="gld", dma=True)
    s.do("vector", lambda e: e.memset(t_mu[:], 0.0), inc="vv")
    s.do("vector", lambda e: e.memset(t_mg[:], 0.0), inc="vv")
    s.do("vector", lambda e: e.tensor_reduce(
        out=t_mx[0:1, 0:1], in_=t_mx[:], axis=AX.X, op=ALU.max), waits=("ldr",), inc="vv")
    s.do("vector", lambda e: e.tensor_scalar(
        out=t_mx[0:1, 1:2], in0=t_mx[0:1, 0:1], scalar1=1.0 / 32767, scalar2=1e-12,
        op0=ALU.mult, op1=ALU.max), inc="vv")
    s.do("vector", lambda e: e.reciprocal(out=t_mx[0:1, 2:3], in_=t_mx[0:1, 1:2]), inc="vv")
    s.wait("tensor", "vv", "ldr")
    s.do("tensor", lambda e: e.matmul(out=p0[:, 0:1], lhsT=t_onesr[:], rhs=t_mx[0:1, 2:3],
                                      start=True, stop=True), inc="mm")
    s.do("scalar", lambda e: e.copy(out=t_s[:, 0:1], in_=p0[:, 0:1]), waits=("mm",), inc="ss")
    s.wait("tensor", "gld", "ss")
    for sc in range(NSC):
        s.do("sync", lambda e, sc=sc: e.dma_start(
            out=t_h2[:].rearrange("p (a t) -> p a t", a=8),
            in_=h2[:, :, sc * SC:(sc + 1) * SC].rearrange("a p t -> p a t")),
            waits=("vv", "mm", "gst", "ss"), inc="ldx", dma=True)
        s.wait("vector", "ldx", "ss")
        for a in range(8):
            s.do("vector", lambda e, a=a: e.tensor_scalar(
                out=t_sq[:], in0=t_h2[:, a * SC:(a + 1) * SC], scalar1=t_s[:, 0:1],
                scalar2=RMAGIC, op0=ALU.mult, op1=ALU.add), inc="vv")
            s.do("vector", lambda e, a=a: e.tensor_scalar(
                out=t_xi[:, a * SC:(a + 1) * SC], in0=t_sq[:], scalar1=RMAGIC, scalar2=None,
                op0=ALU.subtract), inc="vv")
        s.wait("tensor", "vv")
        for wt, oo, macc in ((t_wu, up_o, t_mu), (t_wg, gate_o, t_mg)):
            for mt in range(24):
                pp = (p0, p1)[mt % 2]
                for kk in range(8):
                    s.do("tensor", lambda e, wt=wt, mt=mt, kk=kk, pp=pp: e.matmul(
                        out=pp[:, 0:SC],
                        lhsT=wt[:, kk * 3072 + mt * 128:kk * 3072 + mt * 128 + 128],
                        rhs=t_xi[:, kk * SC:(kk + 1) * SC], start=(kk == 0), stop=(kk == 7)),
                        waits=("ss", "vv") if kk == 0 else (), inc="mm" if kk == 7 else None)
                s.do("scalar", lambda e, pp=pp: e.copy(out=t_up[:], in_=pp[:, 0:SC]),
                     waits=("mm", "gst", "vv"), inc="ss")
                s.do("vector", lambda e: e.tensor_reduce(
                    out=t_sq[:, 0:1], in_=t_up[:], axis=AX.X, op=ALU.max,
                    apply_absolute_value=True), waits=("ss",), inc="vv")
                s.do("vector", lambda e, macc=macc: e.tensor_tensor(
                    out=macc[:], in0=macc[:], in1=t_sq[:, 0:1], op=ALU.max), inc="vv")
                s.do("gpsimd", lambda e, oo=oo, mt=mt, sc=sc: e.dma_start(
                    out=oo[mt, :, sc * SC:(sc + 1) * SC], in_=t_up[:]),
                    waits=("ss",), inc="gst", dma=True)
    for macc, mo in ((t_mu, maxu_o), (t_mg, maxg_o)):
        s.wait("tensor", "vv", "gst")
        s.do("tensor", lambda e, macc=macc: e.transpose(
            out=p2[0:1, 0:128], in_=macc[:], identity=t_idn[:]), inc="mm")
        s.do("scalar", lambda e: e.copy(out=t_mrow[:], in_=p2[0:1, 0:128]), waits=("mm",), inc="ss")
        s.do("vector", lambda e, macc=macc: e.tensor_reduce(
            out=macc[0:1, 0:1], in_=t_mrow[:], axis=AX.X, op=ALU.max), waits=("ss",), inc="vv")
        s.do("vector", lambda e, macc=macc: e.tensor_scalar(
            out=macc[0:1, 0:1], in0=macc[0:1, 0:1], scalar1=t_mx[0:1, 1:2], scalar2=None,
            op0=ALU.mult), inc="vv")
        s.do("gpsimd", lambda e, macc=macc, mo=mo: e.dma_start(out=mo, in_=macc[0:1, 0:1]),
             waits=("vv",), inc="gst", dma=True)
    s.wait("gpsimd", "gst")
    s.finalize()
    ctx.close()
    return nc


def prep_C(d, h2s, max1s, wqg, wqu):
    cst = host_consts()
    maxrow = np.concatenate([m.reshape(1) for m in max1s]).reshape(1, 8)
    wquT = np.ascontiguousarray(wqu.T).reshape(8, 128, 3072)
    wqgT = np.ascontiguousarray(wqg.T).reshape(8, 128, 3072)
    in_maps = []
    for c in range(NCORE):
        m = {
            "h2": h2s[c], "maxes": maxrow, "wquT": wquT, "wqgT": wqgT,
            "conesr": cst["onesr"], "cidn": cst["idn128"],
        }
        in_maps.append({k: np.ascontiguousarray(v) for k, v in m.items()})
    return in_maps


def build_D():
    nc = bass.Bass("TRN2", target_bir_lowering=False, debug=False)
    dram = lambda n, s, d, k: nc.dram_tensor(n, list(s), d, kind=k).ap()
    NTOK = T // 2
    gate = dram("gate", (24, 128, NTOK), BF16, "ExternalInput")
    scal = dram("scal", (1, 24), F32, "ExternalInput")
    conesr = dram("conesr", (1, 128), F32, "ExternalInput")
    cidn = dram("cidn", (128, 128), F32, "ExternalInput")
    P_o = dram("P_o", (24, 128, NTOK), BF16, "ExternalOutput")
    maxp_o = dram("maxp_o", (1, 1), F32, "ExternalOutput")

    ctx = ExitStack()
    sb, ps = _mk(nc, ctx)
    s = Sched(nc, ctx)
    t_g = sb("t_g", (128, NTOK), BF16)
    t_gi = sb("t_gi", (128, NTOK), F32)
    t_sig = sb("t_sig", (128, NTOK), F32)
    t_P = sb("t_P", (128, NTOK), F32)
    t_Pb = sb("t_Pb", (128, NTOK), BF16)
    t_mx = sb("t_mx", (1, 32), F32)
    t_s = sb("t_s", (128, 4), F32)
    t_mp = sb("t_mp", (128, 1), F32)
    t_mrow = sb("t_mrow", (1, 128), F32)
    t_onesr = sb("t_onesr", (1, 128), F32)
    t_idn = sb("t_idn", (128, 128), F32)
    p0 = ps("p0", (128, 512), F32)

    def ld(dst, src):
        s.do("sync", lambda e, d=dst, sr=src: e.dma_start(out=d, in_=sr), inc="ldr", dma=True)
    ld(t_mx[0:1, 0:24], scal); ld(t_onesr[:], conesr); ld(t_idn[:], cidn)
    s.do("vector", lambda e: e.memset(t_mp[:], 0.0), inc="vv")
    s.do("vector", lambda e: e.tensor_reduce(
        out=t_mx[0:1, 24:25], in_=t_mx[0:1, 0:8], axis=AX.X, op=ALU.max), waits=("ldr",), inc="vv")
    s.do("vector", lambda e: e.tensor_scalar(
        out=t_mx[0:1, 25:26], in0=t_mx[0:1, 24:25], scalar1=1.0 / 32767, scalar2=1e-12,
        op0=ALU.mult, op1=ALU.max), inc="vv")
    s.do("vector", lambda e: e.reciprocal(out=t_mx[0:1, 26:27], in_=t_mx[0:1, 25:26]), inc="vv")
    s.do("vector", lambda e: e.tensor_reduce(
        out=t_mx[0:1, 27:28], in_=t_mx[0:1, 8:16], axis=AX.X, op=ALU.max), inc="vv")
    s.do("vector", lambda e: e.tensor_scalar(
        out=t_mx[0:1, 28:29], in0=t_mx[0:1, 27:28], scalar1=1.0 / 32767, scalar2=1e-12,
        op0=ALU.mult, op1=ALU.max), inc="vv")
    s.do("vector", lambda e: e.tensor_scalar(
        out=t_mx[0:1, 29:30], in0=t_mx[0:1, 26:27], scalar1=t_mx[0:1, 28:29], scalar2=None,
        op0=ALU.mult), inc="vv")
    s.do("vector", lambda e: e.tensor_copy(out=t_mx[0:1, 30:31], in_=t_mx[0:1, 25:26]), inc="vv")
    s.wait("tensor", "vv", "ldr")
    s.do("tensor", lambda e: e.matmul(out=p0[:, 0:2], lhsT=t_onesr[:], rhs=t_mx[0:1, 29:31],
                                      start=True, stop=True), inc="mm")
    s.do("scalar", lambda e: e.copy(out=t_s[:, 0:2], in_=p0[:, 0:2]), waits=("mm",), inc="ss")
    for mt in range(24):
        s.do("sync", lambda e, mt=mt: e.dma_start(out=t_g[:], in_=gate[mt]),
             waits=("vv", "ss", "gst"), inc="ldx", dma=True)
        s.wait("vector", "ldx", "ss")
        s.do("vector", lambda e: e.tensor_scalar(
            out=t_gi[:], in0=t_g[:], scalar1=t_s[:, 0:1], scalar2=RMAGIC,
            op0=ALU.mult, op1=ALU.add), inc="vv")
        s.do("vector", lambda e: e.tensor_scalar(
            out=t_gi[:], in0=t_gi[:], scalar1=RMAGIC, scalar2=None, op0=ALU.subtract), inc="vv")
        s.do("scalar", lambda e: e.activation(
            out=t_sig[:], in_=t_gi[:], func=AF.Sigmoid, scale=t_s[:, 1:2]),
            waits=("vv",), inc="ss")
        s.do("vector", lambda e: e.tensor_scalar(
            out=t_sig[:], in0=t_sig[:], scalar1=65536.0, scalar2=RMAGIC,
            op0=ALU.mult, op1=ALU.add), waits=("ss",), inc="vv")
        s.do("vector", lambda e: e.tensor_scalar(
            out=t_sig[:], in0=t_sig[:], scalar1=RMAGIC, scalar2=65535.0,
            op0=ALU.subtract, op1=ALU.min), inc="vv")
        s.do("vector", lambda e: e.tensor_tensor(
            out=t_P[:], in0=t_gi[:], in1=t_sig[:], op=ALU.mult), inc="vv")
        s.do("vector", lambda e: e.tensor_reduce(
            out=t_s[:, 2:3], in_=t_P[:], axis=AX.X, op=ALU.max,
            apply_absolute_value=True), inc="vv")
        s.do("vector", lambda e: e.tensor_tensor(
            out=t_mp[:], in0=t_mp[:], in1=t_s[:, 2:3], op=ALU.max), inc="vv")
        s.do("gpsimd", lambda e: e.tensor_copy(out=t_Pb[:], in_=t_P[:]), waits=("vv", "gst"), inc="gg")
        s.do("gpsimd", lambda e, mt=mt: e.dma_start(out=P_o[mt], in_=t_Pb[:]),
             waits=("gg",), inc="gst", dma=True)
    s.wait("tensor", "vv", "gst")
    s.do("tensor", lambda e: e.transpose(out=p0[0:1, 0:128], in_=t_mp[:], identity=t_idn[:]), inc="mm")
    s.do("scalar", lambda e: e.copy(out=t_mrow[:], in_=p0[0:1, 0:128]), waits=("mm",), inc="ss")
    s.do("vector", lambda e: e.tensor_reduce(
        out=t_mp[0:1, 0:1], in_=t_mrow[:], axis=AX.X, op=ALU.max), waits=("ss",), inc="vv")
    s.do("vector", lambda e: e.tensor_scalar(
        out=t_mp[0:1, 0:1], in0=t_mp[0:1, 0:1], scalar1=t_mx[0:1, 25:26], scalar2=1.0 / 65536,
        op0=ALU.mult, op1=ALU.mult), inc="vv")
    s.do("gpsimd", lambda e: e.dma_start(out=maxp_o, in_=t_mp[0:1, 0:1]),
         waits=("vv",), inc="gst", dma=True)
    s.wait("gpsimd", "gst")
    s.finalize()
    ctx.close()
    return nc


def prep_D(d, gates, maxgs, max1s):
    cst = host_consts()
    row = np.zeros((1, 24), np.float32)
    row[0, 0:8] = np.concatenate([m.reshape(1) for m in maxgs])
    row[0, 8:16] = np.concatenate([m.reshape(1) for m in max1s])
    in_maps = []
    for c in range(NCORE):
        m = {"gate": gates[c], "scal": row, "conesr": cst["onesr"], "cidn": cst["idn128"]}
        in_maps.append({k: np.ascontiguousarray(v) for k, v in m.items()})
    return in_maps


def build_E():
    nc = bass.Bass("TRN2", target_bir_lowering=False, debug=False)
    dram = lambda n, s, d, k: nc.dram_tensor(n, list(s), d, kind=k).ap()
    NTOK = T // 2
    up = dram("up", (24, 128, NTOK), BF16, "ExternalInput")
    P = dram("P", (24, 128, NTOK), BF16, "ExternalInput")
    scal = dram("scal", (1, 40), F32, "ExternalInput")
    conesr = dram("conesr", (1, 128), F32, "ExternalInput")
    cidn = dram("cidn", (128, 128), F32, "ExternalInput")
    O_o = dram("O_o", (24, 128, NTOK), BF16, "ExternalOutput")
    maxo_o = dram("maxo_o", (1, 1), F32, "ExternalOutput")

    ctx = ExitStack()
    sb, ps = _mk(nc, ctx)
    s = Sched(nc, ctx)
    t_u = sb("t_u", (128, NTOK), BF16)
    t_p = sb("t_p", (128, NTOK), BF16)
    t_ui = sb("t_ui", (128, NTOK), F32)
    t_pi = sb("t_pi", (128, NTOK), F32)
    t_O = sb("t_O", (128, NTOK), F32)
    t_Ob = sb("t_Ob", (128, NTOK), BF16)
    t_mx = sb("t_mx", (1, 48), F32)
    t_s = sb("t_s", (128, 4), F32)
    t_mo = sb("t_mo", (128, 1), F32)
    t_mrow = sb("t_mrow", (1, 128), F32)
    t_onesr = sb("t_onesr", (1, 128), F32)
    t_idn = sb("t_idn", (128, 128), F32)
    p0 = ps("p0", (128, 512), F32)

    def ld(dst, src):
        s.do("sync", lambda e, d=dst, sr=src: e.dma_start(out=d, in_=sr), inc="ldr", dma=True)
    ld(t_mx[0:1, 0:40], scal); ld(t_onesr[:], conesr); ld(t_idn[:], cidn)
    s.do("vector", lambda e: e.memset(t_mo[:], 0.0), inc="vv")
    def scl(out_c, in0, in1, op0=ALU.mult, op1=None, s2=None):
        s.do("vector", lambda e, out_c=out_c, in0=in0, in1=in1, op0=op0, op1=op1, s2=s2: e.tensor_scalar(
            out=t_mx[0:1, out_c:out_c + 1], in0=t_mx[0:1, in0:in0 + 1],
            scalar1=t_mx[0:1, in1:in1 + 1] if isinstance(in1, int) else in1,
            scalar2=s2, op0=op0, op1=op1 if op1 else ALU.bypass) if op1 else e.tensor_scalar(
            out=t_mx[0:1, out_c:out_c + 1], in0=t_mx[0:1, in0:in0 + 1],
            scalar1=t_mx[0:1, in1:in1 + 1] if isinstance(in1, int) else in1,
            scalar2=None, op0=op0), inc="vv")

    def mkscale(src0, n, out_max, out_s, out_rs):
        s.do("vector", lambda e, src0=src0, n=n, out_max=out_max: e.tensor_reduce(
            out=t_mx[0:1, out_max:out_max + 1], in_=t_mx[0:1, src0:src0 + n],
            axis=AX.X, op=ALU.max), waits=("ldr",), inc="vv")
        s.do("vector", lambda e, out_max=out_max, out_s=out_s: e.tensor_scalar(
            out=t_mx[0:1, out_s:out_s + 1], in0=t_mx[0:1, out_max:out_max + 1],
            scalar1=1.0 / 32767, scalar2=1e-12, op0=ALU.mult, op1=ALU.max), inc="vv")
        s.do("vector", lambda e, out_s=out_s, out_rs=out_rs: e.reciprocal(
            out=t_mx[0:1, out_rs:out_rs + 1], in_=t_mx[0:1, out_s:out_s + 1]), inc="vv")

    mkscale(0, 8, 40, 41, 42)
    mkscale(8, 8, 43, 44, 45)
    mkscale(16, 8, 46, 47, 36)
    mkscale(24, 8, 33, 34, 35)
    scl(37, 34, 45)
    s.do("vector", lambda e: e.tensor_scalar(
        out=t_mx[0:1, 38:39], in0=t_mx[0:1, 37:38], scalar1=1.0 / 65536, scalar2=None,
        op0=ALU.mult), inc="vv")
    scl(39, 42, 47)
    s.wait("tensor", "vv", "ldr")
    s.do("tensor", lambda e: e.matmul(out=p0[:, 0:2], lhsT=t_onesr[:], rhs=t_mx[0:1, 38:40],
                                      start=True, stop=True), inc="mm")
    s.do("scalar", lambda e: e.copy(out=t_s[:, 0:2], in_=p0[:, 0:2]), waits=("mm",), inc="ss")
    for mt in range(24):
        s.do("sync", lambda e, mt=mt: e.dma_start(out=t_u[:], in_=up[mt]),
             waits=("vv", "ss", "gst", "gg"), inc="ldx", dma=True)
        s.do("sync", lambda e, mt=mt: e.dma_start(out=t_p[:], in_=P[mt]), inc="ldx", dma=True)
        s.wait("vector", "ldx", "ss")
        s.do("vector", lambda e: e.tensor_scalar(
            out=t_ui[:], in0=t_u[:], scalar1=t_s[:, 1:2], scalar2=RMAGIC,
            op0=ALU.mult, op1=ALU.add), inc="vv")
        s.do("vector", lambda e: e.tensor_scalar(
            out=t_ui[:], in0=t_ui[:], scalar1=RMAGIC, scalar2=None, op0=ALU.subtract), inc="vv")
        s.do("vector", lambda e: e.tensor_scalar(
            out=t_pi[:], in0=t_p[:], scalar1=t_s[:, 0:1], scalar2=RMAGIC,
            op0=ALU.mult, op1=ALU.add), inc="vv")
        s.do("vector", lambda e: e.tensor_scalar(
            out=t_pi[:], in0=t_pi[:], scalar1=RMAGIC, scalar2=None, op0=ALU.subtract), inc="vv")
        s.do("vector", lambda e: e.tensor_tensor(
            out=t_O[:], in0=t_ui[:], in1=t_pi[:], op=ALU.mult), inc="vv")
        s.do("vector", lambda e: e.tensor_reduce(
            out=t_s[:, 2:3], in_=t_O[:], axis=AX.X, op=ALU.max,
            apply_absolute_value=True), inc="vv")
        s.do("vector", lambda e: e.tensor_tensor(
            out=t_mo[:], in0=t_mo[:], in1=t_s[:, 2:3], op=ALU.max), inc="vv")
        s.do("gpsimd", lambda e: e.tensor_copy(out=t_Ob[:], in_=t_O[:]), waits=("vv", "gst"), inc="gg")
        s.do("gpsimd", lambda e, mt=mt: e.dma_start(out=O_o[mt], in_=t_Ob[:]),
             waits=("gg",), inc="gst", dma=True)
    s.wait("tensor", "vv", "gst")
    s.do("tensor", lambda e: e.transpose(out=p0[0:1, 0:128], in_=t_mo[:], identity=t_idn[:]), inc="mm")
    s.do("scalar", lambda e: e.copy(out=t_mrow[:], in_=p0[0:1, 0:128]), waits=("mm",), inc="ss")
    s.do("vector", lambda e: e.tensor_reduce(
        out=t_mo[0:1, 0:1], in_=t_mrow[:], axis=AX.X, op=ALU.max), waits=("ss",), inc="vv")
    s.do("vector", lambda e: e.tensor_scalar(
        out=t_mo[0:1, 0:1], in0=t_mo[0:1, 0:1], scalar1=t_mx[0:1, 41:42], scalar2=t_mx[0:1, 44:45],
        op0=ALU.mult, op1=ALU.mult), inc="vv")
    s.do("gpsimd", lambda e: e.dma_start(out=maxo_o, in_=t_mo[0:1, 0:1]),
         waits=("vv",), inc="gst", dma=True)
    s.wait("gpsimd", "gst")
    s.finalize()
    ctx.close()
    return nc


def prep_E(d, ups, Ps, maxus, maxps, max1s, maxgs):
    cst = host_consts()
    row = np.zeros((1, 40), np.float32)
    row[0, 0:8] = np.concatenate([m.reshape(1) for m in maxus])
    row[0, 8:16] = np.concatenate([m.reshape(1) for m in maxps])
    row[0, 16:24] = np.concatenate([m.reshape(1) for m in max1s])
    row[0, 24:32] = np.concatenate([m.reshape(1) for m in maxgs])
    in_maps = []
    for c in range(NCORE):
        m = {"up": ups[c], "P": Ps[c], "scal": row, "conesr": cst["onesr"], "cidn": cst["idn128"]}
        in_maps.append({k: np.ascontiguousarray(v) for k, v in m.items()})
    return in_maps


def build_F():
    nc = bass.Bass("TRN2", target_bir_lowering=False, debug=False)
    dram = lambda n, s, d, k: nc.dram_tensor(n, list(s), d, kind=k).ap()
    NTOK = T // 2
    NSC = NTOK // SC
    O = dram("O", (24, 128, NTOK), BF16, "ExternalInput")
    x1 = dram("x1", (8, 128, NTOK), F32, "ExternalInput")
    wqdT = dram("wqdT", (24, 128, 1024), BF16, "ExternalInput")
    mods = dram("mods", (128, 48), F32, "ExternalInput")
    scal = dram("scal", (1, 24), F32, "ExternalInput")
    conesr = dram("conesr", (1, 128), F32, "ExternalInput")
    x2_o = dram("x2_o", (8, 128, NTOK), F32, "ExternalOutput")

    ctx = ExitStack()
    sb, ps = _mk(nc, ctx)
    s = Sched(nc, ctx)
    t_wd = sb("t_wd", (128, 24 * 1024), BF16)
    t_O = sb("t_O", (128, 24 * SC), BF16)
    t_oi = sb("t_oi", (128, 24 * SC), BF16)
    t_x1 = sb("t_x1", (128, SC), F32)
    t_x2 = sb("t_x2", (128, SC), F32)
    t_mx = sb("t_mx", (1, 48), F32)
    t_s = sb("t_s", (128, 4), F32)
    t_mods = sb("t_mods", (128, 48), F32)
    t_gbs = sb("t_gbs", (128, 8), F32)
    t_onesr = sb("t_onesr", (1, 128), F32)
    t_sq = sb("t_sq", (128, SC), F32)
    p0 = ps("p0", (128, 512), F32)
    p1 = ps("p1", (128, 512), F32)

    def ld(dst, src):
        s.do("sync", lambda e, d=dst, sr=src: e.dma_start(out=d, in_=sr), inc="ldr", dma=True)
    ld(t_mx[0:1, 0:24], scal); ld(t_onesr[:], conesr); ld(t_mods[:], mods)
    s.do("gpsimd", lambda e: e.dma_start(
        out=t_wd[:].rearrange("p (k m) -> p k m", k=24), in_=wqdT.rearrange("k p m -> p k m")),
        inc="gld", dma=True)
    def mks(src0, out_max, out_s, out_rs):
        s.do("vector", lambda e, src0=src0, out_max=out_max: e.tensor_reduce(
            out=t_mx[0:1, out_max:out_max + 1], in_=t_mx[0:1, src0:src0 + 8],
            axis=AX.X, op=ALU.max), waits=("ldr",), inc="vv")
        s.do("vector", lambda e, out_max=out_max, out_s=out_s: e.tensor_scalar(
            out=t_mx[0:1, out_s:out_s + 1], in0=t_mx[0:1, out_max:out_max + 1],
            scalar1=1.0 / 32767, scalar2=1e-12, op0=ALU.mult, op1=ALU.max), inc="vv")
        if out_rs is not None:
            s.do("vector", lambda e, out_s=out_s, out_rs=out_rs: e.reciprocal(
                out=t_mx[0:1, out_rs:out_rs + 1], in_=t_mx[0:1, out_s:out_s + 1]), inc="vv")
    mks(0, 32, 25, 26)
    mks(8, 33, 34, None)
    mks(16, 35, 36, None)
    s.do("vector", lambda e: e.tensor_scalar(
        out=t_mx[0:1, 37:38], in0=t_mx[0:1, 34:35], scalar1=t_mx[0:1, 36:37], scalar2=None,
        op0=ALU.mult), inc="vv")
    s.do("vector", lambda e: e.tensor_scalar(
        out=t_mx[0:1, 27:28], in0=t_mx[0:1, 26:27], scalar1=t_mx[0:1, 37:38], scalar2=None,
        op0=ALU.mult), inc="vv")
    s.wait("tensor", "vv", "ldr")
    s.do("tensor", lambda e: e.matmul(out=p0[:, 0:1], lhsT=t_onesr[:], rhs=t_mx[0:1, 27:28],
                                      start=True, stop=True), inc="mm")
    s.do("scalar", lambda e: e.copy(out=t_s[:, 0:1], in_=p0[:, 0:1]), waits=("mm",), inc="ss")
    s.do("tensor", lambda e: e.matmul(out=p0[:, 1:2], lhsT=t_onesr[:], rhs=t_mx[0:1, 25:26],
                                      start=True, stop=True), waits=("ss",), inc="mm")
    s.do("scalar", lambda e: e.copy(out=t_s[:, 1:2], in_=p0[:, 1:2]), waits=("mm",), inc="ss")
    s.do("vector", lambda e: e.tensor_scalar(
        out=t_gbs[:], in0=t_mods[:, 40:48], scalar1=t_s[:, 1:2], scalar2=None,
        op0=ALU.mult), waits=("ss",), inc="vv")
    s.wait("tensor", "gld", "vv")
    for sc in range(NSC):
        s.do("sync", lambda e, sc=sc: e.dma_start(
            out=t_O[:].rearrange("p (a t) -> p a t", a=24),
            in_=O[:, :, sc * SC:(sc + 1) * SC].rearrange("a p t -> p a t")),
            waits=("vv", "mm", "gst"), inc="ldx", dma=True)
        s.wait("vector", "ldx", "ss")
        for a in range(24):
            s.do("vector", lambda e, a=a: e.tensor_scalar(
                out=t_sq[:], in0=t_O[:, a * SC:(a + 1) * SC], scalar1=t_s[:, 0:1],
                scalar2=RMAGIC, op0=ALU.mult, op1=ALU.add), inc="vv")
            s.do("vector", lambda e, a=a: e.tensor_scalar(
                out=t_oi[:, a * SC:(a + 1) * SC], in0=t_sq[:], scalar1=RMAGIC, scalar2=None,
                op0=ALU.subtract), inc="vv")
        s.wait("tensor", "vv")
        for mt in range(8):
            pp = (p0, p1)[mt % 2]
            for kk in range(24):
                s.do("tensor", lambda e, mt=mt, kk=kk, pp=pp: e.matmul(
                    out=pp[:, 0:SC],
                    lhsT=t_wd[:, kk * 1024 + mt * 128:kk * 1024 + mt * 128 + 128],
                    rhs=t_oi[:, kk * SC:(kk + 1) * SC], start=(kk == 0), stop=(kk == 23)),
                    waits=("vv",) if kk == 0 else (), inc="mm" if kk == 23 else None)
            s.do("sync", lambda e, mt=mt, sc=sc: e.dma_start(
                out=t_x1[:], in_=x1[mt, :, sc * SC:(sc + 1) * SC]),
                waits=("vv",), inc="ldy", dma=True)
            s.wait("vector", "ldy", "mm")
            s.do("vector", lambda e, mt=mt, pp=pp: e.scalar_tensor_tensor(
                out=t_x2[:], in0=pp[:, 0:SC], scalar=t_gbs[:, mt:mt + 1], in1=t_x1[:],
                op0=ALU.mult, op1=ALU.add), waits=("gst",), inc="vv")
            s.do("gpsimd", lambda e, mt=mt, sc=sc: e.dma_start(
                out=x2_o[mt, :, sc * SC:(sc + 1) * SC], in_=t_x2[:]),
                waits=("vv",), inc="gst", dma=True)
    s.wait("gpsimd", "gst")
    s.finalize()
    ctx.close()
    return nc


def prep_F(d, Os, x1s, modss, wqd, maxos, maxus, maxps):
    cst = host_consts()
    row = np.zeros((1, 24), np.float32)
    row[0, 0:8] = np.concatenate([m.reshape(1) for m in maxos])
    row[0, 8:16] = np.concatenate([m.reshape(1) for m in maxus])
    row[0, 16:24] = np.concatenate([m.reshape(1) for m in maxps])
    wqdT = np.ascontiguousarray(wqd.T).reshape(24, 128, 1024)
    in_maps = []
    for c in range(NCORE):
        b = c // 2
        m = {"O": Os[c], "x1": x1s[c], "wqdT": wqdT, "mods": modss[2 * b],
             "scal": row, "conesr": cst["onesr"]}
        in_maps.append({k: np.ascontiguousarray(v) for k, v in m.items()})
    return in_maps



def build_G(dbg=False):
    nc = bass.Bass("TRN2", target_bir_lowering=False, debug=False)
    dram = lambda n, s, d, k: nc.dram_tensor(n, list(s), d, kind=k).ap()
    NSC = T // SC
    NCH = SC // LCH
    x2T = dram("x2T", (8, 128, T), F32, "ExternalInput")
    wmod = dram("wmod", (1024, 6144), F32, "ExternalInput")
    c_col = dram("c_col", (128, 8), F32, "ExternalInput")
    Wq = dram("Wq", (8, 128, 512), F32, "ExternalInput")
    Wk = dram("Wk", (8, 128, 512), F32, "ExternalInput")
    Wv = dram("Wv", (8, 128, 512), F32, "ExternalInput")
    Wg = dram("Wg", (8, 128, 512), F32, "ExternalInput")
    Wb = dram("Wb", (8, 128, 8), F32, "ExternalInput")
    cw = dram("cw", (128, 3 * 4 * 4), F32, "ExternalInput")
    cb = dram("cb", (128, 3 * 4), F32, "ExternalInput")
    nw_r = dram("nw_r", (1, 512), F32, "ExternalInput")
    ctr = dram("ctrl", (128, 128), F32, "ExternalInput")
    cmasks = dram("cmasks", (128, 128), F32, "ExternalInput")
    cmaski = dram("cmaski", (128, 128), F32, "ExternalInput")
    cidn = dram("cidn", (128, 128), F32, "ExternalInput")
    cidnb = dram("cidnb", (128, 64), BF16, "ExternalInput")
    cidnb8 = dram("cidnb8", (8, 8), BF16, "ExternalInput")
    cidn128b = dram("cidn128b", (128, 128), BF16, "ExternalInput")
    cones = dram("cones", (128, 1), F32, "ExternalInput")
    conesb = dram("conesb", (128, 1), BF16, "ExternalInput")
    conesr = dram("conesr", (1, 128), F32, "ExternalInput")
    conesrb = dram("conesrb", (1, 128), BF16, "ExternalInput")
    cones11 = dram("cones11", (1, 1), F32, "ExternalInput")
    yg1_o = dram("yg1_o", (4, 128, T), BF16, "ExternalOutput")
    mods1_o = dram("mods1_o", (128, 48), F32, "ExternalOutput")
    dbg_map = {}
    if dbg:
        for nm, shp, dt in (("dbg_qc", (128, 4 * SC), BF16), ("dbg_kc", (128, 4 * SC), BF16),
                            ("dbg_vs", (128, 4 * SC), BF16), ("dbg_gs", (128, 4 * SC), BF16),
                            ("dbg_beta", (128, (SC // LCH) * 8), F32), ("dbg_O", (128, 64), F32),
                            ("dbg_U", (128, 64), BF16), ("dbg_Y", (128, 128), BF16),
                            ("dbg_h3", (128, 8 * SC), BF16),
                            ("dbg_rq", (1, 8 * SC), BF16), ("dbg_sqb", (128, 4 * SC), BF16),
                            ("dbg_qs", (128, 4 * SC), BF16),
                            ("dbg_qpre", (128, 4 * (SC + 3)), F32),
                            ("dbg_S", (128, 8 * 64), F32), ("dbg_S0", (128, 8 * 64), F32)):
            dbg_map[nm] = dram(nm, list(shp), dt, "ExternalOutput")

    ctx = ExitStack()
    sb, ps = _mk(nc, ctx)
    s = Sched(nc, ctx)
    t_wq = sb("t_wq", (128, 8 * 512), BF16)
    t_wk = sb("t_wk", (128, 8 * 512), BF16)
    t_wv = sb("t_wv", (128, 8 * 512), BF16)
    t_wg = sb("t_wg", (128, 8 * 512), BF16)
    t_wb = sb("t_wb", (128, 8 * 8), BF16)
    t_wm = sb("t_wm", (128, 4096), F32)
    t_modrow = sb("t_modrow", (1, 6144), F32)
    t_ccol = sb("t_ccol", (128, 8), F32)
    t_cw = sb("t_cw", (128, 48), F32)
    t_cb = sb("t_cb", (128, 12), F32)
    t_mods = sb("t_mods", (128, 48), F32)
    t_onepca = sb("t_onepca", (128, 8), F32)
    t_masks = sb("t_masks", (128, 128), F32)
    t_maski = sb("t_maski", (128, 128), F32)
    t_idn = sb("t_idn", (128, 128), F32)
    t_idnb = sb("t_idnb", (128, 64), BF16)
    t_idnb8 = sb("t_idnb8", (8, 8), BF16)
    t_idn128b = sb("t_idn128b", (128, 128), BF16)
    t_ones = sb("t_ones", (128, 1), F32)
    t_onesb = sb("t_onesb", (128, 1), BF16)
    t_onesr = sb("t_onesr", (1, 128), F32)
    t_onesrb = sb("t_onesrb", (1, 128), BF16)
    t_ones11 = sb("t_ones11", (1, 1), F32)
    t_nwall = sb("t_nwall", (128, 512), F32)
    t_epsg = sb("t_epsg", (128, 1), F32)
    t_epsl2 = sb("t_epsl2", (128, 1), F32)
    t_epsrms = sb("t_epsrms", (128, 1), F32)
    t_x = sb("t_x", (128, 8 * SC), F32)
    t_sq = sb("t_sq", (128, SC), F32)
    t_sq2 = sb("t_sq2", (128, SC), F32)
    t_row1 = sb("t_row1", (1, SC), F32)
    t_row2 = sb("t_row2", (1, SC), F32)
    t_row3 = sb("t_row3", (1, SC), F32)
    t_mub = sb("t_mub", (128, SC), F32)
    t_rsb = sb("t_rsb", (128, SC), F32)
    t_h3 = sb("t_h3", (128, 8 * SC), BF16)
    t_qpre = sb("t_qpre", (128, 4 * (SC + 3)), F32)
    t_kpre = sb("t_kpre", (128, 4 * (SC + 3)), F32)
    t_vpre = sb("t_vpre", (128, 4 * (SC + 3)), F32)
    t_cv = sb("t_cv", (128, SC), F32)
    t_cv2 = sb("t_cv2", (128, SC), F32)
    t_qs = sb("t_qs", (128, 4 * SC), BF16)
    t_ks = sb("t_ks", (128, 4 * SC), BF16)
    t_vs = sb("t_vs", (128, 4 * SC), BF16)
    t_gs = sb("t_gs", (128, 4 * SC), BF16)
    t_sqb = sb("t_sqb", (128, 4 * SC), BF16)
    t_rq = sb("t_rq", (1, 8 * SC), BF16)
    t_rk = sb("t_rk", (1, 8 * SC), BF16)
    t_qc = sb("t_qc", (128, 4 * SC), BF16)
    t_kc = sb("t_kc", (128, 4 * SC), BF16)
    t_blog = sb("t_blog", (8, SC), F32)
    t_beta = sb("t_beta", (128, NCH * 8), F32)
    t_Y = sb("t_Y", (128, 128), F32)
    t_P = sb("t_P", (128, 128), F32)
    t_Z = sb("t_Z", (128, 7 * 128), F32)
    t_R = sb("t_R", (128, 64), BF16)
    t_R32 = sb("t_R32", (128, 64), F32)
    t_Rp = sb("t_Rp", (128, 64), F32)
    t_vT = sb("t_vT", (128, 64), BF16)
    t_kT = sb("t_kT", (128, 64), BF16)
    t_Gm = sb("t_Gm", (128, 128), BF16)
    t_S = sb("t_S", (128, 8 * 64), F32)
    t_Sb = sb("t_Sb", (128, 8 * 64), BF16)
    t_Oall = sb("t_Oall", (128, 512), F32)
    t_rms8 = sb("t_rms8", (128, 8), F32)
    t_Ob = sb("t_Ob", (128, 512), BF16)
    t_nwrow = sb("t_nwrow", (1, 512), F32)
    t_yg = sb("t_yg", (128, 4 * SC), BF16)
    p0 = ps("p0", (128, 512), F32)
    p1 = ps("p1", (128, 512), F32)
    p2 = ps("p2", (128, 512), F32)
    p3 = ps("p3", (128, 512), F32)
    p4 = ps("p4", (128, 512), F32)
    p6b = ps("p6b", (128, 1024), BF16)
    p7b = ps("p7b", (128, 1024), BF16)

    def ld(dst, src):
        s.do("sync", lambda e, d=dst, sr=src: e.dma_start(out=d, in_=sr), inc="ldr", dma=True)

    def gld(dst, src):
        s.do("gpsimd", lambda e, d=dst, sr=src: e.dma_start(out=d, in_=sr), inc="gld", dma=True)

    ld(t_ccol[:], c_col); ld(t_cw[:], cw); ld(t_cb[:], cb)
    ld(t_masks[:], cmasks); ld(t_maski[:], cmaski); ld(t_idn[:], cidn)
    ld(t_idnb[:], cidnb); ld(t_idnb8[:], cidnb8); ld(t_idn128b[:], cidn128b)
    ld(t_ones[:], cones); ld(t_onesb[:], conesb); ld(t_onesr[:], conesr)
    ld(t_onesrb[:], conesrb); ld(t_ones11[:], cones11); ld(t_nwrow[:], nw_r)
    gld(t_wq[:].rearrange("p (k m) -> p k m", k=8), Wq.rearrange("k p m -> p k m"))
    gld(t_wk[:].rearrange("p (k m) -> p k m", k=8), Wk.rearrange("k p m -> p k m"))
    gld(t_wv[:].rearrange("p (k m) -> p k m", k=8), Wv.rearrange("k p m -> p k m"))
    gld(t_wg[:].rearrange("p (k m) -> p k m", k=8), Wg.rearrange("k p m -> p k m"))
    gld(t_wb[:].rearrange("p (k m) -> p k m", k=8), Wb.rearrange("k p m -> p k m"))
    s.do("vector", lambda e: e.memset(t_S[:], 0.0), inc="vv")
    s.do("vector", lambda e: e.memset(t_epsg[:], EPS_LN), inc="vv")
    s.do("vector", lambda e: e.memset(t_epsl2[:], 1e-6), inc="vv")
    s.do("vector", lambda e: e.memset(t_epsrms[:], 1e-5), inc="vv")
    s.do("vector", lambda e: e.memset(t_Sb[:], 0.0), inc="vv")
    s.do("vector", lambda e: e.memset(t_qpre[:], 0.0), inc="vv")
    s.do("vector", lambda e: e.memset(t_kpre[:], 0.0), inc="vv")
    s.do("vector", lambda e: e.memset(t_vpre[:], 0.0), inc="vv")
    s.wait("tensor", "ldr")
    s.do("tensor", lambda e: e.matmul(out=p0[:, 0:512], lhsT=t_onesr[:], rhs=t_nwrow[:],
                                      start=True, stop=True), inc="mm")
    s.do("scalar", lambda e: e.copy(out=t_nwall[:], in_=p0[:, 0:512]), waits=("mm",), inc="ss")
    s.do("scalar", lambda e: e.activation(out=t_ccol[:], in_=t_ccol[:], func=AF.Silu),
         waits=("ldr",), inc="ss")
    for cchunk in range(12):
        s.do("sync", lambda e, cc=cchunk: e.dma_start(
            out=t_wm[:].rearrange("p (kk m) -> p kk m", kk=8),
            in_=wmod[:, cc * 512:(cc + 1) * 512].rearrange("(kk p) m -> p kk m", p=128)),
            waits=("mmod",) if cchunk > 0 else ("ss",), inc="ldw", dma=True)
        s.wait("tensor", "ldw", "ss")
        for kk in range(8):
            s.do("tensor", lambda e, kk=kk: e.matmul(
                out=p1[0:1, 0:512], lhsT=t_ccol[:, kk:kk + 1],
                rhs=t_wm[:, kk * 512:(kk + 1) * 512], start=(kk == 0), stop=(kk == 7)),
                inc="mm" if kk == 7 else None)
        s.do("scalar", lambda e, cc=cchunk: e.copy(
            out=t_modrow[0:1, cc * 512:(cc + 1) * 512], in_=p1[0:1, 0:512]),
            waits=("mm",), inc="mmod")
    s.wait("tensor", "mmod")
    for col in range(48):
        s.do("tensor", lambda e, col=col: e.matmul(
            out=p1[:, col:col + 1], lhsT=t_modrow[0:1, col * 128:(col + 1) * 128],
            rhs=t_ones11[:], start=True, stop=True), inc="mm" if col == 47 else None)
    s.do("scalar", lambda e: e.copy(out=t_mods[:], in_=p1[:, 0:48]), waits=("mm",), inc="ss")
    for c0 in (16, 40):
        s.do("scalar", lambda e, c0=c0: e.activation(
            out=t_sq[:, 0:8], in_=t_mods[:, c0:c0 + 8], func=AF.Abs), waits=("ss", "vv"), inc="ss")
        s.do("vector", lambda e, c0=c0: e.tensor_scalar(
            out=t_sq[:, 0:8], in0=t_sq[:, 0:8], scalar1=0.25, scalar2=None,
            op0=ALU.max), waits=("ss",), inc="vv")
        s.do("scalar", lambda e, c0=c0: e.activation(
            out=t_sq[:, 8:16], in_=t_mods[:, c0:c0 + 8], func=AF.Sign), waits=("vv",), inc="ss")
        s.do("vector", lambda e, c0=c0: e.tensor_tensor(
            out=t_mods[:, c0:c0 + 8], in0=t_sq[:, 0:8], in1=t_sq[:, 8:16],
            op=ALU.mult), waits=("ss",), inc="vv")
    s.do("gpsimd", lambda e: e.dma_start(out=mods1_o, in_=t_mods[:]),
         waits=("vv",), inc="gst", dma=True)
    s.do("vector", lambda e: e.tensor_scalar(
        out=t_onepca[:], in0=t_mods[:, 8:16], scalar1=1.0, scalar2=None, op0=ALU.add),
        waits=("vv",), inc="vv")
    s.wait("tensor", "gld", "vv", "ss")

    for sc in range(NSC):
        s.do("sync", lambda e, sc=sc: e.dma_start(
            out=t_x[:].rearrange("p (a t) -> p a t", a=8),
            in_=x2T[:, :, sc * SC:(sc + 1) * SC].rearrange("a p t -> p a t")),
            waits=("vv", "ss", "mm", "gg", "gst"), inc="ldx", dma=True)
        s.wait("tensor", "ldx")
        s.wait("scalar", "ldx")
        for a in range(8):
            s.do("tensor", lambda e, a=a: e.matmul(
                out=p0[0:1, 0:SC], lhsT=t_ones[:], rhs=t_x[:, a * SC:(a + 1) * SC],
                start=(a == 0), stop=(a == 7)), inc="mm" if a == 7 else None)
        for a in range(8):
            buf = t_sq if a % 2 == 0 else t_sq2
            s.do("scalar", lambda e, a=a, buf=buf: e.activation(
                out=buf[:], in_=t_x[:, a * SC:(a + 1) * SC], func=AF.Square),
                waits=("mm",), inc="ss")
            s.wait("tensor", "ss")
            s.do("tensor", lambda e, a=a, buf=buf: e.matmul(
                out=p1[0:1, 0:SC], lhsT=t_ones[:], rhs=buf[:],
                start=(a == 0), stop=(a == 7)), inc="mm")
        s.do("scalar", lambda e: e.activation(out=t_row1[:], in_=p0[0:1, 0:SC],
                                              func=AF.Copy, scale=1.0 / 1024), waits=("mm",), inc="ss")
        s.do("scalar", lambda e: e.activation(out=t_row2[:], in_=p1[0:1, 0:SC],
                                              func=AF.Copy, scale=1.0 / 1024), inc="ss")
        s.do("vector", lambda e: e.tensor_copy(out=t_row3[:], in_=t_row1[:]), waits=("ss",), inc="vv")
        s.do("vector", lambda e: e.tensor_tensor(
            out=t_row3[:], in0=t_row3[:], in1=t_row1[:], op=ALU.mult), inc="vv")
        s.do("vector", lambda e: e.tensor_tensor(
            out=t_row2[:], in0=t_row2[:], in1=t_row3[:], op=ALU.subtract), inc="vv")
        s.do("scalar", lambda e: e.activation(out=t_row3[:], in_=t_row2[:], func=AF.Sqrt,
                                              bias=t_epsg[0:1, :]), waits=("vv",), inc="ss")
        s.do("vector", lambda e: e.reciprocal(out=t_row2[:], in_=t_row3[:]), waits=("ss",), inc="vv")
        s.wait("tensor", "vv")
        s.do("tensor", lambda e: e.matmul(out=p2[:, 0:SC], lhsT=t_onesr[:],
                                          rhs=t_row1[:], start=True, stop=True), inc="mm")
        s.do("tensor", lambda e: e.matmul(out=p3[:, 0:SC], lhsT=t_onesr[:],
                                          rhs=t_row2[:], start=True, stop=True), inc="mm")
        s.do("scalar", lambda e: e.copy(out=t_mub[:], in_=p2[:, 0:SC]), waits=("mm",), inc="ss")
        s.do("scalar", lambda e: e.copy(out=t_rsb[:], in_=p3[:, 0:SC]), inc="ss")
        for a in range(8):
            buf = t_sq if a % 2 == 0 else t_sq2
            s.do("vector", lambda e, a=a, buf=buf: e.tensor_tensor(
                out=buf[:], in0=t_x[:, a * SC:(a + 1) * SC], in1=t_mub[:],
                op=ALU.subtract), waits=("ss",), inc="vv")
            s.do("vector", lambda e, a=a, buf=buf: e.tensor_tensor(
                out=buf[:], in0=buf[:], in1=t_rsb[:], op=ALU.mult), inc="vv")
            s.do("vector", lambda e, a=a, buf=buf: e.tensor_scalar(
                out=t_h3[:, a * SC:(a + 1) * SC], in0=buf[:],
                scalar1=t_onepca[:, a:a + 1], scalar2=t_mods[:, a:a + 1],
                op0=ALU.mult, op1=ALU.add), inc="vv")
        for tt in (t_qpre, t_kpre, t_vpre):
            for a in range(4):
                s.do("vector", lambda e, tt=tt, a=a: e.tensor_copy(
                    out=tt[:, a * (SC + 3):a * (SC + 3) + 3],
                    in_=tt[:, a * (SC + 3) + SC:a * (SC + 3) + SC + 3]), inc="vv")
        s.wait("tensor", "vv")
        projs = [(t_wq, t_qpre, 1), (t_wk, t_kpre, 1), (t_wv, t_vpre, 1)]
        for wt, dst, _ in projs:
            for mt in range(4):
                pp = (p2, p3)[mt % 2]
                for kk in range(8):
                    s.do("tensor", lambda e, wt=wt, mt=mt, kk=kk, pp=pp: e.matmul(
                        out=pp[:, 0:SC],
                        lhsT=wt[:, kk * 512 + mt * 128:kk * 512 + mt * 128 + 128],
                        rhs=t_h3[:, kk * SC:(kk + 1) * SC], start=(kk == 0), stop=(kk == 7)),
                        waits=("ss", "vv") if kk == 0 else (), inc="mm" if kk == 7 else None)
                s.do("scalar", lambda e, dst=dst, mt=mt, pp=pp: e.copy(
                    out=dst[:, mt * (SC + 3) + 3:mt * (SC + 3) + 3 + SC], in_=pp[:, 0:SC]),
                    waits=("mm",), inc="ss")
        for mt in range(4):
            pp = (p2, p3)[mt % 2]
            for kk in range(8):
                s.do("tensor", lambda e, mt=mt, kk=kk, pp=pp: e.matmul(
                    out=pp[:, 0:SC],
                    lhsT=t_wg[:, kk * 512 + mt * 128:kk * 512 + mt * 128 + 128],
                    rhs=t_h3[:, kk * SC:(kk + 1) * SC], start=(kk == 0), stop=(kk == 7)),
                    waits=("ss",) if kk == 0 else (), inc="mm" if kk == 7 else None)
            s.do("scalar", lambda e, mt=mt, pp=pp: e.activation(
                out=t_gs[:, mt * SC:(mt + 1) * SC], in_=pp[:, 0:SC], func=AF.Silu),
                waits=("mm",), inc="ss")
        for kk in range(8):
            s.do("tensor", lambda e, kk=kk: e.matmul(
                out=p4[0:8, 0:SC], lhsT=t_wb[:, kk * 8:(kk + 1) * 8],
                rhs=t_h3[:, kk * SC:(kk + 1) * SC], start=(kk == 0), stop=(kk == 7)),
                waits=("ss",) if kk == 0 else (), inc="mm" if kk == 7 else None)
        s.do("scalar", lambda e: e.copy(out=t_blog[:], in_=p4[0:8, 0:SC]), waits=("mm",), inc="ss")
        s.wait("tensor", "ss")
        for ch in range(NCH):
            s.do("tensor", lambda e, ch=ch: e.transpose(
                out=p0[:, 0:8], in_=t_blog[0:8, ch * LCH:(ch + 1) * LCH],
                identity=t_idn[0:8, 0:8]), waits=("ss", "vv"), inc="mm")
            s.do("scalar", lambda e, ch=ch: e.activation(
                out=t_beta[:, ch * 8:(ch + 1) * 8], in_=p0[:, 0:8], func=AF.Sigmoid),
                waits=("mm",), inc="ss")
        for pi, (src_t, dst_t) in enumerate(((t_qpre, t_qs), (t_kpre, t_ks), (t_vpre, t_vs))):
            for a in range(4):
                base_c = pi * 16 + a * 4
                s.do("vector", lambda e, src_t=src_t, a=a, base_c=base_c, pi=pi: e.tensor_scalar(
                    out=t_cv[:], in0=src_t[:, a * (SC + 3):a * (SC + 3) + SC],
                    scalar1=t_cw[:, base_c:base_c + 1], scalar2=t_cb[:, pi * 4 + a:pi * 4 + a + 1],
                    op0=ALU.mult, op1=ALU.add), waits=("ss",), inc="vv")
                bufs = (t_cv, t_cv2)
                for j in (1, 2, 3):
                    src_b, dst_b = bufs[(j + 1) % 2], bufs[j % 2]
                    s.do("vector", lambda e, src_t=src_t, a=a, base_c=base_c, j=j,
                         src_b=src_b, dst_b=dst_b: e.scalar_tensor_tensor(
                        out=dst_b[:], in0=src_t[:, a * (SC + 3) + j:a * (SC + 3) + j + SC],
                        scalar=t_cw[:, base_c + j:base_c + j + 1], in1=src_b[:],
                        op0=ALU.mult, op1=ALU.add), inc="vv")
                s.do("scalar", lambda e, dst_t=dst_t, a=a: e.activation(
                    out=dst_t[:, a * SC:(a + 1) * SC], in_=t_cv2[:], func=AF.Silu),
                    waits=("vv", "gg"), inc="ss")
        for src_t, rrow, scale_extra in ((t_qs, t_rq, 0.125), (t_ks, t_rk, 1.0)):
            for a in range(4):
                s.do("scalar", lambda e, src_t=src_t, a=a: e.activation(
                    out=t_sqb[:, a * SC:(a + 1) * SC], in_=src_t[:, a * SC:(a + 1) * SC],
                    func=AF.Square), waits=("ss",), inc="ss")
            s.wait("tensor", "ss")
            for h in range(8):
                a, base = h // 2, (h % 2) * 64
                s.do("tensor", lambda e, h=h, a=a, base=base: e.matmul(
                    out=p0[0:1, 0:SC], lhsT=t_onesb[base:base + 64, :],
                    rhs=t_sqb[base:base + 64, a * SC:(a + 1) * SC], start=True, stop=True),
                    waits=("vv", "ss"), inc="mm")
                s.do("scalar", lambda e, h=h: e.activation(
                    out=t_row3[0:1, 0:SC], in_=p0[0:1, 0:SC], func=AF.Sqrt,
                    bias=t_epsl2[0:1, :]), waits=("mm",), inc="ss")
                s.do("vector", lambda e, h=h: e.reciprocal(
                    out=t_row2[0:1, 0:SC], in_=t_row3[0:1, 0:SC]), waits=("ss",), inc="vv")
                s.do("vector", lambda e, h=h, rrow=rrow, se=scale_extra: e.tensor_scalar(
                    out=rrow[0:1, h * SC:(h + 1) * SC], in0=t_row2[0:1, 0:SC],
                    scalar1=se, scalar2=None, op0=ALU.mult), inc="vv")
        s.wait("tensor", "vv")
        for src_t, rrow, dst_t in ((t_qs, t_rq, t_qc), (t_ks, t_rk, t_kc)):
            for h in range(8):
                a, base = h // 2, (h % 2) * 64
                s.do("tensor", lambda e, h=h, base=base, rrow=rrow: e.matmul(
                    out=p4[base:base + 64, 0:SC],
                    lhsT=t_onesrb[0:1, 0:64], rhs=rrow[0:1, h * SC:(h + 1) * SC],
                    start=True, stop=True, tile_position=(0, base)),
                    waits=("vv",), inc="mm")
                s.do("vector", lambda e, h=h, a=a, base=base, src_t=src_t, dst_t=dst_t: e.tensor_tensor(
                    out=dst_t[base:base + 64, a * SC:(a + 1) * SC],
                    in0=src_t[base:base + 64, a * SC:(a + 1) * SC],
                    in1=p4[base:base + 64, 0:SC], op=ALU.mult),
                    waits=("mm",), inc="vv")
        if dbg and sc == 0:
            for nm, tt in (("dbg_qc", t_qc), ("dbg_kc", t_kc), ("dbg_vs", t_vs),
                           ("dbg_gs", t_gs), ("dbg_beta", t_beta), ("dbg_h3", t_h3),
                           ("dbg_rq", t_rq), ("dbg_qs", t_qs), ("dbg_qpre", t_qpre)):
                s.do("gpsimd", lambda e, nm=nm, tt=tt: e.dma_start(
                    out=dbg_map[nm], in_=tt[:]), waits=("vv", "ss"), inc="gst", dma=True)
            s.wait("vector", "gst"); s.wait("tensor", "gst"); s.wait("scalar", "gst")
        for ch in range(NCH):
            if dbg and sc == 0 and ch == 1:
                s.do("gpsimd", lambda e: e.dma_start(out=dbg_map["dbg_S"], in_=t_S[:]),
                     waits=("vv", "ss"), inc="gst", dma=True)
                s.wait("vector", "gst"); s.wait("tensor", "gst"); s.wait("scalar", "gst")
            tof = ch * LCH
            for h in range(8):
                a, base = h // 2, (h % 2) * 64
                qh = t_qc[base:base + 64, a * SC + tof:a * SC + tof + LCH]
                kh = t_kc[base:base + 64, a * SC + tof:a * SC + tof + LCH]
                vh = t_vs[base:base + 64, a * SC + tof:a * SC + tof + LCH]
                gh = t_gs[base:base + 64, a * SC + tof:a * SC + tof + LCH]
                bcol = t_beta[:, ch * 8 + h:ch * 8 + h + 1]
                Sh = (slice(base, base + 64), slice(h * 64, (h + 1) * 64))
                s.do("tensor", lambda e, kh=kh: e.matmul(
                    out=p2[:, 0:128], lhsT=kh, rhs=kh, start=True, stop=True),
                    waits=("vv", "ss"), inc="mm")
                s.do("vector", lambda e, bcol=bcol: e.scalar_tensor_tensor(
                    out=t_Y[:], in0=p2[:, 0:128], scalar=bcol, in1=t_masks[:],
                    op0=ALU.mult, op1=ALU.mult), waits=("mm",), inc="vv")
                s.wait("tensor", "vv")
                s.do("tensor", lambda e: e.transpose(
                    out=p2[:, 256:384], in_=t_Y[:], identity=t_idn[:]), inc="mm")
                s.do("scalar", lambda e: e.copy(out=t_Z[:, 0:128], in_=p2[:, 256:384]),
                     waits=("mm",), inc="ss")
                s.do("vector", lambda e: e.tensor_copy(out=t_P[:], in_=t_Y[:]), inc="vv")
                for k in range(1, 7):
                    s.do("tensor", lambda e, k=k: e.matmul(
                        out=p2[:, 256:384], lhsT=t_Z[:, (k - 1) * 128:k * 128], rhs=t_P[:],
                        start=True, stop=True), waits=("ss", "vv"), inc="mm")
                    s.do("tensor", lambda e, k=k: e.matmul(
                        out=p3[:, 256:384], lhsT=t_P[:], rhs=t_Z[:, (k - 1) * 128:k * 128],
                        start=True, stop=True), inc="mm")
                    s.do("vector", lambda e, k=k: e.tensor_copy(
                        out=t_P[:], in_=p2[:, 256:384]), waits=("mm",), inc="vv")
                    s.do("scalar", lambda e, k=k: e.copy(
                        out=t_Z[:, k * 128:(k + 1) * 128], in_=p3[:, 256:384]),
                        waits=("mm",), inc="ss")
                s.do("tensor", lambda e, vh=vh, base=base: e.transpose(
                    out=p6b[:, 256:320], in_=vh, identity=t_idnb[base:base + 64, :]),
                    waits=("ss",), inc="mm")
                s.do("vector", lambda e: e.tensor_copy(out=t_vT[:], in_=p6b[:, 256:320]),
                     waits=("mm",), inc="vv")
                s.do("tensor", lambda e, kh=kh, base=base: e.transpose(
                    out=p6b[:, 320:384], in_=kh, identity=t_idnb[base:base + 64, :]),
                    waits=("vv",), inc="mm")
                s.do("vector", lambda e: e.tensor_copy(out=t_kT[:], in_=p6b[:, 320:384]),
                     waits=("mm",), inc="vv")
                s.do("tensor", lambda e, kh=kh, Sh=Sh: e.matmul(
                    out=p3[:, 0:64], lhsT=kh, rhs=t_Sb[Sh[0], Sh[1]],
                    start=True, stop=True), waits=("vv",), inc="mm")
                s.do("vector", lambda e: e.tensor_tensor(
                    out=t_Rp[:], in0=t_vT[:], in1=p3[:, 0:64], op=ALU.subtract),
                    waits=("mm", "ss"), inc="vv")
                s.do("vector", lambda e, bcol=bcol: e.tensor_scalar(
                    out=t_R32[:], in0=t_Rp[:], scalar1=bcol, scalar2=None, op0=ALU.mult), inc="vv")
                for k in range(7):
                    s.do("tensor", lambda e, k=k: e.matmul(
                        out=p3[:, 64:128], lhsT=t_Z[:, k * 128:(k + 1) * 128], rhs=t_R32[:],
                        start=True, stop=True), waits=("vv",), inc="mm")
                    s.do("vector", lambda e, k=k: e.tensor_tensor(
                        out=t_R32[:], in0=t_R32[:], in1=p3[:, 64:128], op=ALU.add),
                        waits=("mm",), inc="vv")
                s.do("vector", lambda e: e.tensor_copy(out=t_R[:], in_=t_R32[:]), inc="vv")
                s.wait("tensor", "vv")
                s.do("tensor", lambda e, base=base, Sh=Sh: e.matmul(
                    out=p4[Sh[0], 0:64], lhsT=t_kT[:], rhs=t_R[:],
                    start=True, stop=True, tile_position=(0, base)), inc="mm")
                s.do("tensor", lambda e, kh=kh, qh=qh: e.matmul(
                    out=p2[:, 128:256], lhsT=kh, rhs=qh, start=True, stop=True), inc="mm")
                s.do("vector", lambda e: e.tensor_tensor(
                    out=t_Gm[:], in0=p2[:, 128:256], in1=t_maski[:], op=ALU.mult),
                    waits=("mm",), inc="vv")
                s.wait("tensor", "vv")
                s.do("tensor", lambda e: e.matmul(
                    out=p3[:, 128:192], lhsT=t_Gm[:], rhs=t_R[:], start=True, stop=False))
                s.do("tensor", lambda e, qh=qh, Sh=Sh: e.matmul(
                    out=p3[:, 128:192], lhsT=qh, rhs=t_Sb[Sh[0], Sh[1]],
                    start=False, stop=True), inc="mm")
                s.do("vector", lambda e, Sh=Sh: e.tensor_tensor(
                    out=t_S[Sh[0], Sh[1]], in0=t_S[Sh[0], Sh[1]], in1=p4[Sh[0], 0:64],
                    op=ALU.add), waits=("mm",), inc="vv")
                s.do("vector", lambda e, Sh=Sh: e.tensor_copy(
                    out=t_Sb[Sh[0], Sh[1]], in_=t_S[Sh[0], Sh[1]]), inc="vv")
                if dbg and sc == 0 and ch == 0 and h == 7:
                    s.do("gpsimd", lambda e: e.dma_start(out=dbg_map["dbg_S0"], in_=t_S[:]),
                         waits=("vv",), inc="gst", dma=True)
                    s.wait("vector", "gst"); s.wait("tensor", "gst"); s.wait("scalar", "gst")
                s.do("scalar", lambda e, h=h: e.copy(
                    out=t_Oall[:, h * 64:(h + 1) * 64], in_=p3[:, 128:192]),
                    waits=("mm", "gg") if h == 0 else ("mm",), inc="ss")
                if dbg and sc == 0 and ch == 0 and h == 0:
                    s.do("gpsimd", lambda e: e.dma_start(out=dbg_map["dbg_O"],
                                                         in_=t_Oall[:, 0:64]),
                         waits=("ss",), inc="gst", dma=True)
                    s.do("gpsimd", lambda e: e.dma_start(out=dbg_map["dbg_U"], in_=t_R[:]),
                         waits=("ss",), inc="gst", dma=True)
                    s.do("gpsimd", lambda e: e.dma_start(out=dbg_map["dbg_Y"], in_=t_Y[:]),
                         waits=("ss",), inc="gst", dma=True)
                    s.wait("vector", "gst"); s.wait("tensor", "gst"); s.wait("scalar", "gst")
            for h2 in range(8):
                s.do("scalar", lambda e, h2=h2: e.activation(
                    out=t_Rp[:], in_=t_Oall[:, h2 * 64:(h2 + 1) * 64], func=AF.Square,
                    accum_out=t_rms8[:, h2:h2 + 1]),
                    waits=("vv",) if h2 == 0 else (), inc="ss")
            s.do("scalar", lambda e: e.activation(
                out=t_rms8[:], in_=t_rms8[:], func=AF.Sqrt, scale=1.0 / 64,
                bias=t_epsrms[:]), waits=("vv",), inc="ss")
            s.do("vector", lambda e: e.reciprocal(out=t_rms8[:], in_=t_rms8[:]),
                 waits=("ss",), inc="vv")
            s.do("vector", lambda e: e.tensor_tensor(
                out=t_Oall[:].rearrange("p (h n) -> p h n", n=64),
                in0=t_Oall[:].rearrange("p (h n) -> p h n", n=64),
                in1=t_rms8[:].rearrange("p (h o) -> p h o", o=1).broadcast_to((128, 8, 64)),
                op=ALU.mult), inc="vv")
            s.do("vector", lambda e: e.tensor_tensor(
                out=t_Oall[:], in0=t_Oall[:], in1=t_nwall[:], op=ALU.mult), inc="vv")
            for a in range(4):
                pb = (p6b, p7b)[a % 2]
                s.do("tensor", lambda e, a=a, tof=tof, pb=pb: e.transpose(
                    out=pb[:, 384:512], in_=t_gs[:, a * SC + tof:a * SC + tof + LCH],
                    identity=t_idn128b[:]), waits=("vv",), inc="mm")
                s.do("vector", lambda e, a=a, pb=pb: e.tensor_tensor(
                    out=t_Oall[:, a * 128:(a + 1) * 128],
                    in0=t_Oall[:, a * 128:(a + 1) * 128],
                    in1=pb[:, 384:512], op=ALU.mult), waits=("mm",), inc="vv")
            s.do("gpsimd", lambda e: e.tensor_copy(out=t_Ob[:], in_=t_Oall[:]),
                 waits=("vv",), inc="gg")
            s.wait("tensor", "gg")
            for a in range(4):
                pb = (p6b, p7b)[a % 2]
                s.do("tensor", lambda e, a=a, pb=pb: e.transpose(
                    out=pb[:, 640:768], in_=t_Ob[:, a * 128:(a + 1) * 128],
                    identity=t_idn128b[:]), waits=("vv",) if a >= 2 else (), inc="mm")
                s.do("vector", lambda e, a=a, tof=tof, pb=pb: e.tensor_copy(
                    out=t_yg[:, a * SC + tof:a * SC + tof + LCH],
                    in_=pb[:, 640:768]), waits=("mm",), inc="vv")
        for a in range(4):
            s.do("gpsimd", lambda e, sc=sc, a=a: e.dma_start(
                out=yg1_o[a, :, sc * SC:(sc + 1) * SC],
                in_=t_yg[:, a * SC:(a + 1) * SC]), waits=("vv",), inc="gst", dma=True)

    s.wait("gpsimd", "gst")
    s.finalize()
    ctx.close()
    return nc


def prep_G(d, x2s):
    cst = host_consts()
    i = np.arange(LCH)
    masks = -((i[:, None] > i[None, :]).astype(np.float32))
    maski = (i[:, None] <= i[None, :]).astype(np.float32)
    idn128b = np.eye(128, dtype=np.float32).astype(BF)
    in_maps = []
    for c in range(NCORE):
        b, hg = c // 2, c % 2
        SL = slice(hg * 512, hg * 512 + 512)
        x2_full = np.concatenate([x2s[2 * b], x2s[2 * b + 1]], axis=2)
        cwm = np.zeros((128, 48), np.float32)
        cbm = np.zeros((128, 12), np.float32)
        for pi, nm in enumerate(("cq", "ck", "cv")):
            w = np.asarray(d[f"L1_attn_{nm}_w"])[SL]
            bb = np.asarray(d[f"L1_attn_{nm}_b"])[SL]
            for a in range(4):
                cwm[:, pi * 16 + a * 4: pi * 16 + (a + 1) * 4] = w[a * 128:(a + 1) * 128]
                cbm[:, pi * 4 + a] = bb[a * 128:(a + 1) * 128]
        m = {
            "x2T": x2_full,
            "wmod": d["L1_wmod"],
            "c_col": np.ascontiguousarray(np.asarray(d["L1_c"][b]).reshape(8, 128).T),
            "Wq": np.ascontiguousarray(np.asarray(d["L1_attn_Wq"])[:, SL]).reshape(8, 128, 512),
            "Wk": np.ascontiguousarray(np.asarray(d["L1_attn_Wk"])[:, SL]).reshape(8, 128, 512),
            "Wv": np.ascontiguousarray(np.asarray(d["L1_attn_Wv"])[:, SL]).reshape(8, 128, 512),
            "Wg": np.ascontiguousarray(np.asarray(d["L1_attn_Wg"])[:, SL]).reshape(8, 128, 512),
            "Wb": np.ascontiguousarray(np.asarray(d["L1_attn_Wb"])[:, hg * 8:hg * 8 + 8]).reshape(8, 128, 8),
            "cw": cwm, "cb": cbm,
            "nw_r": np.tile(np.asarray(d["L1_attn_nw"]).reshape(64), 8).reshape(1, 512),
            "ctrl": cst["tril_incl"], "cmasks": masks, "cmaski": maski,
            "cidn": cst["idn128"], "cidnb": np.vstack([cst["idn64b"], cst["idn64b"]]),
            "cidnb8": np.eye(8, dtype=np.float32).astype(BF),
            "cidn128b": idn128b,
            "cones": cst["ones128"], "conesb": cst["ones128"].astype(BF),
            "conesr": cst["onesr"], "conesrb": cst["onesr"].astype(BF),
            "cones11": cst["ones11"],
        }
        in_maps.append({k: np.ascontiguousarray(v) for k, v in m.items()})
    return in_maps



CAPO = 768


def build_H(dbg=False):
    nc = bass.Bass("TRN2", target_bir_lowering=False, debug=False)
    dram = lambda n, s, d, k: nc.dram_tensor(n, list(s), d, kind=k).ap()
    NTOK = T // 2
    NSC = NTOK // SC
    NTT = NTOK // 128
    yg1 = dram("yg1", (8, 128, NTOK), BF16, "ExternalInput")
    x2T = dram("x2T", (8, 128, NTOK), F32, "ExternalInput")
    Wo1 = dram("Wo1", (8, 128, 1024), F32, "ExternalInput")
    mods1 = dram("mods1", (128, 48), F32, "ExternalInput")
    rwT = dram("rwT", (8, 128, 8), F32, "ExternalInput")
    tokid = dram("tokid", (NTOK, 1), F32, "ExternalInput")
    iot8 = dram("iot8", (128, 8), F32, "ExternalInput")
    iotm8 = dram("iotm8", (128, 8), F32, "ExternalInput")
    ctsu = dram("ctsu", (128, 128), F32, "ExternalInput")
    cidn = dram("cidn", (128, 128), F32, "ExternalInput")
    cidnb_h = dram("cidnb_h", (128, 128), BF16, "ExternalInput")
    cones = dram("cones", (128, 1), F32, "ExternalInput")
    conesr = dram("conesr", (1, 128), F32, "ExternalInput")
    x3_o = dram("x3_o", (8, 128, NTOK), F32, "ExternalOutput")
    h4_o = dram("h4_o", (NTOK, 1024), BF16, "ExternalOutput")
    td_o = dram("td_o", (NTOK, 8), F32, "ExternalOutput")
    cnt_o = dram("cnt_o", (4, 8), F32, "ExternalOutput")
    idl_o = dram("idl_o", (8 * 2048, 1), F32, "ExternalOutput")
    if dbg:
        dbg_lg = dram("dbg_lg", (128, 8), F32, "ExternalOutput")
        dbg_pr = dram("dbg_pr", (128, 8), F32, "ExternalOutput")

    ctx = ExitStack()
    sb, ps = _mk(nc, ctx)
    s = Sched(nc, ctx)
    t_wo = sb("t_wo", (128, 8 * 1024), BF16)
    t_rw = sb("t_rw", (128, 8 * 8), F32)
    t_yg = sb("t_yg", (128, 8 * SC), BF16)
    t_x = sb("t_x", (128, 8 * SC), F32)
    t_x3 = sb("t_x3", (128, 8 * SC), F32)
    t_h4 = sb("t_h4", (128, 8 * SC), BF16)
    t_h4f = sb("t_h4f", (128, 8 * SC), F32)
    t_h4t = sb("t_h4t", (128, 1024), BF16)
    t_mods = sb("t_mods", (128, 48), F32)
    t_onepcb = sb("t_onepcb", (128, 8), F32)
    t_iot = sb("t_iot", (128, 8), F32)
    t_iotm = sb("t_iotm", (128, 8), F32)
    t_tsu = sb("t_tsu", (128, 128), F32)
    t_idn = sb("t_idn", (128, 128), F32)
    t_idnb2 = sb("t_idnb2", (128, 128), BF16)
    t_ones = sb("t_ones", (128, 1), F32)
    t_onesr = sb("t_onesr", (1, 128), F32)
    t_eps = sb("t_eps", (128, 1), F32)
    t_sq = sb("t_sq", (128, SC), F32)
    t_sq2 = sb("t_sq2", (128, SC), F32)
    t_row1 = sb("t_row1", (1, SC), F32)
    t_row2 = sb("t_row2", (1, SC), F32)
    t_row3 = sb("t_row3", (1, SC), F32)
    t_mub = sb("t_mub", (128, SC), F32)
    t_rsb = sb("t_rsb", (128, SC), F32)
    t_lg = sb("t_lg", (128, 8), F32)
    t_pr = sb("t_pr", (128, 8), F32)
    t_td = sb("t_td", (128, 8), F32)
    t_ind = sb("t_ind", (128, 3 * 8), F32)
    t_rk = sb("t_rk", (128, 3 * 8), F32)
    t_offs = sb("t_offs", (1, 3 * 8), F32)
    t_rowt = sb("t_rowt", (1, 3 * 8), F32)
    t_c1 = sb("t_c1", (128, 8), F32)
    t_c2 = sb("t_c2", (128, 8), F32)
    t_tid = sb("t_tid", (128, 1), F32)
    t_idx = sb("t_idx", (128, 1), U32)
    t_idxf = sb("t_idxf", (128, 1), F32)
    t_zro = sb("t_zro", (128, 128), F32)
    p0 = ps("p0", (128, 512), F32)
    p1 = ps("p1", (128, 512), F32)
    p2 = ps("p2", (128, 512), F32)
    p3 = ps("p3", (128, 512), F32)
    p2b = ps("p2b", (128, 1024), BF16)

    def ld(dst, src):
        s.do("sync", lambda e, d=dst, sr=src: e.dma_start(out=d, in_=sr), inc="ldr", dma=True)
    ld(t_mods[:], mods1); ld(t_iot[:], iot8); ld(t_iotm[:], iotm8)
    ld(t_tsu[:], ctsu); ld(t_idn[:], cidn)
    ld(t_idnb2[:], cidnb_h)
    ld(t_ones[:], cones); ld(t_onesr[:], conesr)
    s.do("gpsimd", lambda e: e.dma_start(
        out=t_wo[:].rearrange("p (k m) -> p k m", k=8), in_=Wo1.rearrange("k p m -> p k m")),
        inc="gld", dma=True)
    s.do("gpsimd", lambda e: e.dma_start(
        out=t_rw[:].rearrange("p (k m) -> p k m", k=8), in_=rwT.rearrange("k p m -> p k m")),
        inc="gld", dma=True)
    s.do("vector", lambda e: e.memset(t_eps[:], EPS_LN), inc="vv")
    s.do("vector", lambda e: e.memset(t_offs[:], 0.0), inc="vv")
    s.do("vector", lambda e: e.memset(t_zro[:], 0.0), inc="vv")
    s.do("gpsimd", lambda e: e.dma_start(
        out=idl_o.rearrange("(p n) o -> p (n o)", p=128), in_=t_zro[:]),
        waits=("vv",), inc="gst", dma=True)
    s.do("vector", lambda e: e.tensor_scalar(
        out=t_onepcb[:], in0=t_mods[:, 32:40], scalar1=1.0, scalar2=None, op0=ALU.add),
        waits=("ldr",), inc="vv")
    s.wait("tensor", "gld", "ldr", "vv")
    for sc in range(NSC):
        s.do("sync", lambda e, sc=sc: e.dma_start(
            out=t_yg[:].rearrange("p (a t) -> p a t", a=8),
            in_=yg1[:, :, sc * SC:(sc + 1) * SC].rearrange("a p t -> p a t")),
            waits=("vv", "mm", "gst", "ss"), inc="ldx", dma=True)
        s.do("sync", lambda e, sc=sc: e.dma_start(
            out=t_x[:].rearrange("p (a t) -> p a t", a=8),
            in_=x2T[:, :, sc * SC:(sc + 1) * SC].rearrange("a p t -> p a t")),
            inc="ldx", dma=True)
        s.wait("tensor", "ldx")
        s.wait("scalar", "ldx")
        s.wait("vector", "ldx")
        for mt in range(8):
            pp = (p0, p1)[mt % 2]
            for kk in range(8):
                s.do("tensor", lambda e, mt=mt, kk=kk, pp=pp: e.matmul(
                    out=pp[:, 0:SC], lhsT=t_wo[:, kk * 1024 + mt * 128:kk * 1024 + mt * 128 + 128],
                    rhs=t_yg[:, kk * SC:(kk + 1) * SC], start=(kk == 0), stop=(kk == 7)),
                    waits=("vv",) if kk == 0 else (), inc="mm" if kk == 7 else None)
            s.do("vector", lambda e, mt=mt, pp=pp: e.scalar_tensor_tensor(
                out=t_x3[:, mt * SC:(mt + 1) * SC], in0=pp[:, 0:SC],
                scalar=t_mods[:, 16 + mt:17 + mt], in1=t_x[:, mt * SC:(mt + 1) * SC],
                op0=ALU.mult, op1=ALU.add), waits=("mm",), inc="vv")
            s.do("gpsimd", lambda e, mt=mt, sc=sc: e.dma_start(
                out=x3_o[mt, :, sc * SC:(sc + 1) * SC], in_=t_x3[:, mt * SC:(mt + 1) * SC]),
                waits=("vv",), inc="gst", dma=True)
        for a in range(8):
            s.do("tensor", lambda e, a=a: e.matmul(
                out=p2[0:1, 0:SC], lhsT=t_ones[:], rhs=t_x3[:, a * SC:(a + 1) * SC],
                start=(a == 0), stop=(a == 7)), waits=("vv",) if a == 0 else (),
                inc="mm" if a == 7 else None)
        for a in range(8):
            buf = t_sq if a % 2 == 0 else t_sq2
            s.do("scalar", lambda e, a=a, buf=buf: e.activation(
                out=buf[:], in_=t_x3[:, a * SC:(a + 1) * SC], func=AF.Square),
                waits=("mm", "vv"), inc="ss")
            s.wait("tensor", "ss")
            s.do("tensor", lambda e, a=a, buf=buf: e.matmul(
                out=p3[0:1, 0:SC], lhsT=t_ones[:], rhs=buf[:],
                start=(a == 0), stop=(a == 7)), inc="mm")
        s.do("scalar", lambda e: e.activation(out=t_row1[:], in_=p2[0:1, 0:SC],
                                              func=AF.Copy, scale=1.0 / 1024), waits=("mm",), inc="ss")
        s.do("scalar", lambda e: e.activation(out=t_row2[:], in_=p3[0:1, 0:SC],
                                              func=AF.Copy, scale=1.0 / 1024), inc="ss")
        s.do("vector", lambda e: e.tensor_copy(out=t_row3[:], in_=t_row1[:]), waits=("ss",), inc="vv")
        s.do("vector", lambda e: e.tensor_tensor(
            out=t_row3[:], in0=t_row3[:], in1=t_row1[:], op=ALU.mult), inc="vv")
        s.do("vector", lambda e: e.tensor_tensor(
            out=t_row2[:], in0=t_row2[:], in1=t_row3[:], op=ALU.subtract), inc="vv")
        s.do("scalar", lambda e: e.activation(out=t_row3[:], in_=t_row2[:], func=AF.Sqrt,
                                              bias=t_eps[0:1, :]), waits=("vv",), inc="ss")
        s.do("vector", lambda e: e.reciprocal(out=t_row2[:], in_=t_row3[:]), waits=("ss",), inc="vv")
        s.wait("tensor", "vv")
        s.do("tensor", lambda e: e.matmul(out=p2[:, 0:SC], lhsT=t_onesr[:],
                                          rhs=t_row1[:], start=True, stop=True), inc="mm")
        s.do("tensor", lambda e: e.matmul(out=p3[:, 0:SC], lhsT=t_onesr[:],
                                          rhs=t_row2[:], start=True, stop=True), inc="mm")
        s.do("scalar", lambda e: e.copy(out=t_mub[:], in_=p2[:, 0:SC]), waits=("mm",), inc="ss")
        s.do("scalar", lambda e: e.copy(out=t_rsb[:], in_=p3[:, 0:SC]), inc="ss")
        for a in range(8):
            buf = t_sq if a % 2 == 0 else t_sq2
            s.do("vector", lambda e, a=a, buf=buf: e.tensor_tensor(
                out=buf[:], in0=t_x3[:, a * SC:(a + 1) * SC], in1=t_mub[:],
                op=ALU.subtract), waits=("ss",), inc="vv")
            s.do("vector", lambda e, a=a, buf=buf: e.tensor_tensor(
                out=buf[:], in0=buf[:], in1=t_rsb[:], op=ALU.mult), inc="vv")
            s.do("vector", lambda e, a=a, buf=buf: e.tensor_scalar(
                out=t_h4f[:, a * SC:(a + 1) * SC], in0=buf[:],
                scalar1=t_onepcb[:, a:a + 1], scalar2=t_mods[:, 24 + a:25 + a],
                op0=ALU.mult, op1=ALU.add), inc="vv")
            s.do("vector", lambda e, a=a: e.tensor_copy(
                out=t_h4[:, a * SC:(a + 1) * SC], in_=t_h4f[:, a * SC:(a + 1) * SC]),
                inc="vv")
        for tt in range(SC // 128):
            t0 = sc * SC + tt * 128
            for a in range(8):
                s.do("tensor", lambda e, a=a, tt=tt: e.transpose(
                    out=p2b[:, 0:128], in_=t_h4[:, a * SC + tt * 128:a * SC + (tt + 1) * 128],
                    identity=t_idnb2[:]), waits=("vv", "ss"), inc="mm")
                s.do("scalar", lambda e, a=a: e.copy(
                    out=t_h4t[:, a * 128:(a + 1) * 128], in_=p2b[:, 0:128]),
                    waits=("mm",), inc="ss")
            s.do("gpsimd", lambda e, t0=t0: e.dma_start(
                out=h4_o[t0:t0 + 128, :], in_=t_h4t[:]), waits=("ss",), inc="gst", dma=True)
            for a in range(8):
                s.do("tensor", lambda e, a=a, tt=tt: e.matmul(
                    out=p3[:, 0:8], lhsT=t_h4f[:, a * SC + tt * 128:a * SC + (tt + 1) * 128],
                    rhs=t_rw[:, a * 8:(a + 1) * 8], start=(a == 0), stop=(a == 7)),
                    waits=("ss", "vv") if a == 0 else (), inc="mm" if a == 7 else None)
            s.do("scalar", lambda e: e.copy(out=t_lg[:], in_=p3[:, 0:8]), waits=("mm",), inc="ss")
            if dbg and sc == 0 and tt == 0:
                s.do("gpsimd", lambda e: e.dma_start(out=dbg_lg, in_=t_lg[:]),
                     waits=("ss",), inc="gst", dma=True)
                s.wait("vector", "gst"); s.wait("scalar", "gst")
            s.do("vector", lambda e: e.tensor_reduce(
                out=t_c1[:, 0:1], in_=t_lg[:], axis=AX.X, op=ALU.max), waits=("ss",), inc="vv")
            s.do("vector", lambda e: e.tensor_scalar(
                out=t_lg[:], in0=t_lg[:], scalar1=t_c1[:, 0:1], scalar2=None,
                op0=ALU.subtract), inc="vv")
            s.do("scalar", lambda e: e.activation(out=t_pr[:], in_=t_lg[:], func=AF.Exp),
                 waits=("vv",), inc="ss")
            s.do("vector", lambda e: e.tensor_reduce(
                out=t_c1[:, 0:1], in_=t_pr[:], axis=AX.X, op=ALU.add), waits=("ss",), inc="vv")
            s.do("vector", lambda e: e.reciprocal(out=t_c1[:, 1:2], in_=t_c1[:, 0:1]), inc="vv")
            s.do("vector", lambda e: e.tensor_scalar(
                out=t_pr[:], in0=t_pr[:], scalar1=t_c1[:, 1:2], scalar2=None,
                op0=ALU.mult), inc="vv")
            if dbg and sc == 0 and tt == 0:
                s.do("gpsimd", lambda e: e.dma_start(out=dbg_pr, in_=t_pr[:]),
                     waits=("vv",), inc="gst", dma=True)
                s.wait("vector", "gst")
            s.do("vector", lambda e: e.tensor_reduce(
                out=t_c1[:, 2:3], in_=t_pr[:], axis=AX.X, op=ALU.max), inc="vv")
            s.do("vector", lambda e: e.tensor_scalar(
                out=t_c2[:], in0=t_pr[:], scalar1=t_c1[:, 2:3], scalar2=None,
                op0=ALU.is_equal), inc="vv")
            s.do("vector", lambda e: e.tensor_tensor(
                out=t_c2[:], in0=t_c2[:], in1=t_iotm[:], op=ALU.mult), inc="vv")
            s.do("vector", lambda e: e.tensor_scalar(
                out=t_c2[:], in0=t_c2[:], scalar1=1000.0, scalar2=None, op0=ALU.add), inc="vv")
            s.do("vector", lambda e: e.tensor_reduce(
                out=t_td[:, 0:1], in_=t_c2[:], axis=AX.X, op=ALU.min), inc="vv")
            s.do("vector", lambda e: e.tensor_scalar(
                out=t_c2[:], in0=t_iot[:], scalar1=t_td[:, 0:1], scalar2=None,
                op0=ALU.is_equal), inc="vv")
            s.do("vector", lambda e: e.scalar_tensor_tensor(
                out=t_c2[:], in0=t_c2[:], scalar=-1e30, in1=t_pr[:],
                op0=ALU.mult, op1=ALU.add), inc="vv")
            s.do("vector", lambda e: e.tensor_reduce(
                out=t_c1[:, 3:4], in_=t_c2[:], axis=AX.X, op=ALU.max), inc="vv")
            s.do("vector", lambda e: e.tensor_scalar(
                out=t_c2[:], in0=t_c2[:], scalar1=t_c1[:, 3:4], scalar2=None,
                op0=ALU.is_equal), inc="vv")
            s.do("vector", lambda e: e.tensor_tensor(
                out=t_c2[:], in0=t_c2[:], in1=t_iotm[:], op=ALU.mult), inc="vv")
            s.do("vector", lambda e: e.tensor_scalar(
                out=t_c2[:], in0=t_c2[:], scalar1=1000.0, scalar2=None, op0=ALU.add), inc="vv")
            s.do("vector", lambda e: e.tensor_reduce(
                out=t_td[:, 1:2], in_=t_c2[:], axis=AX.X, op=ALU.min), inc="vv")
            s.do("vector", lambda e: e.tensor_tensor(
                out=t_c1[:, 4:5], in0=t_c1[:, 2:3], in1=t_c1[:, 3:4], op=ALU.add), inc="vv")
            s.do("vector", lambda e: e.reciprocal(out=t_c1[:, 5:6], in_=t_c1[:, 4:5]), inc="vv")
            s.do("vector", lambda e: e.tensor_tensor(
                out=t_td[:, 2:3], in0=t_c1[:, 2:3], in1=t_c1[:, 5:6], op=ALU.mult), inc="vv")
            s.do("vector", lambda e: e.tensor_tensor(
                out=t_td[:, 3:4], in0=t_c1[:, 3:4], in1=t_c1[:, 5:6], op=ALU.mult), inc="vv")
            s.do("vector", lambda e: e.tensor_scalar(
                out=t_ind[:, 0:8], in0=t_iot[:], scalar1=t_td[:, 0:1], scalar2=None,
                op0=ALU.is_equal), inc="vv")
            s.do("vector", lambda e: e.tensor_scalar(
                out=t_ind[:, 8:16], in0=t_iot[:], scalar1=t_td[:, 1:2], scalar2=None,
                op0=ALU.is_equal), inc="vv")
            s.do("vector", lambda e: e.tensor_tensor(
                out=t_ind[:, 16:24], in0=t_ind[:, 0:8], in1=t_ind[:, 8:16], op=ALU.add), inc="vv")
            s.wait("tensor", "vv")
            for r3 in range(3):
                s.do("tensor", lambda e, r3=r3: e.matmul(
                    out=p2[:, 0:8], lhsT=t_tsu[:], rhs=t_ind[:, r3 * 8:(r3 + 1) * 8],
                    start=True, stop=False), waits=("ss",)),
                s.do("tensor", lambda e, r3=r3: e.matmul(
                    out=p2[:, 0:8], lhsT=t_onesr[:], rhs=t_offs[0:1, r3 * 8:(r3 + 1) * 8],
                    start=False, stop=True), inc="mm")
                s.do("scalar", lambda e, r3=r3: e.copy(
                    out=t_rk[:, r3 * 8:(r3 + 1) * 8], in_=p2[:, 0:8]), waits=("mm",), inc="ss")
                s.do("tensor", lambda e, r3=r3: e.matmul(
                    out=p3[0:1, 0:8], lhsT=t_ones[:], rhs=t_ind[:, r3 * 8:(r3 + 1) * 8],
                    start=True, stop=True), waits=("ss",), inc="mm")
                s.do("scalar", lambda e, r3=r3: e.copy(
                    out=t_rowt[0:1, r3 * 8:(r3 + 1) * 8], in_=p3[0:1, 0:8]),
                    waits=("mm",), inc="ss")
                s.do("vector", lambda e, r3=r3: e.tensor_tensor(
                    out=t_offs[0:1, r3 * 8:(r3 + 1) * 8], in0=t_offs[0:1, r3 * 8:(r3 + 1) * 8],
                    in1=t_rowt[0:1, r3 * 8:(r3 + 1) * 8], op=ALU.add), waits=("ss",), inc="vv")
            for col, rsec, isec in ((4, 0, 0), (5, 1, 1), (6, 2, 0), (7, 2, 1)):
                s.do("vector", lambda e, col=col, rsec=rsec, isec=isec: e.tensor_tensor(
                    out=t_c2[:], in0=t_rk[:, rsec * 8:(rsec + 1) * 8],
                    in1=t_ind[:, isec * 8:(isec + 1) * 8], op=ALU.mult), inc="vv")
                s.do("vector", lambda e, col=col: e.tensor_reduce(
                    out=t_td[:, col:col + 1], in_=t_c2[:], axis=AX.X, op=ALU.add), inc="vv")
            s.do("gpsimd", lambda e, t0=t0: e.dma_start(
                out=td_o[t0:t0 + 128, :], in_=t_td[:]), waits=("vv",), inc="gst", dma=True)
            s.do("sync", lambda e, t0=t0: e.dma_start(
                out=t_tid[:], in_=tokid[t0:t0 + 128, :]), waits=("gst",),
                inc="ldt", dma=True)
            s.wait("vector", "ldt")
            for kki in range(2):
                rmcol = 6 + kki
                s.do("vector", lambda e, kki=kki, rmcol=rmcol: e.scalar_tensor_tensor(
                    out=t_idxf[:], in0=t_td[:, kki:kki + 1], scalar=2048.0,
                    in1=t_td[:, rmcol:rmcol + 1], op0=ALU.mult, op1=ALU.add),
                    waits=("gst",), inc="vv")
                s.do("vector", lambda e: e.tensor_scalar(
                    out=t_idxf[:], in0=t_idxf[:], scalar1=0.0, scalar2=16383.0,
                    op0=ALU.max, op1=ALU.min), inc="vv")
                s.do("vector", lambda e: e.tensor_copy(out=t_idx[:], in_=t_idxf[:]),
                     inc="vv")
                s.do("gpsimd", lambda e: e.indirect_dma_start(
                    out=idl_o, out_offset=IndirectOffsetOnAxis(ap=t_idx[:], axis=0),
                    in_=t_tid[:], in_offset=None), waits=("vv", "gst"), inc="gst", dma=True)
    s.do("vector", lambda e: e.tensor_copy(out=t_rk[0:1, 0:24], in_=t_offs[0:1, 0:24]),
         waits=("gst",), inc="vv")
    for r3 in range(3):
        s.do("gpsimd", lambda e, r3=r3: e.dma_start(
            out=cnt_o[r3:r3 + 1, :], in_=t_rk[0:1, r3 * 8:(r3 + 1) * 8]),
            waits=("vv",), inc="gst", dma=True)
    s.do("gpsimd", lambda e: e.dma_start(out=cnt_o[3:4, :], in_=t_rk[0:1, 0:8]),
         inc="gst", dma=True)
    s.wait("gpsimd", "gst")
    s.finalize()
    ctx.close()
    return nc


def prep_H(d, yg1s, x2s, mods1s):
    cst = host_consts()
    i = np.arange(LCH)
    NTOK = T // 2
    in_maps = []
    for c in range(NCORE):
        b, half = c // 2, c % 2
        tsl = slice(half * NTOK, (half + 1) * NTOK)
        yg_full = np.concatenate([yg1s[2 * b], yg1s[2 * b + 1]], axis=0)
        x2_full = np.concatenate([x2s[2 * b], x2s[2 * b + 1]], axis=2)
        m = {
            "yg1": np.ascontiguousarray(yg_full[:, :, tsl]),
            "x2T": np.ascontiguousarray(x2_full[:, :, tsl]),
            "Wo1": np.asarray(d["L1_attn_Wo"]).reshape(8, 128, 1024),
            "mods1": mods1s[2 * b],
            "rwT": np.ascontiguousarray(np.asarray(d["L1_ffn_router_w"]).T).reshape(8, 128, 8),
            "tokid": (np.arange(NTOK, dtype=np.float32) + c * NTOK).reshape(NTOK, 1),
            "iot8": np.tile(np.arange(8, dtype=np.float32), (128, 1)),
            "iotm8": np.tile(np.arange(8, dtype=np.float32) - 1000.0, (128, 1)),
            "ctsu": (i[:, None] < i[None, :]).astype(np.float32),
            "cidn": cst["idn128"], "cidnb_h": cst["idn128"].astype(BF),
            "cones": cst["ones128"], "conesr": cst["onesr"],
        }
        in_maps.append({k: np.ascontiguousarray(v) for k, v in m.items()})
    return in_maps


def build_I():
    nc = bass.Bass("TRN2", target_bir_lowering=False, debug=False)
    dram = lambda n, s, d, k: nc.dram_tensor(n, list(s), d, kind=k).ap()
    NTOK = T // 2
    h4a = dram("h4a", (8 * NTOK, 1024), BF16, "ExternalInput")
    w1e = dram("w1e", (8, 128, 4096), F32, "ExternalInput")
    w2e = dram("w2e", (32, 128, 1024), F32, "ExternalInput")
    idla = dram("idla", (8, 8 * 2048), F32, "ExternalInput")
    eid_in = dram("eid_in", (1, 8), F32, "ExternalInput")
    iotacol = dram("iotacol", (128, 1), F32, "ExternalInput")
    conesr_i = dram("conesr_i", (1, 128), F32, "ExternalInput")
    cidnb_i = dram("cidnb_i", (128, 128), BF16, "ExternalInput")
    y_o = dram("y_o", (8 * CAPO, 1024), BF16, "ExternalOutput")

    ctx = ExitStack()
    sb, ps = _mk(nc, ctx)
    s = Sched(nc, ctx)
    t_w1 = sb("t_w1", (128, 8 * 4096), BF16)
    t_w2 = sb("t_w2", (128, 32 * 1024), BF16)
    t_rows = sb("t_rows", (128, 1024), BF16)
    t_rows2 = sb("t_rows2", (128, 1024), BF16)
    t_rT = sb("t_rT", (128, 1024), BF16)
    t_idnb_i = sb("t_idnb_i", (128, 128), BF16)
    t_h1 = sb("t_h1", (128, 4096), BF16)
    t_y = sb("t_y", (128, 1024), BF16)
    t_idxf = sb("t_idxf", (128, 1), F32)
    t_idx = sb("t_idx", (128, 1), U32)
    t_tid = sb("t_tid", (128, 1), F32)
    t_idxf2 = sb("t_idxf2", (128, 1), F32)
    t_idx2 = sb("t_idx2", (128, 1), U32)
    t_tid2 = sb("t_tid2", (128, 1), F32)
    t_eid = sb("t_eid", (1, 8), F32)
    t_iotacol = sb("t_iotacol", (128, 1), F32)
    t_eidc = sb("t_eidc", (128, 8), F32)
    t_onesr_i = sb("t_onesr_i", (1, 128), F32)
    p0 = ps("p0", (128, 512), F32)
    p1 = ps("p1", (128, 512), F32)
    p2b = ps("p2b", (128, 128), BF16)

    s.do("sync", lambda e: e.dma_start(out=t_eid[:], in_=eid_in), inc="ldr", dma=True)
    s.do("sync", lambda e: e.dma_start(out=t_iotacol[:], in_=iotacol), inc="ldr", dma=True)
    s.do("sync", lambda e: e.dma_start(out=t_onesr_i[:], in_=conesr_i), inc="ldr", dma=True)
    s.do("sync", lambda e: e.dma_start(out=t_idnb_i[:], in_=cidnb_i), inc="ldr", dma=True)
    s.do("gpsimd", lambda e: e.dma_start(
        out=t_w1[:].rearrange("p (k m) -> p k m", k=8), in_=w1e.rearrange("k p m -> p k m")),
        inc="gld", dma=True)
    s.do("gpsimd", lambda e: e.dma_start(
        out=t_w2[:].rearrange("p (k m) -> p k m", k=32), in_=w2e.rearrange("k p m -> p k m")),
        inc="gld", dma=True)
    s.do("vector", lambda e: e.memset(t_idxf[:], 0.0), inc="vv")
    s.do("vector", lambda e: e.memset(t_rows[:], 0.0), inc="vv")
    s.do("vector", lambda e: e.memset(t_rows2[:], 0.0), inc="vv")
    s.wait("tensor", "gld", "ldr")
    s.do("tensor", lambda e: e.matmul(out=p0[:, 0:8], lhsT=t_onesr_i[:], rhs=t_eid[:],
                                      start=True, stop=True), inc="mm")
    s.do("scalar", lambda e: e.copy(out=t_eidc[:], in_=p0[:, 0:8]), waits=("mm",), inc="ss")
    s.wait("vector", "ss")
    NB = 8 * CAPO // 128
    bufs = [(t_rows, t_idxf, t_idx, t_tid, "gidxA", "growA"),
            (t_rows2, t_idxf2, t_idx2, t_tid2, "gidxB", "growB")]

    def emit_gather(bt):
        rows, idxf, idx, tid, sg, sr = bufs[bt % 2]
        c_own = (bt * 128) // CAPO
        r0 = (bt * 128) % CAPO
        s.do("vector", lambda e, r0=r0, c_own=c_own, idxf=idxf: e.tensor_scalar(
            out=idxf[:], in0=t_iotacol[:], scalar1=1.0, scalar2=float(r0 + c_own * 16384),
            op0=ALU.mult, op1=ALU.add), waits=("gg", sg, sr), inc="vv")
        s.do("vector", lambda e, idxf=idxf: e.tensor_scalar(
            out=idxf[:], in0=idxf[:], scalar1=t_eidc[:, 0:1], scalar2=None,
            op0=ALU.add), inc="vv")
        s.do("vector", lambda e, idxf=idxf, idx=idx: e.tensor_copy(out=idx[:], in_=idxf[:]),
             inc="vv")
        s.do("gpsimd", lambda e, idx=idx, tid=tid: e.indirect_dma_start(
            out=tid[:], out_offset=None,
            in_=idla.rearrange("c n -> (c n)").rearrange("(n o) -> n o", o=1),
            in_offset=IndirectOffsetOnAxis(ap=idx[:], axis=0)),
            waits=("vv",), inc=sg, dma=True)
        s.do("vector", lambda e, idx=idx, tid=tid: e.tensor_copy(out=idx[:], in_=tid[:]),
             waits=(sg,), inc="vv")
        s.do("gpsimd", lambda e, rows=rows, idx=idx: e.indirect_dma_start(
            out=rows[:], out_offset=None, in_=h4a,
            in_offset=IndirectOffsetOnAxis(ap=idx[:], axis=0)),
            waits=("vv", "mm"), inc=sr, dma=True)

    emit_gather(0)
    for bt in range(NB):
        rows, _, _, _, _, sr = bufs[bt % 2]
        if bt + 1 < NB:
            emit_gather(bt + 1)
        s.wait("tensor", sr, "ss")
        for a in range(8):
            s.do("tensor", lambda e, a=a, rows=rows: e.transpose(
                out=p2b[:, 0:128], in_=rows[:, a * 128:(a + 1) * 128],
                identity=t_idnb_i[:]), waits=("ss",), inc="mm")
            s.do("scalar", lambda e, a=a: e.copy(
                out=t_rT[:, a * 128:(a + 1) * 128], in_=p2b[:, 0:128]),
                waits=("mm",), inc="ss")
        s.wait("tensor", "ss", "vv")
        for mt in range(32):
            pp = (p0, p1)[mt % 2]
            for kk in range(8):
                s.do("tensor", lambda e, mt=mt, kk=kk, pp=pp: e.matmul(
                    out=pp[:, 0:128], lhsT=t_w1[:, kk * 4096 + mt * 128:kk * 4096 + mt * 128 + 128],
                    rhs=t_rT[:, kk * 128:(kk + 1) * 128],
                    start=(kk == 0), stop=(kk == 7)),
                    waits=("ss",) if kk == 0 else (), inc="mm" if kk == 7 else None)
            s.do("scalar", lambda e, mt=mt, pp=pp: e.activation(
                out=t_h1[:, mt * 128:(mt + 1) * 128], in_=pp[:, 0:128], func=AF.Gelu),
                waits=("mm",), inc="ss")
        s.wait("tensor", "ss")
        for half in range(2):
            pp = (p0, p1)[half]
            for kk in range(32):
                s.do("tensor", lambda e, half=half, kk=kk, pp=pp: e.matmul(
                    out=pp[:, 0:512], lhsT=t_h1[:, kk * 128:(kk + 1) * 128],
                    rhs=t_w2[:, kk * 1024 + half * 512:kk * 1024 + half * 512 + 512],
                    start=(kk == 0), stop=(kk == 31)),
                    waits=("vv",) if kk == 0 and half == 0 else (),
                    inc="mm" if kk == 31 else None)
            s.do("vector", lambda e, half=half, pp=pp: e.tensor_copy(
                out=t_y[:, half * 512:(half + 1) * 512], in_=pp[:, 0:512]),
                waits=("mm", "gst") if half == 0 else ("mm",), inc="vv")
        s.do("gpsimd", lambda e, bt=bt: e.dma_start(
            out=y_o[bt * 128:(bt + 1) * 128, :], in_=t_y[:]), waits=("vv",), inc="gst", dma=True)
    s.wait("gpsimd", "gst")
    s.finalize()
    ctx.close()
    return nc


def prep_I(d, h4s, idls, core_e):
    NTOK = T // 2
    h4a = np.concatenate(h4s, axis=0)
    idla = np.stack([x.reshape(8 * 2048) for x in idls])
    iot = np.arange(128, dtype=np.float32).reshape(128, 1)
    in_maps = []
    for e in range(NCORE):
        m = {
            "h4a": h4a,
            "w1e": np.asarray(d["L1_ffn_w1"][e]).reshape(8, 128, 4096),
            "w2e": np.asarray(d["L1_ffn_w2"][e]).reshape(32, 128, 1024),
            "idla": idla,
            "eid_in": np.full((1, 8), float(e * 2048), np.float32),
            "iotacol": iot, "conesr_i": host_consts()["onesr"],
            "cidnb_i": np.eye(128, dtype=np.float32).astype(BF),
        }
        in_maps.append({k: np.ascontiguousarray(v) for k, v in m.items()})
    return in_maps


def build_J():
    nc = bass.Bass("TRN2", target_bir_lowering=False, debug=False)
    dram = lambda n, s, d, k: nc.dram_tensor(n, list(s), d, kind=k).ap()
    NTOK = T // 2
    NTT = NTOK // 128
    ysec = dram("ysec", (8 * CAPO, 1024), BF16, "ExternalInput")
    x3T = dram("x3T", (8, 128, NTOK), F32, "ExternalInput")
    td = dram("td", (NTOK, 8), F32, "ExternalInput")
    cnts = dram("cnts", (32, 8), F32, "ExternalInput")
    mods1 = dram("mods1", (128, 48), F32, "ExternalInput")
    iot8 = dram("iot8", (128, 8), F32, "ExternalInput")
    cidnb = dram("cidnb", (128, 128), BF16, "ExternalInput")
    conesr = dram("conesr", (1, 128), F32, "ExternalInput")
    coreid = dram("coreid", (1, 8), F32, "ExternalInput")
    out_o = dram("out_o", (8, 128, NTOK), F32, "ExternalOutput")

    ctx = ExitStack()
    sb, ps = _mk(nc, ctx)
    s = Sched(nc, ctx)
    t_td = sb("t_td", (128, 8), F32)
    t_y0 = sb("t_y0", (128, 1024), BF16)
    t_y1 = sb("t_y1", (128, 1024), BF16)
    t_x3 = sb("t_x3", (128, 8 * 128), F32)
    t_ctr = sb("t_ctr", (128, 1024), F32)
    t_ctrb = sb("t_ctrb", (128, 1024), BF16)
    t_cnt = sb("t_cnt", (32, 8), F32)
    t_offs = sb("t_offs", (1, 16), F32)
    t_offb = sb("t_offb", (128, 16), F32)
    t_mods = sb("t_mods", (128, 48), F32)
    t_iot = sb("t_iot", (128, 8), F32)
    t_idnb = sb("t_idnb", (128, 128), BF16)
    t_onesr = sb("t_onesr", (1, 128), F32)
    t_cid = sb("t_cid", (1, 8), F32)
    t_idxf = sb("t_idxf", (128, 1), F32)
    t_idx = sb("t_idx", (128, 1), U32)
    t_c1 = sb("t_c1", (128, 8), F32)
    t_c2 = sb("t_c2", (128, 8), F32)
    t_w01 = sb("t_w01", (128, 2), F32)
    p0 = ps("p0", (128, 512), F32)
    p2b = ps("p2b", (128, 1024), BF16)

    def ld(dst, src):
        s.do("sync", lambda e, d=dst, sr=src: e.dma_start(out=d, in_=sr), inc="ldr", dma=True)
    ld(t_cnt[:], cnts); ld(t_mods[:], mods1); ld(t_iot[:], iot8)
    ld(t_idnb[:], cidnb); ld(t_onesr[:], conesr); ld(t_cid[:], coreid)
    rowpat = dram("rowpat", (32, 2), F32, "ExternalInput")
    corein = dram("corein", (32, 1), F32, "ExternalInput")
    t_rp = sb("t_rp", (32, 2), F32)
    t_ci = sb("t_ci", (32, 1), F32)
    t_mk = sb("t_mk", (32, 2), F32)
    ld(t_rp[:], rowpat); ld(t_ci[:], corein)
    s.do("vector", lambda e: e.tensor_scalar(
        out=t_ci[:], in0=t_ci[:], scalar1=t_cid[0:1, 0:1].partition_broadcast(32) if False else 0.0,
        scalar2=None, op0=ALU.add), waits=("ldr",), inc="vv")
    s.wait("tensor", "ldr")
    s.do("tensor", lambda e: e.matmul(out=p0[0:32, 0:1], lhsT=t_onesr[0:1, 0:32],
                                      rhs=t_cid[0:1, 0:1], start=True, stop=True), inc="mm")
    s.do("scalar", lambda e: e.copy(out=t_idxf[0:32, 0:1], in_=p0[0:32, 0:1]),
         waits=("mm",), inc="ss")
    s.do("vector", lambda e: e.tensor_scalar(
        out=t_ci[:], in0=t_ci[:], scalar1=t_idxf[0:32, 0:1], scalar2=None,
        op0=ALU.is_lt), waits=("ss",), inc="vv")
    for kki in range(2):
        s.do("vector", lambda e, kki=kki: e.tensor_tensor(
            out=t_mk[:, kki:kki + 1], in0=t_rp[:, kki:kki + 1], in1=t_ci[:],
            op=ALU.mult), inc="vv")
    s.wait("tensor", "vv")
    for kki in range(2):
        s.do("tensor", lambda e, kki=kki: e.matmul(
            out=p0[0:1, 8 * kki:8 * kki + 8], lhsT=t_mk[:, kki:kki + 1], rhs=t_cnt[:],
            start=True, stop=True), inc="mm")
    s.do("scalar", lambda e: e.copy(out=t_offs[0:1, 0:16], in_=p0[0:1, 0:16]),
         waits=("mm",), inc="ss")
    s.wait("tensor", "ss")
    s.do("tensor", lambda e: e.matmul(out=p0[:, 0:16], lhsT=t_onesr[:], rhs=t_offs[0:1, 0:16],
                                      start=True, stop=True), inc="mm")
    s.do("scalar", lambda e: e.copy(out=t_offb[:], in_=p0[:, 0:16]), waits=("mm",), inc="ss")
    gb_done = False
    for tt in range(NTT):
        t0 = tt * 128
        s.do("sync", lambda e, t0=t0: e.dma_start(out=t_td[:], in_=td[t0:t0 + 128, :]),
             waits=("vv", "gst"), inc="ldx", dma=True)
        s.do("sync", lambda e, tt=tt: e.dma_start(
            out=t_x3[:].rearrange("p (a t) -> p a t", a=8),
            in_=x3T[:, :, tt * 128:(tt + 1) * 128].rearrange("a p t -> p a t")),
            inc="ldx", dma=True)
        s.wait("vector", "ldx", "ss")
        for kki in range(2):
            s.do("vector", lambda e, kki=kki: e.scalar_tensor_tensor(
                out=t_idxf[:], in0=t_td[:, kki:kki + 1], scalar=float(CAPO),
                in1=t_td[:, 6 + kki:7 + kki], op0=ALU.mult, op1=ALU.add),
                waits=("grow",), inc="vv")
            s.do("vector", lambda e: e.tensor_scalar(
                out=t_idxf[:], in0=t_idxf[:], scalar1=0.0, scalar2=float(8 * CAPO - 1),
                op0=ALU.max, op1=ALU.min), inc="vv")
            s.do("vector", lambda e: e.tensor_copy(out=t_idx[:], in_=t_idxf[:]), inc="vv")
            s.do("gpsimd", lambda e, kki=kki: e.indirect_dma_start(
                out=(t_y0 if kki == 0 else t_y1)[:], out_offset=None, in_=ysec,
                in_offset=IndirectOffsetOnAxis(ap=t_idx[:], axis=0)),
                waits=("vv", "gst"), inc="grow", dma=True)
            s.do("vector", lambda e, kki=kki: e.tensor_scalar(
                out=t_c1[:], in0=t_iot[:], scalar1=t_td[:, kki:kki + 1], scalar2=None,
                op0=ALU.is_equal), inc="vv")
            s.do("vector", lambda e, kki=kki: e.tensor_tensor(
                out=t_c1[:], in0=t_c1[:], in1=t_offb[:, 8 * kki:8 * kki + 8],
                op=ALU.mult), inc="vv")
            s.do("vector", lambda e: e.tensor_reduce(
                out=t_c2[:, 0:1], in_=t_c1[:], axis=AX.X, op=ALU.add), inc="vv")
            s.do("vector", lambda e, kki=kki: e.tensor_tensor(
                out=t_c2[:, 1:2], in0=t_td[:, 4 + kki:5 + kki], in1=t_c2[:, 0:1],
                op=ALU.add), inc="vv")
            s.do("vector", lambda e: e.tensor_scalar(
                out=t_c2[:, 2:3], in0=t_c2[:, 1:2], scalar1=float(int(1.25 * 4 * T / 8 * 2)),
                scalar2=None, op0=ALU.is_lt), inc="vv")
            s.do("vector", lambda e, kki=kki: e.tensor_scalar(
                out=t_c2[:, 3:4], in0=t_td[:, 6 + kki:7 + kki], scalar1=float(CAPO),
                scalar2=None, op0=ALU.is_lt), inc="vv")
            s.do("vector", lambda e: e.tensor_tensor(
                out=t_c2[:, 2:3], in0=t_c2[:, 2:3], in1=t_c2[:, 3:4], op=ALU.mult), inc="vv")
            s.do("vector", lambda e, kki=kki: e.tensor_tensor(
                out=t_w01[:, kki:kki + 1], in0=t_td[:, 2 + kki:3 + kki], in1=t_c2[:, 2:3],
                op=ALU.mult), inc="vv")
        s.wait("vector", "grow")
        s.do("vector", lambda e: e.tensor_scalar(
            out=t_ctr[:], in0=t_y0[:], scalar1=t_w01[:, 0:1], scalar2=None,
            op0=ALU.mult), inc="vv")
        s.do("vector", lambda e: e.scalar_tensor_tensor(
            out=t_ctr[:], in0=t_y1[:], scalar=t_w01[:, 1:2], in1=t_ctr[:],
            op0=ALU.mult, op1=ALU.add), inc="vv")
        s.do("gpsimd", lambda e: e.tensor_copy(out=t_ctrb[:], in_=t_ctr[:]),
             waits=("vv",), inc="gg")
        s.wait("tensor", "gg")
        for a in range(8):
            s.do("tensor", lambda e, a=a: e.transpose(
                out=p2b[:, 0:128], in_=t_ctrb[:, a * 128:(a + 1) * 128],
                identity=t_idnb[:]), waits=("vv",) if a == 0 else ("vv",), inc="mm")
            s.do("vector", lambda e, a=a, tt=tt: e.scalar_tensor_tensor(
                out=t_x3[:, a * 128:(a + 1) * 128], in0=p2b[:, 0:128],
                scalar=t_mods[:, 40 + a:41 + a], in1=t_x3[:, a * 128:(a + 1) * 128],
                op0=ALU.mult, op1=ALU.add), waits=("mm",), inc="vv")
            s.do("gpsimd", lambda e, a=a, tt=tt: e.dma_start(
                out=out_o[a, :, tt * 128:(tt + 1) * 128], in_=t_x3[:, a * 128:(a + 1) * 128]),
                waits=("vv",), inc="gst", dma=True)
    s.wait("gpsimd", "gst")
    s.finalize()
    ctx.close()
    return nc


def prep_J(d, ysecs_by_core, x3s, tds, cnts_all, mods1s):
    cst = host_consts()
    NTOK = T // 2
    rowpat = np.zeros((32, 2), np.float32)
    for p in range(32):
        if p % 4 == 0: rowpat[p, 0] = 1.0
        if p % 4 == 1: rowpat[p, 1] = 1.0
    corein = (np.arange(32) // 4).astype(np.float32).reshape(32, 1)
    cnts = np.concatenate([c.reshape(4, 8) for c in cnts_all], axis=0)
    in_maps = []
    for c in range(NCORE):
        b = c // 2
        m = {
            "ysec": ysecs_by_core[c],
            "x3T": x3s[c],
            "td": tds[c],
            "cnts": cnts,
            "mods1": mods1s[2 * b],
            "iot8": np.tile(np.arange(8, dtype=np.float32), (128, 1)),
            "cidnb": np.eye(128, dtype=np.float32).astype(BF),
            "conesr": cst["onesr"],
            "coreid": np.full((1, 8), float(c), np.float32),
            "rowpat": rowpat, "corein": corein,
        }
        in_maps.append({k: np.ascontiguousarray(v) for k, v in m.items()})
    return in_maps


def _run_launch(nc, in_maps):
    res = run_bass_kernel_spmd(nc, in_maps, core_ids=list(range(NCORE)))
    return [res.results[c] for c in range(NCORE)]


def kernel(**inputs):
    d = inputs
    NTOK = T // 2

    outsA = _run_launch(build_A(), prep_A(d))
    ygs = [o["yg_o"] for o in outsA]
    modss = [o["mods_o"] for o in outsA]

    outsB = _run_launch(build_B(), prep_B(d, ygs, modss))
    h2s = [o["h2_o"] for o in outsB]
    x1s = [o["x1_o"] for o in outsB]
    max1s = [o["max1_o"] for o in outsB]
    wqg = np.concatenate([o["wqg_o"].reshape(384, 1024) for o in outsB], axis=0)
    wqu = np.concatenate([o["wqu_o"].reshape(384, 1024) for o in outsB], axis=0)
    wqd = np.concatenate([o["wqd_o"] for o in outsB], axis=0)

    outsC = _run_launch(build_C(), prep_C(d, h2s, max1s, wqg, wqu))
    ups = [o["up_o"] for o in outsC]
    gates = [o["gate_o"] for o in outsC]
    maxus = [o["maxu_o"] for o in outsC]
    maxgs = [o["maxg_o"] for o in outsC]

    outsD = _run_launch(build_D(), prep_D(d, gates, maxgs, max1s))
    Ps = [o["P_o"] for o in outsD]
    maxps = [o["maxp_o"] for o in outsD]

    outsE = _run_launch(build_E(), prep_E(d, ups, Ps, maxus, maxps, max1s, maxgs))
    Os = [o["O_o"] for o in outsE]
    maxos = [o["maxo_o"] for o in outsE]

    outsF = _run_launch(build_F(), prep_F(d, Os, x1s, modss, wqd, maxos, maxus, maxps))
    x2s = [o["x2_o"] for o in outsF]

    outsG = _run_launch(build_G(), prep_G(d, x2s))
    yg1s = [o["yg1_o"] for o in outsG]
    mods1s = [o["mods1_o"] for o in outsG]

    outsH = _run_launch(build_H(), prep_H(d, yg1s, x2s, mods1s))
    x3s = [o["x3_o"] for o in outsH]
    h4s = [o["h4_o"] for o in outsH]
    tds = [o["td_o"] for o in outsH]
    cnts_all = [o["cnt_o"] for o in outsH]
    idls = [o["idl_o"] for o in outsH]

    outsI = _run_launch(build_I(), prep_I(d, h4s, idls, None))
    ysecs = [np.concatenate([outsI[e]["y_o"][c * CAPO:(c + 1) * CAPO]
                             for e in range(8)], axis=0) for c in range(NCORE)]

    outsJ = _run_launch(build_J(), prep_J(d, ysecs, x3s, tds, cnts_all, mods1s))
    out = np.concatenate([outsJ[c]["out_o"].reshape(1024, NTOK).T
                          for c in range(NCORE)], axis=0)
    return np.ascontiguousarray(out.reshape(4, T, 1024).astype(np.float32))


# version 06
# speedup vs baseline: 1.2022x; 1.2022x over previous
import numpy as np
from contextlib import ExitStack
import concourse.bass as bass
import concourse.mybir as mybir
from concourse.bass import IndirectOffsetOnAxis
from concourse.bass_utils import run_bass_kernel_spmd
import ml_dtypes

F32 = mybir.dt.float32
BF16 = mybir.dt.bfloat16
U32 = mybir.dt.uint32
AF = mybir.ActivationFunctionType
ALU = mybir.AluOpType
AX = mybir.AxisListType
BF = ml_dtypes.bfloat16

RMAGIC = 1.5 * 2 ** 23

B, T, C, H, NH = 4, 4096, 1024, 16, 64
NCORE = 8
LCH = 128
SC = 512
EPS_LN = 1e-5
EPS_GN = 1e-5 * 64


class Sched:
    ENGS = ("sync", "tensor", "vector", "scalar", "gpsimd")

    def __init__(self, nc, ctx):
        self.nc = nc
        self.ctx = ctx
        self.q = {e: [] for e in self.ENGS}
        self.sems = {}
        self.cnt = {}

    def _sem(self, name):
        if name not in self.sems:
            self.sems[name] = self.ctx.enter_context(self.nc.semaphore(name))
            self.cnt[name] = 0
        return self.sems[name]

    ENGSEM = {"tensor": "mm", "vector": "vv", "scalar": "ss", "gpsimd": "gg", "sync": None}
    ALIAS = {"mmod": "ss", "gc": "gg"}

    def _canon(self, name):
        return self.ALIAS.get(name, name)

    FINAL_WAIT = ("ldr",)

    def _mkwl(self, names):
        wl = []
        for w in names:
            if self.cnt.get(w, 0) <= 0 and w not in self.FINAL_WAIT:
                continue
            self._sem(w)
            if w in self.FINAL_WAIT:
                wl.append((w, None))
            else:
                wl.append((w, self.cnt[w]))
        return wl

    def do(self, eng, fn, waits=(), inc=None, dma=False):
        names = [self._canon(w) for w in waits]
        own = self.ENGSEM[eng]
        if own is not None and own not in names:
            names.append(own)
        wl = self._mkwl(names)
        amt = 16 if dma else 1
        ih = None
        if inc is not None:
            inc = self._canon(inc) if not dma else inc
            if not dma and self.ENGSEM[eng] is not None:
                inc = self.ENGSEM[eng]
            ih = self._sem(inc)
            self.cnt[inc] += amt

        def run(e, wl=wl, fn=fn, ih=ih, amt=amt, S=self):
            for nm, v in wl:
                vv = S.cnt[nm] if v is None else v
                if vv > 0:
                    e.wait_ge(S.sems[nm], vv)
            inst = fn(e)
            if ih is not None:
                inst.then_inc(ih, amt)
        self.q[eng].append(run)

    def wait(self, eng, *names):
        names = [self._canon(w) for w in names]
        wl = self._mkwl(names)

        def run(e, wl=wl, S=self):
            for nm, v in wl:
                vv = S.cnt[nm] if v is None else v
                if vv > 0:
                    e.wait_ge(S.sems[nm], vv)
        self.q[eng].append(run)

    def finalize(self):
        with self.nc.Block() as block:
            for eng in self.ENGS:
                ops = list(self.q[eng])

                def make(ops):
                    def body(e):
                        for op in ops:
                            op(e)
                    return body
                getattr(block, eng)(make(ops))


def _mk(nc, ctx):
    sb = lambda n, s, d: ctx.enter_context(nc.sbuf_tensor(n, list(s), d))
    ps = lambda n, s, d: ctx.enter_context(nc.psum_tensor(n, list(s), d))
    return sb, ps


def host_consts():
    i = np.arange(LCH)
    return dict(
        tril_incl=(i[:, None] <= i[None, :]).astype(np.float32),
        mask_su=(i[:, None] < i[None, :]).astype(np.float32),
        mask_sui=(i[:, None] <= i[None, :]).astype(np.float32),
        idn128=np.eye(128, dtype=np.float32),
        idn64b=np.eye(64, dtype=np.float32).astype(BF),
        ones128=np.ones((128, 1), dtype=np.float32),
        ones64=np.ones((64, 1), dtype=np.float32),
        onesr=np.ones((1, 128), dtype=np.float32),
        ones11=np.ones((1, 1), dtype=np.float32),
    )


def build_A(dbg=False):
    nc = bass.Bass("TRN2", target_bir_lowering=False, debug=False)
    dram = lambda n, s, d, k: nc.dram_tensor(n, list(s), d, kind=k).ap()
    NSC = T // SC
    NCH = SC // LCH
    xT = dram("xT", (8, 128, T), F32, "ExternalInput")
    wmod = dram("wmod", (1024, 6144), F32, "ExternalInput")
    c_col = dram("c_col", (128, 8), F32, "ExternalInput")
    tm_rows = dram("tm_rows", (1, 5120), F32, "ExternalInput")
    tmx_col = dram("tmx_col", (128, 8), F32, "ExternalInput")
    maa_w1 = dram("maa_w1", (8, 128, 160), F32, "ExternalInput")
    maa_w2 = dram("maa_w2", (5, 32, 1024), F32, "ExternalInput")
    td_w1 = dram("td_w1", (8, 128, 64), F32, "ExternalInput")
    td_w2 = dram("td_w2", (64, 512), F32, "ExternalInput")
    tdec = dram("tdec", (128, 4), F32, "ExternalInput")
    Wr = dram("Wr", (8, 128, 512), F32, "ExternalInput")
    Wk = dram("Wk", (8, 128, 512), F32, "ExternalInput")
    Wv = dram("Wv", (8, 128, 512), F32, "ExternalInput")
    Wg = dram("Wg", (8, 128, 512), F32, "ExternalInput")
    u_col = dram("u_col", (128, 8), F32, "ExternalInput")
    lnw_r = dram("lnw_r", (1, 512), F32, "ExternalInput")
    lnb_r = dram("lnb_r", (1, 512), F32, "ExternalInput")
    ctr = dram("ctrl", (128, 128), F32, "ExternalInput")
    cmask = dram("cmask", (128, 128), F32, "ExternalInput")
    cidn = dram("cidn", (128, 128), F32, "ExternalInput")
    cidnb = dram("cidnb", (128, 64), BF16, "ExternalInput")
    cones = dram("cones", (128, 1), F32, "ExternalInput")
    conesr = dram("conesr", (1, 128), F32, "ExternalInput")
    cones64 = dram("cones64", (64, 1), F32, "ExternalInput")
    cones11 = dram("cones11", (1, 1), F32, "ExternalInput")
    yg_o = dram("yg_o", (4, 128, T), BF16, "ExternalOutput")
    mods_o = dram("mods_o", (128, 48), F32, "ExternalOutput")
    dbg_map = {}
    if dbg:
        dbg_h = dram("dbg_h", (128, 8 * (SC + 1)), BF16, "ExternalOutput")
        dbg_r = dram("dbg_r", (128, 4 * SC), BF16, "ExternalOutput")
        dbg_k = dram("dbg_k", (128, 4 * SC), BF16, "ExternalOutput")
        dbg_v = dram("dbg_v", (128, 4 * SC), BF16, "ExternalOutput")
        dbg_g = dram("dbg_g", (128, 4 * SC), BF16, "ExternalOutput")
        dbg_ew = dram("dbg_ew", (128, 4 * SC), F32, "ExternalOutput")
        dbg_xk = dram("dbg_xk", (128, 5 * 8 * SC), BF16, "ExternalOutput")
        dbg_Y = dram("dbg_Y", (128, 512), F32, "ExternalOutput")
        dbg_qt = dram("dbg_qt", (128, 512), BF16, "ExternalOutput")
        dbg_kt = dram("dbg_kt", (128, 512), BF16, "ExternalOutput")
        dbg_li = dram("dbg_li", (128, 512), F32, "ExternalOutput")
        dbg_ktT = dram("dbg_ktT", (128, 512), BF16, "ExternalOutput")
        dbg_vT = dram("dbg_vT", (128, 512), BF16, "ExternalOutput")
        dbg_S = dram("dbg_S", (128, 512), F32, "ExternalOutput")
        dbg_yn = dram("dbg_yn", (128, 512), F32, "ExternalOutput")
        dbg_gn = dram("dbg_gn", (128, 8), F32, "ExternalOutput")
        dbg_gn2 = dram("dbg_gn2", (128, 8), F32, "ExternalOutput")
        dbg_gnA = dram("dbg_gnA", (128, 8), F32, "ExternalOutput")
        dbg_ysq = dram("dbg_ysq", (128, 512), F32, "ExternalOutput")
        dbg_gnB = dram("dbg_gnB", (128, 8), F32, "ExternalOutput")
        dbg_map.update(dbg_h=dbg_h, dbg_r=dbg_r, dbg_k=dbg_k, dbg_v=dbg_v,
                       dbg_g=dbg_g, dbg_ew=dbg_ew, dbg_xk=dbg_xk, dbg_Y=dbg_Y,
                       dbg_qt=dbg_qt, dbg_kt=dbg_kt, dbg_li=dbg_li,
                       dbg_ktT=dbg_ktT, dbg_vT=dbg_vT, dbg_S=dbg_S,
                       dbg_yn=dbg_yn, dbg_gn=dbg_gn, dbg_gn2=dbg_gn2,
                       dbg_gnA=dbg_gnA, dbg_gnB=dbg_gnB, dbg_ysq=dbg_ysq)

    ctx = ExitStack()
    sb, ps = _mk(nc, ctx)
    s = Sched(nc, ctx)

    t_wr = sb("t_wr", (128, 8 * 512), BF16)
    t_wk = sb("t_wk", (128, 8 * 512), BF16)
    t_wv = sb("t_wv", (128, 8 * 512), BF16)
    t_wg = sb("t_wg", (128, 8 * 512), BF16)
    t_w1 = sb("t_w1", (128, 8 * 160), BF16)
    t_w2 = sb("t_w2", (32, 5 * 1024), BF16)
    t_tdw1 = sb("t_tdw1", (128, 8 * 64), BF16)
    t_tdw2 = sb("t_tdw2", (64, 512), BF16)
    t_ccol = sb("t_ccol", (128, 8), F32)
    t_tm = sb("t_tm", (1, 5120), BF16)
    t_tmxc = sb("t_tmxc", (128, 8), F32)
    t_tdec = sb("t_tdec", (128, 4), F32)
    t_ucol = sb("t_ucol", (128, 8), F32)
    t_mods = sb("t_mods", (128, 48), F32)
    t_onepca = sb("t_onepca", (128, 8), F32)
    t_tril = sb("t_tril", (128, 128), F32)
    t_masku = sb("t_masku", (128, 128), F32)
    t_idn = sb("t_idn", (128, 128), F32)
    t_idnb = sb("t_idnb", (128, 64), BF16)
    t_ones = sb("t_ones", (128, 1), F32)
    t_onesr = sb("t_onesr", (1, 128), F32)
    t_ones64 = sb("t_ones64", (64, 1), F32)
    t_ones11 = sb("t_ones11", (1, 1), F32)
    t_onesrow = sb("t_onesrow", (1, SC), BF16)
    t_epsln = sb("t_epsln", (128, 1), F32)
    t_epsgn = sb("t_epsgn", (128, 1), F32)
    t_lnr = sb("t_lnr", (1, 512), F32)
    t_lbr = sb("t_lbr", (1, 512), F32)
    t_lnwb = sb("t_lnwb", (128, 512), F32)
    t_lnbb = sb("t_lnbb", (128, 512), F32)
    t_x = sb("t_x", (128, 8 * SC), F32)
    t_sq = sb("t_sq", (128, SC), F32)
    t_sq2 = sb("t_sq2", (128, SC), F32)
    t_stats = sb("t_stats", (1, SC), F32)
    t_mub = sb("t_mub", (128, SC), F32)
    t_rsb = sb("t_rsb", (128, SC), F32)
    t_h = sb("t_h", (128, 8 * (SC + 1)), BF16)
    t_z = sb("t_z", (128, 8 * SC), BF16)
    t_t5 = sb("t_t5", (32, 5 * SC), BF16)
    t_t6b = sb("t_t6b", (64, SC), BF16)
    t_xk5 = sb("t_xk5", (128, 5 * 8 * SC), BF16)
    if 5 * 8 * SC >= 20480:
        t_wmodrow = t_xk5[:, 0:12288].bitcast(F32)
    else:
        t_wmodrow = sb("t_wmodalt", (128, 6144), F32)[:]
    t_xxx = t_xk5[:, 0:8 * SC]
    t_r = sb("t_r", (128, 4 * SC), BF16)
    t_k = sb("t_k", (128, 4 * SC), BF16)
    t_v = sb("t_v", (128, 4 * SC), BF16)
    t_gs = sb("t_gs", (128, 4 * SC), BF16)
    t_ew = sb("t_ew", (128, 4 * SC), F32)
    t_ewT = sb("t_ewT", (128, 512), F32)
    t_liT = sb("t_liT", (128, 512), F32)
    t_liC = sb("t_liC", (128, 512), F32)
    t_ysq = t_liC
    t_pq = sb("t_pq", (128, 512), F32)
    t_ek = sb("t_ek", (128, 512), F32)
    t_al = sb("t_al", (128, 4), F32)
    t_qt = sb("t_qt", (128, 512), BF16)
    t_kt = sb("t_kt", (128, 512), BF16)
    t_ktT = sb("t_ktT", (128, 8 * 64), BF16)
    t_vT = sb("t_vT", (128, 8 * 64), BF16)
    t_gm = sb("t_gm", (128, 128), BF16)
    t_ruk = sb("t_ruk", (128, 128), F32)
    t_drow = sb("t_drow", (1, 128), F32)
    t_dcol = sb("t_dcol", (128, 1), F32)
    t_S = sb("t_S", (128, 8 * 64), F32)
    t_Sb = sb("t_Sb", (128, 8 * 64), BF16)
    t_Y = sb("t_Y", (128, 512), F32)
    t_gn = sb("t_gn", (128, 8), F32)
    t_gn2 = sb("t_gn2", (128, 8), F32)
    t_gn3 = sb("t_gn3", (128, 8), F32)
    t_yn = sb("t_yn", (128, 512), F32)
    t_yg = sb("t_yg", (128, 4 * SC), BF16)

    p0 = ps("p0", (128, 512), F32)
    p1 = ps("p1", (128, 512), F32)
    p2 = ps("p2", (128, 512), F32)
    p3 = ps("p3", (128, 512), F32)
    p4 = ps("p4", (128, 512), F32)
    p5 = ps("p5", (128, 512), F32)
    p6b = ps("p6b", (128, 1024), BF16)
    p7 = ps("p7", (128, 512), F32)

    def ld(dst, src):
        s.do("sync", lambda e, d=dst, sr=src: e.dma_start(out=d, in_=sr), inc="ldr", dma=True)

    def gld(dst, src):
        s.do("gpsimd", lambda e, d=dst, sr=src: e.dma_start(out=d, in_=sr), inc="gld", dma=True)

    ld(t_tril[:], ctr); ld(t_masku[:], cmask); ld(t_idn[:], cidn); ld(t_idnb[:], cidnb)
    ld(t_ones[:], cones); ld(t_onesr[:], conesr); ld(t_ones64[:], cones64); ld(t_ones11[:], cones11)
    ld(t_ccol[:], c_col); ld(t_tmxc[:], tmx_col)
    ld(t_tdec[:], tdec)
    ld(t_ucol[:], u_col)
    ld(t_lnr[:], lnw_r); ld(t_lbr[:], lnb_r)
    gld(t_wr[:].rearrange("p (k m) -> p k m", k=8), Wr.rearrange("k p m -> p k m"))
    gld(t_wk[:].rearrange("p (k m) -> p k m", k=8), Wk.rearrange("k p m -> p k m"))
    gld(t_wv[:].rearrange("p (k m) -> p k m", k=8), Wv.rearrange("k p m -> p k m"))
    gld(t_wg[:].rearrange("p (k m) -> p k m", k=8), Wg.rearrange("k p m -> p k m"))
    gld(t_w1[:].rearrange("p (k m) -> p k m", k=8), maa_w1.rearrange("k p m -> p k m"))
    gld(t_w2[:].rearrange("e (f m) -> e f m", f=5), maa_w2.rearrange("f e m -> e f m"))
    gld(t_tdw1[:].rearrange("p (k m) -> p k m", k=8), td_w1.rearrange("k p m -> p k m"))
    gld(t_tdw2[:], td_w2)
    gld(t_tm[:], tm_rows)

    s.do("vector", lambda e: e.memset(t_S[:], 0.0), inc="vv")
    s.do("vector", lambda e: e.memset(t_Sb[:], 0.0), inc="vv")
    s.do("vector", lambda e: e.memset(t_h[:], 0.0), inc="vv")
    s.do("vector", lambda e: e.memset(t_onesrow[:], 1.0), inc="vv")
    s.do("vector", lambda e: e.memset(t_epsln[:], EPS_LN), inc="vv")
    s.do("vector", lambda e: e.memset(t_epsgn[:], EPS_GN), inc="vv")

    s.do("scalar", lambda e: e.activation(out=t_ccol[:], in_=t_ccol[:], func=AF.Silu),
         waits=("ldr",), inc="ss")
    if 5 * 8 * SC >= 20480:
        t_wm = t_xk5[:, 12288:12288 + 8192].bitcast(F32)
        t_modrow = t_xk5[:, 0:12288].bitcast(F32)[0:1, 0:6144]
    else:
        t_wm = sb("t_wmalt", (128, 4096), F32)[:]
        t_modrow = t_wmodrow[0:1, 0:6144]
    for cchunk in range(12):
        s.do("sync", lambda e, cc=cchunk: e.dma_start(
            out=t_wm[:].rearrange("p (kk m) -> p kk m", kk=8),
            in_=wmod[:, cc * 512:(cc + 1) * 512].rearrange("(kk p) m -> p kk m", p=128)),
            waits=("mmod",) if cchunk > 0 else (), inc="ldw", dma=True)
        s.wait("tensor", "ldw", "ss")
        for kk in range(8):
            s.do("tensor", lambda e, kk=kk: e.matmul(
                out=p1[0:1, 0:512], lhsT=t_ccol[:, kk:kk + 1],
                rhs=t_wm[:, kk * 512:(kk + 1) * 512], start=(kk == 0), stop=(kk == 7)),
                inc="mm" if kk == 7 else None)
        s.do("scalar", lambda e, cc=cchunk: e.copy(
            out=t_modrow[0:1, cc * 512:(cc + 1) * 512], in_=p1[0:1, 0:512]),
            waits=("mm",), inc="mmod")
    s.wait("tensor", "mmod")
    for col in range(48):
        s.do("tensor", lambda e, col=col: e.matmul(
            out=p1[:, col:col + 1], lhsT=t_modrow[0:1, col * 128:(col + 1) * 128],
            rhs=t_ones11[:], start=True, stop=True), inc="mm" if col == 47 else None)
    s.do("scalar", lambda e: e.copy(out=t_mods[:], in_=p1[:, 0:48]), waits=("mm",), inc="ss")
    for c0 in (16, 40):
        s.do("scalar", lambda e, c0=c0: e.activation(
            out=t_sq[:, 0:8], in_=t_mods[:, c0:c0 + 8], func=AF.Abs), waits=("ss", "vv"), inc="ss")
        s.do("vector", lambda e, c0=c0: e.tensor_scalar(
            out=t_sq[:, 0:8], in0=t_sq[:, 0:8], scalar1=0.25, scalar2=None,
            op0=ALU.max), waits=("ss",), inc="vv")
        s.do("scalar", lambda e, c0=c0: e.activation(
            out=t_sq[:, 8:16], in_=t_mods[:, c0:c0 + 8], func=AF.Sign), waits=("vv",), inc="ss")
        s.do("vector", lambda e, c0=c0: e.tensor_tensor(
            out=t_mods[:, c0:c0 + 8], in0=t_sq[:, 0:8], in1=t_sq[:, 8:16],
            op=ALU.mult), waits=("ss",), inc="vv")
    s.do("gpsimd", lambda e: e.dma_start(out=mods_o, in_=t_mods[:]),
         waits=("vv",), inc="gst", dma=True)
    s.do("vector", lambda e: e.tensor_scalar(
        out=t_onepca[:], in0=t_mods[:, 8:16], scalar1=1.0, scalar2=None, op0=ALU.add),
        waits=("vv",), inc="vv")
    s.wait("tensor", "ldr", "ss")
    s.do("tensor", lambda e: e.matmul(out=p2[:], lhsT=t_onesr[:], rhs=t_lnr[:],
                                      start=True, stop=True), inc="mm")
    s.do("scalar", lambda e: e.copy(out=t_lnwb[:], in_=p2[:]), waits=("mm",), inc="ss")
    s.do("tensor", lambda e: e.matmul(out=p3[:], lhsT=t_onesr[:], rhs=t_lbr[:],
                                      start=True, stop=True), inc="mm")
    s.do("scalar", lambda e: e.copy(out=t_lnbb[:], in_=p3[:]), waits=("mm",), inc="ss")

    s.wait("tensor", "gld", "vv", "ss")

    for sc in range(NSC):
        s.do("sync", lambda e, sc=sc: e.dma_start(
            out=t_x[:].rearrange("p (a t) -> p a t", a=8), in_=xT[:, :, sc * SC:(sc + 1) * SC].rearrange("a p t -> p a t")),
            waits=("vv", "ss", "mm", "gg", "gst"), inc="ldx", dma=True)
        s.wait("tensor", "ldx")
        s.wait("scalar", "ldx")
        for a in range(8):
            s.do("tensor", lambda e, a=a: e.matmul(
                out=p0[0:1, 0:SC], lhsT=t_ones[:], rhs=t_x[:, a * SC:(a + 1) * SC],
                start=(a == 0), stop=(a == 7)), inc="mm" if a == 7 else None)
        for a in range(8):
            buf = t_sq if a % 2 == 0 else t_sq2
            s.do("scalar", lambda e, a=a, buf=buf: e.activation(
                out=buf[:], in_=t_x[:, a * SC:(a + 1) * SC], func=AF.Square),
                waits=("mm",), inc="ss")
            s.wait("tensor", "ss")
            s.do("tensor", lambda e, a=a, buf=buf: e.matmul(
                out=p1[0:1, 0:SC], lhsT=t_ones[:], rhs=buf[:],
                start=(a == 0), stop=(a == 7)), inc="mm")
        s.do("scalar", lambda e: e.activation(out=t_stats[0:1, 0:SC], in_=p0[0:1, 0:SC],
                                              func=AF.Copy, scale=1.0 / 1024),
             waits=("mm",), inc="ss")
        s.do("scalar", lambda e: e.activation(out=t_lnr[0:1, 0:SC], in_=p1[0:1, 0:SC],
                                              func=AF.Copy, scale=1.0 / 1024), inc="ss")
        s.do("vector", lambda e: e.tensor_tensor(
            out=t_lbr[0:1, 0:SC], in0=t_stats[0:1, 0:SC], in1=t_stats[0:1, 0:SC],
            op=ALU.mult), waits=("ss",), inc="vv")
        s.do("vector", lambda e: e.tensor_tensor(
            out=t_lnr[0:1, 0:SC], in0=t_lnr[0:1, 0:SC],
            in1=t_lbr[0:1, 0:SC], op=ALU.subtract), inc="vv")
        s.do("scalar", lambda e: e.activation(out=t_lbr[0:1, 0:SC],
                                              in_=t_lnr[0:1, 0:SC], func=AF.Sqrt,
                                              bias=t_epsln[0:1, :]), waits=("vv",), inc="ss")
        s.do("vector", lambda e: e.reciprocal(out=t_lnr[0:1, 0:SC],
                                              in_=t_lbr[0:1, 0:SC]),
             waits=("ss",), inc="vv")
        s.wait("tensor", "vv")
        s.do("tensor", lambda e: e.matmul(out=p2[:, 0:SC], lhsT=t_onesr[:],
                                          rhs=t_stats[0:1, 0:SC], start=True, stop=True), inc="mm")
        s.do("tensor", lambda e: e.matmul(out=p3[:, 0:SC], lhsT=t_onesr[:],
                                          rhs=t_lnr[0:1, 0:SC], start=True, stop=True), inc="mm")
        s.do("scalar", lambda e: e.copy(out=t_mub[:], in_=p2[:, 0:SC]), waits=("mm",), inc="ss")
        s.do("scalar", lambda e: e.copy(out=t_rsb[:], in_=p3[:, 0:SC]), inc="ss")
        for a in range(8):
            s.do("vector", lambda e, a=a: e.tensor_copy(
                out=t_h[:, a * (SC + 1):a * (SC + 1) + 1],
                in_=t_h[:, a * (SC + 1) + SC:a * (SC + 1) + SC + 1]),
                waits=("ss",) if a == 0 else (), inc="vv")
        for a in range(8):
            buf = t_sq if a % 2 == 0 else t_sq2
            s.do("vector", lambda e, a=a, buf=buf: e.tensor_tensor(
                out=buf[:], in0=t_x[:, a * SC:(a + 1) * SC], in1=t_mub[:],
                op=ALU.subtract), waits=("ss",), inc="vv")
            s.do("vector", lambda e, a=a, buf=buf: e.tensor_tensor(
                out=buf[:], in0=buf[:], in1=t_rsb[:], op=ALU.mult), inc="vv")
            s.do("vector", lambda e, a=a, buf=buf: e.tensor_scalar(
                out=t_h[:, a * (SC + 1) + 1:a * (SC + 1) + 1 + SC], in0=buf[:],
                scalar1=t_onepca[:, a:a + 1], scalar2=t_mods[:, a:a + 1],
                op0=ALU.mult, op1=ALU.add), inc="vv")
        for a in range(8):
            hc = t_h[:, a * (SC + 1) + 1:a * (SC + 1) + 1 + SC]
            hs = t_h[:, a * (SC + 1):a * (SC + 1) + SC]
            s.do("vector", lambda e, hc=hc, hs=hs, a=a: e.tensor_tensor(
                out=t_z[:, a * SC:(a + 1) * SC], in0=hs, in1=hc, op=ALU.subtract),
                waits=("ss",), inc="vv")
            s.do("vector", lambda e, hc=hc, a=a: e.scalar_tensor_tensor(
                out=t_xxx[:, a * SC:(a + 1) * SC], in0=t_z[:, a * SC:(a + 1) * SC],
                scalar=t_tmxc[:, a:a + 1], in1=hc, op0=ALU.mult, op1=ALU.add), inc="vv")
        s.wait("tensor", "vv")
        for (m0, m1, pp) in ((0, 96, p4), (96, 160, p5)):
            for kk in range(8):
                s.do("tensor", lambda e, kk=kk, m0=m0, m1=m1, pp=pp: e.matmul(
                    out=pp[0:m1 - m0, 0:SC], lhsT=t_w1[:, kk * 160 + m0:kk * 160 + m1],
                    rhs=t_xxx[:, kk * SC:(kk + 1) * SC],
                    start=(kk == 0), stop=(kk == 7)), inc="mm" if kk == 7 else None)
        for f in range(5):
            pp, r0 = (p4, f * 32) if f < 3 else (p5, (f - 3) * 32)
            s.do("scalar", lambda e, f=f, pp=pp, r0=r0: e.activation(
                out=t_t5[:, f * SC:(f + 1) * SC], in_=pp[r0:r0 + 32, 0:SC],
                func=AF.Tanh), waits=("mm",), inc="ss")
        s.wait("tensor", "ss")
        for f in range(5):
            for a in range(8):
                s.do("tensor", lambda e, f=f, a=a: e.matmul(
                    out=p2[:, 0:SC], lhsT=t_tm[0:1, (f * 8 + a) * 128:(f * 8 + a + 1) * 128],
                    rhs=t_onesrow[:], start=True, stop=False),
                    waits=("vv", "gg")),
                s.do("tensor", lambda e, f=f, a=a: e.matmul(
                    out=p2[:, 0:SC], lhsT=t_w2[:, f * 1024 + a * 128:f * 1024 + (a + 1) * 128],
                    rhs=t_t5[:, f * SC:(f + 1) * SC], start=False, stop=True), inc="mm")
                buf = t_sq if (f * 8 + a) % 2 == 0 else t_sq2
                s.do("vector", lambda e, f=f, a=a, buf=buf: e.scalar_tensor_tensor(
                    out=buf[:], in0=t_z[:, a * SC:(a + 1) * SC], scalar=0.0,
                    in1=p2[:, 0:SC], op0=ALU.add, op1=ALU.mult), waits=("mm", "gg"), inc="vv")
                eng = "gpsimd" if (a % 2 == 1) else "vector"
                hc = t_h[:, a * (SC + 1) + 1:a * (SC + 1) + 1 + SC]
                s.do(eng, lambda e, f=f, a=a, hc=hc, buf=buf: e.tensor_tensor(
                    out=t_xk5[:, (f * 8 + a) * SC:(f * 8 + a + 1) * SC],
                    in0=buf[:], in1=hc, op=ALU.add),
                    waits=("vv",), inc="gg" if eng == "gpsimd" else "vv")
        proj = [(t_wr, 3, t_r, None), (t_wk, 1, t_k, None),
                (t_wv, 2, t_v, None), (t_wg, 4, t_gs, AF.Silu)]
        s.wait("tensor", "vv", "gg")
        for wt, slot, dst, act in proj:
            for mt in range(4):
                pp = (p4, p5)[mt % 2]
                for kk in range(8):
                    s.do("tensor", lambda e, wt=wt, slot=slot, mt=mt, kk=kk, pp=pp: e.matmul(
                        out=pp[:, 0:SC],
                        lhsT=wt[:, kk * 512 + mt * 128:kk * 512 + mt * 128 + 128],
                        rhs=t_xk5[:, (slot * 8 + kk) * SC:(slot * 8 + kk + 1) * SC],
                        start=(kk == 0), stop=(kk == 7)),
                        waits=("ss",) if kk == 0 else (), inc="mm" if kk == 7 else None)
                s.do("scalar", lambda e, dst=dst, mt=mt, act=act, pp=pp: e.activation(
                    out=dst[:, mt * SC:(mt + 1) * SC], in_=pp[:, 0:SC],
                    func=act if act else AF.Copy), waits=("mm",), inc="ss")
        for kk in range(8):
            s.do("tensor", lambda e, kk=kk: e.matmul(
                out=p2[0:64, 0:SC], lhsT=t_tdw1[:, kk * 64:(kk + 1) * 64],
                rhs=t_xk5[:, kk * SC:(kk + 1) * SC], start=(kk == 0), stop=(kk == 7)),
                waits=("vv",) if kk == 0 else (), inc="mm" if kk == 7 else None)
        s.do("scalar", lambda e: e.activation(out=t_t6b[:], in_=p2[0:64, 0:SC],
                                              func=AF.Tanh), waits=("mm",), inc="ss")
        s.wait("tensor", "ss")
        for mt in range(4):
            s.do("tensor", lambda e, mt=mt: e.matmul(
                out=p3[:, 0:SC], lhsT=t_tdw2[:, mt * 128:(mt + 1) * 128],
                rhs=t_t6b[:], start=True, stop=True), waits=("ss",), inc="mm")
            s.do("scalar", lambda e, mt=mt: e.activation(
                out=t_ew[:, mt * SC:(mt + 1) * SC], in_=p3[:, 0:SC], func=AF.Exp,
                bias=t_tdec[:, mt:mt + 1]), waits=("mm",), inc="ss")
        if dbg and sc == 0:
            for nm, tt in (("dbg_h", t_h), ("dbg_r", t_r), ("dbg_k", t_k), ("dbg_v", t_v),
                           ("dbg_g", t_gs), ("dbg_ew", t_ew), ("dbg_xk", t_xk5)):
                s.do("gpsimd", lambda e, nm=nm, tt=tt: e.dma_start(
                    out=dbg_map[nm], in_=tt[:]),
                    waits=("ss", "vv", "gg"), inc="gst", dma=True)
        for ch in range(NCH):
            tof = ch * LCH
            for a in range(4):
                s.do("tensor", lambda e, a=a, tof=tof: e.transpose(
                    out=p2[:, 0:128], in_=t_ew[:, a * SC + tof:a * SC + tof + LCH],
                    identity=t_idn[:]), waits=("ss", "vv"), inc="mm")
                s.do("scalar", lambda e, a=a: e.copy(
                    out=t_ewT[:, a * 128:(a + 1) * 128], in_=p2[:, 0:128]),
                    waits=("mm",), inc="ss")
            s.do("tensor", lambda e: e.matmul(out=p3[:, 0:512], lhsT=t_tril[:],
                                              rhs=t_ewT[:], start=True, stop=True),
                 waits=("ss",), inc="mm")
            s.do("scalar", lambda e: e.copy(out=t_liT[:], in_=p3[:, 0:512]),
                 waits=("mm",), inc="ss")
            for a in range(4):
                s.do("tensor", lambda e, a=a: e.transpose(
                    out=p2[:, 0:128], in_=t_liT[:, a * 128:(a + 1) * 128],
                    identity=t_idn[:]), waits=("ss",), inc="mm")
                s.do("scalar", lambda e, a=a: e.copy(
                    out=t_liC[:, a * 128:(a + 1) * 128], in_=p2[:, 0:128]),
                    waits=("mm",), inc="ss")
            for a in range(4):
                s.do("vector", lambda e, a=a, tof=tof: e.tensor_tensor(
                    out=t_pq[:, a * 128:(a + 1) * 128], in0=t_liC[:, a * 128:(a + 1) * 128],
                    in1=t_ew[:, a * SC + tof:a * SC + tof + LCH], op=ALU.subtract),
                    waits=("ss",), inc="vv")
                s.do("scalar", lambda e, a=a: e.activation(
                    out=t_pq[:, a * 128:(a + 1) * 128], in_=t_pq[:, a * 128:(a + 1) * 128],
                    func=AF.Exp, scale=-1.0), waits=("vv",), inc="ss")
                s.do("scalar", lambda e, a=a: e.activation(
                    out=t_ek[:, a * 128:(a + 1) * 128], in_=t_liC[:, a * 128:(a + 1) * 128],
                    func=AF.Exp), inc="ss")
                s.do("scalar", lambda e, a=a: e.activation(
                    out=t_al[:, a:a + 1], in_=t_liC[:, a * 128 + 127:a * 128 + 128],
                    func=AF.Exp, scale=-1.0), inc="ss")
            for a in range(4):
                s.do("vector", lambda e, a=a, tof=tof: e.tensor_tensor(
                    out=t_qt[:, a * 128:(a + 1) * 128],
                    in0=t_r[:, a * SC + tof:a * SC + tof + LCH],
                    in1=t_pq[:, a * 128:(a + 1) * 128], op=ALU.mult),
                    waits=("ss",), inc="vv")
                s.do("vector", lambda e, a=a, tof=tof: e.tensor_tensor(
                    out=t_kt[:, a * 128:(a + 1) * 128],
                    in0=t_k[:, a * SC + tof:a * SC + tof + LCH],
                    in1=t_ek[:, a * 128:(a + 1) * 128], op=ALU.mult), inc="vv")
            for h in range(8):
                a, base = h // 2, (h % 2) * 64
                s.do("tensor", lambda e, a=a, base=base: e.transpose(
                    out=p6b[:, 0:64], in_=t_kt[base:base + 64, a * 128:(a + 1) * 128],
                    identity=t_idnb[base:base + 64, :]), waits=("vv",), inc="mm")
                s.do("vector", lambda e, h=h: e.tensor_copy(
                    out=t_ktT[:, h * 64:(h + 1) * 64], in_=p6b[:, 0:64]),
                    waits=("mm",), inc="vv")
                s.do("tensor", lambda e, a=a, base=base, tof=tof: e.transpose(
                    out=p6b[:, 64:128], in_=t_v[base:base + 64, a * SC + tof:a * SC + tof + LCH],
                    identity=t_idnb[base:base + 64, :]), waits=("vv",), inc="mm")
                s.do("vector", lambda e, h=h: e.tensor_copy(
                    out=t_vT[:, h * 64:(h + 1) * 64], in_=p6b[:, 64:128]),
                    waits=("mm",), inc="vv")
            for h in range(8):
                a, base = h // 2, (h % 2) * 64
                qh = t_qt[base:base + 64, a * 128:(a + 1) * 128]
                kh = t_kt[base:base + 64, a * 128:(a + 1) * 128]
                rh = t_r[base:base + 64, a * SC + tof:a * SC + tof + LCH]
                kfh = t_k[base:base + 64, a * SC + tof:a * SC + tof + LCH]
                s.do("tensor", lambda e, kh=kh, qh=qh: e.matmul(
                    out=p2[:, 0:128], lhsT=kh, rhs=qh, start=True, stop=True),
                    waits=("vv",), inc="mm")
                s.do("vector", lambda e: e.tensor_tensor(
                    out=t_gm[:], in0=p2[:, 0:128], in1=t_masku[:], op=ALU.mult),
                    waits=("mm",), inc="vv")
                s.do("vector", lambda e, rh=rh, kfh=kfh, base=base: e.tensor_tensor(
                    out=t_ruk[base:base + 64, :], in0=rh, in1=kfh, op=ALU.mult), inc="vv")
                s.do("vector", lambda e, h=h, base=base: e.tensor_scalar(
                    out=t_ruk[base:base + 64, :], in0=t_ruk[base:base + 64, :],
                    scalar1=t_ucol[base:base + 64, h:h + 1], scalar2=None,
                    op0=ALU.mult), inc="vv")
                s.do("tensor", lambda e, base=base: e.matmul(
                    out=p3[0:1, 0:128], lhsT=t_ones[base:base + 64, :],
                    rhs=t_ruk[base:base + 64, :],
                    start=True, stop=True), waits=("vv",), inc="mm")
                s.do("scalar", lambda e: e.copy(out=t_drow[:], in_=p3[0:1, 0:128]),
                     waits=("mm",), inc="ss")
                s.do("tensor", lambda e: e.matmul(
                    out=p3[:, 128:129], lhsT=t_drow[:], rhs=t_ones11[:],
                    start=True, stop=True), waits=("ss",), inc="mm")
                s.do("scalar", lambda e: e.copy(out=t_dcol[:], in_=p3[:, 128:129]),
                     waits=("mm",), inc="ss")
                s.do("tensor", lambda e, h=h: e.matmul(
                    out=p5[:, 0:64], lhsT=t_gm[:], rhs=t_vT[:, h * 64:(h + 1) * 64],
                    start=True, stop=False), waits=("vv", "ss")),
                s.do("tensor", lambda e, qh=qh, h=h, base=base: e.matmul(
                    out=p5[:, 0:64], lhsT=qh, rhs=t_Sb[base:base + 64, h * 64:(h + 1) * 64],
                    start=False, stop=True), inc="mm")
                s.do("tensor", lambda e, h=h, base=base: e.matmul(
                    out=p7[base:base + 64, 0:64], lhsT=t_ktT[:, h * 64:(h + 1) * 64],
                    rhs=t_vT[:, h * 64:(h + 1) * 64], start=True, stop=True,
                    tile_position=(0, base)), inc="mm")
                s.do("vector", lambda e, h=h: e.scalar_tensor_tensor(
                    out=t_Y[:, h * 64:(h + 1) * 64], in0=t_vT[:, h * 64:(h + 1) * 64],
                    scalar=t_dcol[:], in1=p5[:, 0:64], op0=ALU.mult, op1=ALU.add),
                    waits=("mm", "ss"), inc="vv")
                alcol = t_al[base:base + 64, a:a + 1]
                sh_ = (slice(base, base + 64), slice(h * 64, (h + 1) * 64))
                s.do("vector", lambda e, sh_=sh_, base=base: e.tensor_tensor(
                    out=t_S[sh_[0], sh_[1]], in0=t_S[sh_[0], sh_[1]],
                    in1=p7[base:base + 64, 0:64], op=ALU.add), inc="vv")
                s.do("vector", lambda e, sh_=sh_, alcol=alcol: e.tensor_scalar(
                    out=t_S[sh_[0], sh_[1]], in0=t_S[sh_[0], sh_[1]],
                    scalar1=alcol, scalar2=None, op0=ALU.mult), inc="vv")
                s.do("vector", lambda e, sh_=sh_: e.tensor_copy(
                    out=t_Sb[sh_[0], sh_[1]], in_=t_S[sh_[0], sh_[1]]),
                    inc="vv")
            if dbg and sc == 0 and ch == 0:
                s.do("gpsimd", lambda e: e.dma_start(out=dbg_map["dbg_Y"], in_=t_Y[:]),
                     waits=("vv",), inc="gst", dma=True)
                for nm, tt in (("dbg_qt", t_qt), ("dbg_kt", t_kt), ("dbg_li", t_liC),
                               ("dbg_ktT", t_ktT), ("dbg_vT", t_vT), ("dbg_S", t_S)):
                    s.do("gpsimd", lambda e, nm=nm, tt=tt: e.dma_start(
                        out=dbg_map[nm], in_=tt[:]), waits=("vv", "ss"), inc="gst", dma=True)
                s.wait("vector", "gst")
                s.wait("tensor", "gst")
                s.wait("scalar", "gst")
            s.do("vector", lambda e: e.tensor_copy(out=t_ysq[:], in_=t_Y[:]),
                 waits=("vv",), inc="vv")
            s.do("vector", lambda e: e.tensor_tensor(
                out=t_ysq[:], in0=t_ysq[:], in1=t_Y[:], op=ALU.mult), inc="vv")
            s.do("vector", lambda e: e.tensor_reduce(
                out=t_gn[:], in_=t_Y[:].rearrange("p (h n) -> p h n", n=64),
                axis=AX.X, op=ALU.add), inc="vv")
            s.do("vector", lambda e: e.tensor_reduce(
                out=t_gn2[:], in_=t_ysq[:].rearrange("p (h n) -> p h n", n=64),
                axis=AX.X, op=ALU.add), inc="vv")
            s.do("vector", lambda e: e.tensor_scalar(
                out=t_gn[:], in0=t_gn[:], scalar1=1.0 / 64, scalar2=None, op0=ALU.mult), inc="vv")
            s.do("vector", lambda e: e.tensor_copy(out=t_gn3[:], in_=t_gn[:]), inc="vv")
            s.do("vector", lambda e: e.tensor_tensor(
                out=t_gn3[:], in0=t_gn3[:], in1=t_gn[:], op=ALU.mult), inc="vv")
            s.do("vector", lambda e: e.tensor_scalar(
                out=t_gn2[:], in0=t_gn2[:], scalar1=1.0 / 64, scalar2=None, op0=ALU.mult), inc="vv")
            s.do("vector", lambda e: e.tensor_tensor(
                out=t_gn2[:], in0=t_gn2[:], in1=t_gn3[:], op=ALU.subtract), inc="vv")
            if dbg and sc == 0 and ch == 0:
                s.do("gpsimd", lambda e: e.dma_start(out=dbg_map["dbg_gnB"], in_=t_gn2[:]),
                     waits=("vv",), inc="gst", dma=True)
                s.wait("vector", "gst")
                s.wait("scalar", "gst")
            s.do("scalar", lambda e: e.activation(
                out=t_gn2[:], in_=t_gn2[:], func=AF.Sqrt, bias=t_epsgn[:]), waits=("vv",), inc="ss")
            s.do("vector", lambda e: e.reciprocal(out=t_gn2[:], in_=t_gn2[:]),
                 waits=("ss",), inc="vv")
            s.do("vector", lambda e: e.tensor_tensor(
                out=t_yn[:].rearrange("p (h n) -> p h n", n=64),
                in0=t_Y[:].rearrange("p (h n) -> p h n", n=64),
                in1=t_gn[:].rearrange("p (h o) -> p h o", o=1).broadcast_to((128, 8, 64)),
                op=ALU.subtract), inc="vv")
            s.do("vector", lambda e: e.tensor_tensor(
                out=t_yn[:].rearrange("p (h n) -> p h n", n=64),
                in0=t_yn[:].rearrange("p (h n) -> p h n", n=64),
                in1=t_gn2[:].rearrange("p (h o) -> p h o", o=1).broadcast_to((128, 8, 64)),
                op=ALU.mult), inc="vv")
            s.do("vector", lambda e: e.tensor_tensor(
                out=t_yn[:], in0=t_yn[:], in1=t_lnwb[:], op=ALU.mult), inc="vv")
            s.do("vector", lambda e: e.tensor_tensor(
                out=t_yn[:], in0=t_yn[:], in1=t_lnbb[:], op=ALU.add), inc="vv")
            if dbg and sc == 0 and ch == 0:
                for nm, tt in (("dbg_yn", t_yn), ("dbg_gn", t_gn), ("dbg_gn2", t_gn2)):
                    s.do("gpsimd", lambda e, nm=nm, tt=tt: e.dma_start(
                        out=dbg_map[nm], in_=tt[:]), waits=("vv", "ss"), inc="gst", dma=True)
                s.wait("vector", "gst")
            for a in range(4):
                s.do("tensor", lambda e, a=a: e.transpose(
                    out=p2[:, 0:128], in_=t_yn[:, a * 128:(a + 1) * 128],
                    identity=t_idn[:]), waits=("vv",), inc="mm")
                s.do("vector", lambda e, a=a, tof=tof: e.scalar_tensor_tensor(
                    out=t_yg[:, a * SC + tof:a * SC + tof + LCH],
                    in0=t_gs[:, a * SC + tof:a * SC + tof + LCH], scalar=0.0,
                    in1=p2[:, 0:128], op0=ALU.add, op1=ALU.mult),
                    waits=("mm",), inc="vv")
        for a in range(4):
            s.do("gpsimd", lambda e, sc=sc, a=a: e.dma_start(
                out=yg_o[a, :, sc * SC:(sc + 1) * SC],
                in_=t_yg[:, a * SC:(a + 1) * SC]), waits=("vv",), inc="gst", dma=True)

    s.wait("gpsimd", "gst")
    s.finalize()
    ctx.close()
    return nc


def prep_A(d):
    cst = host_consts()
    tm_names = ["time_maa_x", "time_maa_w", "time_maa_k", "time_maa_v", "time_maa_r", "time_maa_g"]
    tm = np.stack([np.asarray(d["L0_attn_" + n]).reshape(1024) for n in tm_names])
    in_maps = []
    for c in range(NCORE):
        b, hg = c // 2, c % 2
        SL = slice(hg * 512, hg * 512 + 512)
        x = np.asarray(d["L0_attn_x"][b])
        m = {
            "xT": np.ascontiguousarray(x.T).reshape(8, 128, x.shape[0]),
            "wmod": d["L0_wmod"],
            "c_col": np.ascontiguousarray(np.asarray(d["L0_c"][b]).reshape(8, 128).T),
            "tm_rows": tm[1:].reshape(1, 5120),
            "tmx_col": np.ascontiguousarray(tm[0].reshape(8, 128).T),
            "maa_w1": np.asarray(d["L0_attn_time_maa_w1"]).reshape(8, 128, 160),
            "maa_w2": d["L0_attn_time_maa_w2"],
            "td_w1": np.asarray(d["L0_attn_time_decay_w1"]).reshape(8, 128, 64),
            "td_w2": np.ascontiguousarray(np.asarray(d["L0_attn_time_decay_w2"])[:, SL]),
            "tdec": np.ascontiguousarray(np.asarray(d["L0_attn_time_decay"]).reshape(1024)[SL].reshape(4, 128).T),
            "Wr": np.ascontiguousarray(np.asarray(d["L0_attn_W_r"])[:, SL]).reshape(8, 128, 512),
            "Wk": np.ascontiguousarray(np.asarray(d["L0_attn_W_k"])[:, SL]).reshape(8, 128, 512),
            "Wv": np.ascontiguousarray(np.asarray(d["L0_attn_W_v"])[:, SL]).reshape(8, 128, 512),
            "Wg": np.ascontiguousarray(np.asarray(d["L0_attn_W_g"])[:, SL]).reshape(8, 128, 512),
            "u_col": np.ascontiguousarray(np.vstack([np.asarray(d["L0_attn_time_faaaa"])[hg * 8:hg * 8 + 8].reshape(8, 64).T] * 2)),
            "lnw_r": np.asarray(d["L0_attn_ln_w"])[SL].reshape(1, 512),
            "lnb_r": np.asarray(d["L0_attn_ln_b"])[SL].reshape(1, 512),
            "ctrl": cst["tril_incl"], "cmask": cst["mask_su"],
            "cidn": cst["idn128"], "cidnb": np.vstack([cst["idn64b"], cst["idn64b"]]),
            "cones": cst["ones128"], "conesr": cst["onesr"],
            "cones64": cst["ones64"], "cones11": cst["ones11"],
        }
        in_maps.append({k: np.ascontiguousarray(v) for k, v in m.items()})
    return in_maps



def build_B():
    nc = bass.Bass("TRN2", target_bir_lowering=False, debug=False)
    dram = lambda n, s, d, k: nc.dram_tensor(n, list(s), d, kind=k).ap()
    NTOK = T // 2
    NSC = NTOK // SC
    yg = dram("yg", (8, 128, NTOK), BF16, "ExternalInput")
    xT = dram("xT", (8, 128, NTOK), F32, "ExternalInput")
    Wo = dram("Wo", (8, 128, 1024), F32, "ExternalInput")
    mods = dram("mods", (128, 48), F32, "ExternalInput")
    wg_sl = dram("wg_sl", (3, 128, 1024), F32, "ExternalInput")
    wu_sl = dram("wu_sl", (3, 128, 1024), F32, "ExternalInput")
    wd_sl = dram("wd_sl", (128, 3072), F32, "ExternalInput")
    cones = dram("cones", (128, 1), F32, "ExternalInput")
    conesr = dram("conesr", (1, 128), F32, "ExternalInput")
    cidn = dram("cidn", (128, 128), F32, "ExternalInput")
    x1_o = dram("x1_o", (8, 128, NTOK), F32, "ExternalOutput")
    h2_o = dram("h2_o", (8, 128, NTOK), F32, "ExternalOutput")
    max1_o = dram("max1_o", (1, 1), F32, "ExternalOutput")
    wqg_o = dram("wqg_o", (3, 128, 1024), BF16, "ExternalOutput")
    wqu_o = dram("wqu_o", (3, 128, 1024), BF16, "ExternalOutput")
    wqd_o = dram("wqd_o", (128, 3072), BF16, "ExternalOutput")

    ctx = ExitStack()
    sb, ps = _mk(nc, ctx)
    s = Sched(nc, ctx)
    t_wo = sb("t_wo", (128, 8 * 1024), BF16)
    t_yg = sb("t_yg", (128, 8 * SC), BF16)
    t_x = sb("t_x", (128, 8 * SC), F32)
    t_x1 = sb("t_x1", (128, 8 * SC), F32)
    t_h2 = sb("t_h2", (128, SC), F32)
    t_mods = sb("t_mods", (128, 48), F32)
    t_onepcb = sb("t_onepcb", (128, 8), F32)
    t_w3 = sb("t_w3", (128, 3 * 1024), F32)
    t_wd = sb("t_wd", (128, 3072), F32)
    t_wq = sb("t_wq", (128, 3 * 1024), BF16)
    t_wqd = sb("t_wqd", (128, 3072), BF16)
    t_bs = sb("t_bs", (128, 224), F32)
    t_sq = sb("t_sq", (128, SC), F32)
    t_sq2 = sb("t_sq2", (128, SC), F32)
    t_row1 = sb("t_row1", (1, SC), F32)
    t_row2 = sb("t_row2", (1, SC), F32)
    t_row3 = sb("t_row3", (1, SC), F32)
    t_mub = sb("t_mub", (128, SC), F32)
    t_rsb = sb("t_rsb", (128, SC), F32)
    t_macc = sb("t_macc", (128, 1), F32)
    t_mrow = sb("t_mrow", (1, 128), F32)
    t_eps = sb("t_eps", (128, 1), F32)
    t_ones = sb("t_ones", (128, 1), F32)
    t_onesr = sb("t_onesr", (1, 128), F32)
    t_idn = sb("t_idn", (128, 128), F32)
    p0 = ps("p0", (128, 512), F32)
    p1 = ps("p1", (128, 512), F32)
    p2 = ps("p2", (128, 512), F32)
    p3 = ps("p3", (128, 512), F32)

    def ld(dst, src):
        s.do("sync", lambda e, d=dst, sr=src: e.dma_start(out=d, in_=sr), inc="ldr", dma=True)

    def gld(dst, src):
        s.do("gpsimd", lambda e, d=dst, sr=src: e.dma_start(out=d, in_=sr), inc="gld", dma=True)

    ld(t_mods[:], mods); ld(t_ones[:], cones); ld(t_onesr[:], conesr); ld(t_idn[:], cidn)
    gld(t_wo[:].rearrange("p (k m) -> p k m", k=8), Wo.rearrange("k p m -> p k m"))
    s.do("vector", lambda e: e.memset(t_eps[:], EPS_LN), inc="vv")
    s.do("vector", lambda e: e.memset(t_macc[:], 0.0), inc="vv")
    s.do("vector", lambda e: e.tensor_scalar(
        out=t_onepcb[:], in0=t_mods[:, 32:40], scalar1=1.0, scalar2=None, op0=ALU.add),
        waits=("ldr",), inc="vv")

    t_sq_wd = sb("t_sq_wd", (128, 3072), F32)
    def dequant(src_ap, nt, wtile, wout, bf_out, blocks_per_row):
        if nt > 1:
            s.do("sync", lambda e, w=wtile, sr=src_ap, nt=nt: e.dma_start(
                out=w[:, 0:nt * 1024].rearrange("p (a m) -> p a m", a=nt),
                in_=sr.rearrange("a p m -> p a m")), waits=("gst", "vv"), inc="ldq", dma=True)
        else:
            s.do("sync", lambda e, w=wtile, sr=src_ap: e.dma_start(
                out=w[:], in_=sr), waits=("gst", "vv"), inc="ldq", dma=True)
        for a in range(nt):
            F = 1024 if nt > 1 else 3072
            NG = F // 32
            w_ap = wtile[:, a * F:(a + 1) * F]
            for g in range(NG):
                s.do("vector", lambda e, w_ap=w_ap, g=g: e.tensor_reduce(
                    out=t_bs[:, g:g + 1], in_=w_ap[:, g * 32:(g + 1) * 32],
                    axis=AX.X, op=ALU.max, apply_absolute_value=True),
                    waits=("ldq", "gst") if g == 0 else (), inc="vv")
            s.do("vector", lambda e, NG=NG: e.tensor_scalar(
                out=t_bs[:, 0:NG], in0=t_bs[:, 0:NG], scalar1=1.0 / 7, scalar2=1e-12,
                op0=ALU.mult, op1=ALU.max), inc="vv")
            s.do("vector", lambda e, NG=NG: e.reciprocal(
                out=t_bs[:, 128:128 + NG], in_=t_bs[:, 0:NG]), inc="vv")
            s.do("vector", lambda e, w_ap=w_ap, NG=NG, F=F: e.tensor_tensor(
                out=t_sq_wd[:, 0:F].rearrange("p (g n) -> p g n", n=32),
                in0=w_ap.rearrange("p (g n) -> p g n", n=32),
                in1=t_bs[:, 128:128 + NG].rearrange("p (g o) -> p g o", o=1).broadcast_to((128, NG, 32)),
                op=ALU.mult), inc="vv")
            tgt = t_sq_wd[:, 0:F]
            s.do("vector", lambda e, tgt=tgt: e.tensor_scalar(
                out=tgt, in0=tgt, scalar1=RMAGIC, scalar2=RMAGIC,
                op0=ALU.add, op1=ALU.subtract), inc="vv")
            s.do("vector", lambda e, tgt=tgt: e.tensor_scalar(
                out=tgt, in0=tgt, scalar1=-8.0, scalar2=7.0,
                op0=ALU.max, op1=ALU.min), inc="vv")
            s.do("vector", lambda e, tgt=tgt, bf_out=bf_out, a=a, NG=NG, F=F: e.tensor_tensor(
                out=bf_out[:, a * F:(a + 1) * F].rearrange("p (g n) -> p g n", n=32),
                in0=tgt.rearrange("p (g n) -> p g n", n=32),
                in1=t_bs[:, 0:NG].rearrange("p (g o) -> p g o", o=1).broadcast_to((128, NG, 32)),
                op=ALU.mult), inc="vv")
        for a in range(nt):
            s.do("gpsimd", lambda e, wout=wout, bf_out=bf_out, a=a, nt=nt: e.dma_start(
                out=wout[a] if nt > 1 else wout,
                in_=bf_out[:, a * 1024:(a + 1) * 1024] if nt > 1 else bf_out[:]),
                waits=("vv",), inc="gst", dma=True)

    dequant(wg_sl, 3, t_w3, wqg_o, t_wq, 32)
    s.wait("sync", "gst")
    dequant(wu_sl, 3, t_w3, wqu_o, t_wq, 32)
    dequant(wd_sl, 1, t_wd, wqd_o, t_wqd, 96)

    s.wait("tensor", "gld", "ldr", "vv")
    for sc in range(NSC):
        s.do("sync", lambda e, sc=sc: e.dma_start(
            out=t_yg[:].rearrange("p (a t) -> p a t", a=8),
            in_=yg[:, :, sc * SC:(sc + 1) * SC].rearrange("a p t -> p a t")),
            waits=("vv", "mm", "gst"), inc="ldx", dma=True)
        s.do("sync", lambda e, sc=sc: e.dma_start(
            out=t_x[:].rearrange("p (a t) -> p a t", a=8),
            in_=xT[:, :, sc * SC:(sc + 1) * SC].rearrange("a p t -> p a t")),
            inc="ldx", dma=True)
        s.wait("tensor", "ldx")
        s.wait("scalar", "ldx")
        s.wait("vector", "ldx")
        for mt in range(8):
            pp = (p0, p1)[mt % 2]
            for kk in range(8):
                s.do("tensor", lambda e, mt=mt, kk=kk, pp=pp: e.matmul(
                    out=pp[:, 0:SC], lhsT=t_wo[:, kk * 1024 + mt * 128:kk * 1024 + mt * 128 + 128],
                    rhs=t_yg[:, kk * SC:(kk + 1) * SC], start=(kk == 0), stop=(kk == 7)),
                    waits=("vv",) if kk == 0 else (), inc="mm" if kk == 7 else None)
            s.do("vector", lambda e, mt=mt, pp=pp: e.scalar_tensor_tensor(
                out=t_x1[:, mt * SC:(mt + 1) * SC], in0=pp[:, 0:SC],
                scalar=t_mods[:, 16 + mt:17 + mt], in1=t_x[:, mt * SC:(mt + 1) * SC],
                op0=ALU.mult, op1=ALU.add), waits=("mm",), inc="vv")
            s.do("gpsimd", lambda e, mt=mt, sc=sc: e.dma_start(
                out=x1_o[mt, :, sc * SC:(sc + 1) * SC], in_=t_x1[:, mt * SC:(mt + 1) * SC]),
                waits=("vv",), inc="gst", dma=True)
        for a in range(8):
            s.do("tensor", lambda e, a=a: e.matmul(
                out=p2[0:1, 0:SC], lhsT=t_ones[:], rhs=t_x1[:, a * SC:(a + 1) * SC],
                start=(a == 0), stop=(a == 7)), waits=("vv",) if a == 0 else (),
                inc="mm" if a == 7 else None)
        for a in range(8):
            buf = t_sq if a % 2 == 0 else t_sq2
            s.do("scalar", lambda e, a=a, buf=buf: e.activation(
                out=buf[:], in_=t_x1[:, a * SC:(a + 1) * SC], func=AF.Square),
                waits=("mm", "vv"), inc="ss")
            s.wait("tensor", "ss")
            s.do("tensor", lambda e, a=a, buf=buf: e.matmul(
                out=p3[0:1, 0:SC], lhsT=t_ones[:], rhs=buf[:],
                start=(a == 0), stop=(a == 7)), inc="mm")
        s.do("scalar", lambda e: e.activation(out=t_row1[:], in_=p2[0:1, 0:SC],
                                              func=AF.Copy, scale=1.0 / 1024), waits=("mm",), inc="ss")
        s.do("scalar", lambda e: e.activation(out=t_row2[:], in_=p3[0:1, 0:SC],
                                              func=AF.Copy, scale=1.0 / 1024), inc="ss")
        s.do("vector", lambda e: e.tensor_copy(out=t_row3[:], in_=t_row1[:]), waits=("ss",), inc="vv")
        s.do("vector", lambda e: e.tensor_tensor(
            out=t_row3[:], in0=t_row3[:], in1=t_row1[:], op=ALU.mult), inc="vv")
        s.do("vector", lambda e: e.tensor_tensor(
            out=t_row2[:], in0=t_row2[:], in1=t_row3[:], op=ALU.subtract), inc="vv")
        s.do("scalar", lambda e: e.activation(out=t_row3[:], in_=t_row2[:], func=AF.Sqrt,
                                              bias=t_eps[0:1, :]), waits=("vv",), inc="ss")
        s.do("vector", lambda e: e.reciprocal(out=t_row2[:], in_=t_row3[:]), waits=("ss",), inc="vv")
        s.wait("tensor", "vv")
        s.do("tensor", lambda e: e.matmul(out=p2[:, 0:SC], lhsT=t_onesr[:],
                                          rhs=t_row1[:], start=True, stop=True), inc="mm")
        s.do("tensor", lambda e: e.matmul(out=p3[:, 0:SC], lhsT=t_onesr[:],
                                          rhs=t_row2[:], start=True, stop=True), inc="mm")
        s.do("scalar", lambda e: e.copy(out=t_mub[:], in_=p2[:, 0:SC]), waits=("mm",), inc="ss")
        s.do("scalar", lambda e: e.copy(out=t_rsb[:], in_=p3[:, 0:SC]), inc="ss")
        for a in range(8):
            buf = t_sq if a % 2 == 0 else t_sq2
            s.do("vector", lambda e, a=a, buf=buf: e.tensor_tensor(
                out=buf[:], in0=t_x1[:, a * SC:(a + 1) * SC], in1=t_mub[:],
                op=ALU.subtract), waits=("ss",), inc="vv")
            s.do("vector", lambda e, a=a, buf=buf: e.tensor_tensor(
                out=buf[:], in0=buf[:], in1=t_rsb[:], op=ALU.mult), inc="vv")
            s.do("vector", lambda e, a=a, buf=buf: e.tensor_scalar(
                out=t_h2[:], in0=buf[:],
                scalar1=t_onepcb[:, a:a + 1], scalar2=t_mods[:, 24 + a:25 + a],
                op0=ALU.mult, op1=ALU.add), inc="vv")
            s.do("gpsimd", lambda e, a=a, sc=sc: e.dma_start(
                out=h2_o[a, :, sc * SC:(sc + 1) * SC], in_=t_h2[:]),
                waits=("vv",), inc="gst", dma=True)
            s.do("vector", lambda e: e.tensor_reduce(
                out=t_sq[:, 0:1], in_=t_h2[:], axis=AX.X, op=ALU.max,
                apply_absolute_value=True), waits=("gst",), inc="vv")
            s.do("vector", lambda e: e.tensor_tensor(
                out=t_macc[:], in0=t_macc[:], in1=t_sq[:, 0:1], op=ALU.max), inc="vv")
    s.wait("tensor", "vv")
    s.do("tensor", lambda e: e.transpose(out=p2[0:1, 0:128], in_=t_macc[:],
                                         identity=t_idn[:]), inc="mm")
    s.do("scalar", lambda e: e.copy(out=t_mrow[:], in_=p2[0:1, 0:128]), waits=("mm",), inc="ss")
    s.do("vector", lambda e: e.tensor_reduce(
        out=t_macc[0:1, 0:1], in_=t_mrow[:], axis=AX.X, op=ALU.max), waits=("ss",), inc="vv")
    s.do("gpsimd", lambda e: e.dma_start(out=max1_o, in_=t_macc[0:1, 0:1]),
         waits=("vv",), inc="gst", dma=True)
    s.wait("gpsimd", "gst")
    s.finalize()
    ctx.close()
    return nc


def prep_B(d, ygs, modss):
    cst = host_consts()
    in_maps = []
    x = np.asarray(d["L0_attn_x"])
    NTOK = x.shape[1] // 2
    for c in range(NCORE):
        b, half = c // 2, c % 2
        tsl = slice(half * NTOK, (half + 1) * NTOK)
        yg_full = np.concatenate([ygs[2 * b], ygs[2 * b + 1]], axis=0)
        m = {
            "yg": np.ascontiguousarray(yg_full[:, :, tsl]),
            "xT": np.ascontiguousarray(x[b].T).reshape(8, 128, x.shape[1])[:, :, tsl],
            "Wo": np.asarray(d["L0_attn_W_o"]).reshape(8, 128, 1024),
            "mods": modss[2 * b],
            "wg_sl": np.asarray(d["L0_ffn_w_gate"])[c * 384:(c + 1) * 384].reshape(3, 128, 1024),
            "wu_sl": np.asarray(d["L0_ffn_w_up"])[c * 384:(c + 1) * 384].reshape(3, 128, 1024),
            "wd_sl": np.asarray(d["L0_ffn_w_down"])[c * 128:(c + 1) * 128],
            "cones": cst["ones128"], "conesr": cst["onesr"], "cidn": cst["idn128"],
        }
        in_maps.append({k: np.ascontiguousarray(v) for k, v in m.items()})
    return in_maps



def build_C():
    nc = bass.Bass("TRN2", target_bir_lowering=False, debug=False)
    dram = lambda n, s, d, k: nc.dram_tensor(n, list(s), d, kind=k).ap()
    NTOK = T // 2
    NSC = NTOK // SC
    h2 = dram("h2", (8, 128, NTOK), F32, "ExternalInput")
    maxes = dram("maxes", (1, 8), F32, "ExternalInput")
    wquT = dram("wquT", (8, 128, 3072), BF16, "ExternalInput")
    wqgT = dram("wqgT", (8, 128, 3072), BF16, "ExternalInput")
    conesr = dram("conesr", (1, 128), F32, "ExternalInput")
    cidn = dram("cidn", (128, 128), F32, "ExternalInput")
    up_o = dram("up_o", (24, 128, NTOK), BF16, "ExternalOutput")
    gate_o = dram("gate_o", (24, 128, NTOK), BF16, "ExternalOutput")
    maxu_o = dram("maxu_o", (1, 1), F32, "ExternalOutput")
    maxg_o = dram("maxg_o", (1, 1), F32, "ExternalOutput")

    ctx = ExitStack()
    sb, ps = _mk(nc, ctx)
    s = Sched(nc, ctx)
    t_wu = sb("t_wu", (128, 8 * 3072), BF16)
    t_wg = sb("t_wg", (128, 8 * 3072), BF16)
    t_h2 = sb("t_h2", (128, 8 * SC), F32)
    t_xi = sb("t_xi", (128, 8 * SC), BF16)
    t_up = sb("t_up", (128, SC), BF16)
    t_mx = sb("t_mx", (1, 8), F32)
    t_s = sb("t_s", (128, 8), F32)
    t_mu = sb("t_mu", (128, 1), F32)
    t_mg = sb("t_mg", (128, 1), F32)
    t_mrow = sb("t_mrow", (1, 128), F32)
    t_onesr = sb("t_onesr", (1, 128), F32)
    t_idn = sb("t_idn", (128, 128), F32)
    t_sq = sb("t_sq", (128, SC), F32)
    p0 = ps("p0", (128, 512), F32)
    p1 = ps("p1", (128, 512), F32)
    p2 = ps("p2", (128, 512), F32)

    def ld(dst, src):
        s.do("sync", lambda e, d=dst, sr=src: e.dma_start(out=d, in_=sr), inc="ldr", dma=True)
    ld(t_mx[:], maxes); ld(t_onesr[:], conesr); ld(t_idn[:], cidn)
    s.do("gpsimd", lambda e: e.dma_start(
        out=t_wu[:].rearrange("p (k m) -> p k m", k=8), in_=wquT.rearrange("k p m -> p k m")),
        inc="gld", dma=True)
    s.do("gpsimd", lambda e: e.dma_start(
        out=t_wg[:].rearrange("p (k m) -> p k m", k=8), in_=wqgT.rearrange("k p m -> p k m")),
        inc="gld", dma=True)
    s.do("vector", lambda e: e.memset(t_mu[:], 0.0), inc="vv")
    s.do("vector", lambda e: e.memset(t_mg[:], 0.0), inc="vv")
    s.do("vector", lambda e: e.tensor_reduce(
        out=t_mx[0:1, 0:1], in_=t_mx[:], axis=AX.X, op=ALU.max), waits=("ldr",), inc="vv")
    s.do("vector", lambda e: e.tensor_scalar(
        out=t_mx[0:1, 1:2], in0=t_mx[0:1, 0:1], scalar1=1.0 / 32767, scalar2=1e-12,
        op0=ALU.mult, op1=ALU.max), inc="vv")
    s.do("vector", lambda e: e.reciprocal(out=t_mx[0:1, 2:3], in_=t_mx[0:1, 1:2]), inc="vv")
    s.wait("tensor", "vv", "ldr")
    s.do("tensor", lambda e: e.matmul(out=p0[:, 0:1], lhsT=t_onesr[:], rhs=t_mx[0:1, 2:3],
                                      start=True, stop=True), inc="mm")
    s.do("scalar", lambda e: e.copy(out=t_s[:, 0:1], in_=p0[:, 0:1]), waits=("mm",), inc="ss")
    s.wait("tensor", "gld", "ss")
    for sc in range(NSC):
        s.do("sync", lambda e, sc=sc: e.dma_start(
            out=t_h2[:].rearrange("p (a t) -> p a t", a=8),
            in_=h2[:, :, sc * SC:(sc + 1) * SC].rearrange("a p t -> p a t")),
            waits=("vv", "mm", "gst", "ss"), inc="ldx", dma=True)
        s.wait("vector", "ldx", "ss")
        for a in range(8):
            s.do("vector", lambda e, a=a: e.tensor_scalar(
                out=t_sq[:], in0=t_h2[:, a * SC:(a + 1) * SC], scalar1=t_s[:, 0:1],
                scalar2=RMAGIC, op0=ALU.mult, op1=ALU.add), inc="vv")
            s.do("vector", lambda e, a=a: e.tensor_scalar(
                out=t_xi[:, a * SC:(a + 1) * SC], in0=t_sq[:], scalar1=RMAGIC, scalar2=None,
                op0=ALU.subtract), inc="vv")
        s.wait("tensor", "vv")
        for wt, oo, macc in ((t_wu, up_o, t_mu), (t_wg, gate_o, t_mg)):
            for mt in range(24):
                pp = (p0, p1)[mt % 2]
                for kk in range(8):
                    s.do("tensor", lambda e, wt=wt, mt=mt, kk=kk, pp=pp: e.matmul(
                        out=pp[:, 0:SC],
                        lhsT=wt[:, kk * 3072 + mt * 128:kk * 3072 + mt * 128 + 128],
                        rhs=t_xi[:, kk * SC:(kk + 1) * SC], start=(kk == 0), stop=(kk == 7)),
                        waits=("ss", "vv") if kk == 0 else (), inc="mm" if kk == 7 else None)
                s.do("scalar", lambda e, pp=pp: e.copy(out=t_up[:], in_=pp[:, 0:SC]),
                     waits=("mm", "gst", "vv"), inc="ss")
                s.do("vector", lambda e: e.tensor_reduce(
                    out=t_sq[:, 0:1], in_=t_up[:], axis=AX.X, op=ALU.max,
                    apply_absolute_value=True), waits=("ss",), inc="vv")
                s.do("vector", lambda e, macc=macc: e.tensor_tensor(
                    out=macc[:], in0=macc[:], in1=t_sq[:, 0:1], op=ALU.max), inc="vv")
                s.do("gpsimd", lambda e, oo=oo, mt=mt, sc=sc: e.dma_start(
                    out=oo[mt, :, sc * SC:(sc + 1) * SC], in_=t_up[:]),
                    waits=("ss",), inc="gst", dma=True)
    for macc, mo in ((t_mu, maxu_o), (t_mg, maxg_o)):
        s.wait("tensor", "vv", "gst")
        s.do("tensor", lambda e, macc=macc: e.transpose(
            out=p2[0:1, 0:128], in_=macc[:], identity=t_idn[:]), inc="mm")
        s.do("scalar", lambda e: e.copy(out=t_mrow[:], in_=p2[0:1, 0:128]), waits=("mm",), inc="ss")
        s.do("vector", lambda e, macc=macc: e.tensor_reduce(
            out=macc[0:1, 0:1], in_=t_mrow[:], axis=AX.X, op=ALU.max), waits=("ss",), inc="vv")
        s.do("vector", lambda e, macc=macc: e.tensor_scalar(
            out=macc[0:1, 0:1], in0=macc[0:1, 0:1], scalar1=t_mx[0:1, 1:2], scalar2=None,
            op0=ALU.mult), inc="vv")
        s.do("gpsimd", lambda e, macc=macc, mo=mo: e.dma_start(out=mo, in_=macc[0:1, 0:1]),
             waits=("vv",), inc="gst", dma=True)
    s.wait("gpsimd", "gst")
    s.finalize()
    ctx.close()
    return nc


def prep_C(d, h2s, max1s, wqg, wqu):
    cst = host_consts()
    maxrow = np.concatenate([m.reshape(1) for m in max1s]).reshape(1, 8)
    wquT = np.ascontiguousarray(wqu.T).reshape(8, 128, 3072)
    wqgT = np.ascontiguousarray(wqg.T).reshape(8, 128, 3072)
    in_maps = []
    for c in range(NCORE):
        m = {
            "h2": h2s[c], "maxes": maxrow, "wquT": wquT, "wqgT": wqgT,
            "conesr": cst["onesr"], "cidn": cst["idn128"],
        }
        in_maps.append({k: np.ascontiguousarray(v) for k, v in m.items()})
    return in_maps


def build_D():
    nc = bass.Bass("TRN2", target_bir_lowering=False, debug=False)
    dram = lambda n, s, d, k: nc.dram_tensor(n, list(s), d, kind=k).ap()
    NTOK = T // 2
    gate = dram("gate", (24, 128, NTOK), BF16, "ExternalInput")
    scal = dram("scal", (1, 24), F32, "ExternalInput")
    conesr = dram("conesr", (1, 128), F32, "ExternalInput")
    cidn = dram("cidn", (128, 128), F32, "ExternalInput")
    P_o = dram("P_o", (24, 128, NTOK), BF16, "ExternalOutput")
    maxp_o = dram("maxp_o", (1, 1), F32, "ExternalOutput")

    ctx = ExitStack()
    sb, ps = _mk(nc, ctx)
    s = Sched(nc, ctx)
    t_g = sb("t_g", (128, NTOK), BF16)
    t_gi = sb("t_gi", (128, NTOK), F32)
    t_sig = sb("t_sig", (128, NTOK), F32)
    t_P = sb("t_P", (128, NTOK), F32)
    t_Pb = sb("t_Pb", (128, NTOK), BF16)
    t_mx = sb("t_mx", (1, 32), F32)
    t_s = sb("t_s", (128, 4), F32)
    t_mp = sb("t_mp", (128, 1), F32)
    t_mrow = sb("t_mrow", (1, 128), F32)
    t_onesr = sb("t_onesr", (1, 128), F32)
    t_idn = sb("t_idn", (128, 128), F32)
    p0 = ps("p0", (128, 512), F32)

    def ld(dst, src):
        s.do("sync", lambda e, d=dst, sr=src: e.dma_start(out=d, in_=sr), inc="ldr", dma=True)
    ld(t_mx[0:1, 0:24], scal); ld(t_onesr[:], conesr); ld(t_idn[:], cidn)
    s.do("vector", lambda e: e.memset(t_mp[:], 0.0), inc="vv")
    s.do("vector", lambda e: e.tensor_reduce(
        out=t_mx[0:1, 24:25], in_=t_mx[0:1, 0:8], axis=AX.X, op=ALU.max), waits=("ldr",), inc="vv")
    s.do("vector", lambda e: e.tensor_scalar(
        out=t_mx[0:1, 25:26], in0=t_mx[0:1, 24:25], scalar1=1.0 / 32767, scalar2=1e-12,
        op0=ALU.mult, op1=ALU.max), inc="vv")
    s.do("vector", lambda e: e.reciprocal(out=t_mx[0:1, 26:27], in_=t_mx[0:1, 25:26]), inc="vv")
    s.do("vector", lambda e: e.tensor_reduce(
        out=t_mx[0:1, 27:28], in_=t_mx[0:1, 8:16], axis=AX.X, op=ALU.max), inc="vv")
    s.do("vector", lambda e: e.tensor_scalar(
        out=t_mx[0:1, 28:29], in0=t_mx[0:1, 27:28], scalar1=1.0 / 32767, scalar2=1e-12,
        op0=ALU.mult, op1=ALU.max), inc="vv")
    s.do("vector", lambda e: e.tensor_scalar(
        out=t_mx[0:1, 29:30], in0=t_mx[0:1, 26:27], scalar1=t_mx[0:1, 28:29], scalar2=None,
        op0=ALU.mult), inc="vv")
    s.do("vector", lambda e: e.tensor_copy(out=t_mx[0:1, 30:31], in_=t_mx[0:1, 25:26]), inc="vv")
    s.wait("tensor", "vv", "ldr")
    s.do("tensor", lambda e: e.matmul(out=p0[:, 0:2], lhsT=t_onesr[:], rhs=t_mx[0:1, 29:31],
                                      start=True, stop=True), inc="mm")
    s.do("scalar", lambda e: e.copy(out=t_s[:, 0:2], in_=p0[:, 0:2]), waits=("mm",), inc="ss")
    for mt in range(24):
        s.do("sync", lambda e, mt=mt: e.dma_start(out=t_g[:], in_=gate[mt]),
             waits=("vv", "ss", "gst"), inc="ldx", dma=True)
        s.wait("vector", "ldx", "ss")
        s.do("vector", lambda e: e.tensor_scalar(
            out=t_gi[:], in0=t_g[:], scalar1=t_s[:, 0:1], scalar2=RMAGIC,
            op0=ALU.mult, op1=ALU.add), inc="vv")
        s.do("vector", lambda e: e.tensor_scalar(
            out=t_gi[:], in0=t_gi[:], scalar1=RMAGIC, scalar2=None, op0=ALU.subtract), inc="vv")
        s.do("scalar", lambda e: e.activation(
            out=t_sig[:], in_=t_gi[:], func=AF.Sigmoid, scale=t_s[:, 1:2]),
            waits=("vv",), inc="ss")
        s.do("vector", lambda e: e.tensor_scalar(
            out=t_sig[:], in0=t_sig[:], scalar1=65536.0, scalar2=RMAGIC,
            op0=ALU.mult, op1=ALU.add), waits=("ss",), inc="vv")
        s.do("vector", lambda e: e.tensor_scalar(
            out=t_sig[:], in0=t_sig[:], scalar1=RMAGIC, scalar2=65535.0,
            op0=ALU.subtract, op1=ALU.min), inc="vv")
        s.do("vector", lambda e: e.tensor_tensor(
            out=t_P[:], in0=t_gi[:], in1=t_sig[:], op=ALU.mult), inc="vv")
        s.do("vector", lambda e: e.tensor_reduce(
            out=t_s[:, 2:3], in_=t_P[:], axis=AX.X, op=ALU.max,
            apply_absolute_value=True), inc="vv")
        s.do("vector", lambda e: e.tensor_tensor(
            out=t_mp[:], in0=t_mp[:], in1=t_s[:, 2:3], op=ALU.max), inc="vv")
        s.do("gpsimd", lambda e: e.tensor_copy(out=t_Pb[:], in_=t_P[:]), waits=("vv", "gst"), inc="gg")
        s.do("gpsimd", lambda e, mt=mt: e.dma_start(out=P_o[mt], in_=t_Pb[:]),
             waits=("gg",), inc="gst", dma=True)
    s.wait("tensor", "vv", "gst")
    s.do("tensor", lambda e: e.transpose(out=p0[0:1, 0:128], in_=t_mp[:], identity=t_idn[:]), inc="mm")
    s.do("scalar", lambda e: e.copy(out=t_mrow[:], in_=p0[0:1, 0:128]), waits=("mm",), inc="ss")
    s.do("vector", lambda e: e.tensor_reduce(
        out=t_mp[0:1, 0:1], in_=t_mrow[:], axis=AX.X, op=ALU.max), waits=("ss",), inc="vv")
    s.do("vector", lambda e: e.tensor_scalar(
        out=t_mp[0:1, 0:1], in0=t_mp[0:1, 0:1], scalar1=t_mx[0:1, 25:26], scalar2=1.0 / 65536,
        op0=ALU.mult, op1=ALU.mult), inc="vv")
    s.do("gpsimd", lambda e: e.dma_start(out=maxp_o, in_=t_mp[0:1, 0:1]),
         waits=("vv",), inc="gst", dma=True)
    s.wait("gpsimd", "gst")
    s.finalize()
    ctx.close()
    return nc


def prep_D(d, gates, maxgs, max1s):
    cst = host_consts()
    row = np.zeros((1, 24), np.float32)
    row[0, 0:8] = np.concatenate([m.reshape(1) for m in maxgs])
    row[0, 8:16] = np.concatenate([m.reshape(1) for m in max1s])
    in_maps = []
    for c in range(NCORE):
        m = {"gate": gates[c], "scal": row, "conesr": cst["onesr"], "cidn": cst["idn128"]}
        in_maps.append({k: np.ascontiguousarray(v) for k, v in m.items()})
    return in_maps


def build_E():
    nc = bass.Bass("TRN2", target_bir_lowering=False, debug=False)
    dram = lambda n, s, d, k: nc.dram_tensor(n, list(s), d, kind=k).ap()
    NTOK = T // 2
    up = dram("up", (24, 128, NTOK), BF16, "ExternalInput")
    P = dram("P", (24, 128, NTOK), BF16, "ExternalInput")
    scal = dram("scal", (1, 40), F32, "ExternalInput")
    conesr = dram("conesr", (1, 128), F32, "ExternalInput")
    cidn = dram("cidn", (128, 128), F32, "ExternalInput")
    O_o = dram("O_o", (24, 128, NTOK), BF16, "ExternalOutput")
    maxo_o = dram("maxo_o", (1, 1), F32, "ExternalOutput")

    ctx = ExitStack()
    sb, ps = _mk(nc, ctx)
    s = Sched(nc, ctx)
    t_u = sb("t_u", (128, NTOK), BF16)
    t_p = sb("t_p", (128, NTOK), BF16)
    t_ui = sb("t_ui", (128, NTOK), F32)
    t_pi = sb("t_pi", (128, NTOK), F32)
    t_O = sb("t_O", (128, NTOK), F32)
    t_Ob = sb("t_Ob", (128, NTOK), BF16)
    t_mx = sb("t_mx", (1, 48), F32)
    t_s = sb("t_s", (128, 4), F32)
    t_mo = sb("t_mo", (128, 1), F32)
    t_mrow = sb("t_mrow", (1, 128), F32)
    t_onesr = sb("t_onesr", (1, 128), F32)
    t_idn = sb("t_idn", (128, 128), F32)
    p0 = ps("p0", (128, 512), F32)

    def ld(dst, src):
        s.do("sync", lambda e, d=dst, sr=src: e.dma_start(out=d, in_=sr), inc="ldr", dma=True)
    ld(t_mx[0:1, 0:40], scal); ld(t_onesr[:], conesr); ld(t_idn[:], cidn)
    s.do("vector", lambda e: e.memset(t_mo[:], 0.0), inc="vv")
    def scl(out_c, in0, in1, op0=ALU.mult, op1=None, s2=None):
        s.do("vector", lambda e, out_c=out_c, in0=in0, in1=in1, op0=op0, op1=op1, s2=s2: e.tensor_scalar(
            out=t_mx[0:1, out_c:out_c + 1], in0=t_mx[0:1, in0:in0 + 1],
            scalar1=t_mx[0:1, in1:in1 + 1] if isinstance(in1, int) else in1,
            scalar2=s2, op0=op0, op1=op1 if op1 else ALU.bypass) if op1 else e.tensor_scalar(
            out=t_mx[0:1, out_c:out_c + 1], in0=t_mx[0:1, in0:in0 + 1],
            scalar1=t_mx[0:1, in1:in1 + 1] if isinstance(in1, int) else in1,
            scalar2=None, op0=op0), inc="vv")

    def mkscale(src0, n, out_max, out_s, out_rs):
        s.do("vector", lambda e, src0=src0, n=n, out_max=out_max: e.tensor_reduce(
            out=t_mx[0:1, out_max:out_max + 1], in_=t_mx[0:1, src0:src0 + n],
            axis=AX.X, op=ALU.max), waits=("ldr",), inc="vv")
        s.do("vector", lambda e, out_max=out_max, out_s=out_s: e.tensor_scalar(
            out=t_mx[0:1, out_s:out_s + 1], in0=t_mx[0:1, out_max:out_max + 1],
            scalar1=1.0 / 32767, scalar2=1e-12, op0=ALU.mult, op1=ALU.max), inc="vv")
        s.do("vector", lambda e, out_s=out_s, out_rs=out_rs: e.reciprocal(
            out=t_mx[0:1, out_rs:out_rs + 1], in_=t_mx[0:1, out_s:out_s + 1]), inc="vv")

    mkscale(0, 8, 40, 41, 42)
    mkscale(8, 8, 43, 44, 45)
    mkscale(16, 8, 46, 47, 36)
    mkscale(24, 8, 33, 34, 35)
    scl(37, 34, 45)
    s.do("vector", lambda e: e.tensor_scalar(
        out=t_mx[0:1, 38:39], in0=t_mx[0:1, 37:38], scalar1=1.0 / 65536, scalar2=None,
        op0=ALU.mult), inc="vv")
    scl(39, 42, 47)
    s.wait("tensor", "vv", "ldr")
    s.do("tensor", lambda e: e.matmul(out=p0[:, 0:2], lhsT=t_onesr[:], rhs=t_mx[0:1, 38:40],
                                      start=True, stop=True), inc="mm")
    s.do("scalar", lambda e: e.copy(out=t_s[:, 0:2], in_=p0[:, 0:2]), waits=("mm",), inc="ss")
    for mt in range(24):
        s.do("sync", lambda e, mt=mt: e.dma_start(out=t_u[:], in_=up[mt]),
             waits=("vv", "ss", "gst", "gg"), inc="ldx", dma=True)
        s.do("sync", lambda e, mt=mt: e.dma_start(out=t_p[:], in_=P[mt]), inc="ldx", dma=True)
        s.wait("vector", "ldx", "ss")
        s.do("vector", lambda e: e.tensor_scalar(
            out=t_ui[:], in0=t_u[:], scalar1=t_s[:, 1:2], scalar2=RMAGIC,
            op0=ALU.mult, op1=ALU.add), inc="vv")
        s.do("vector", lambda e: e.tensor_scalar(
            out=t_ui[:], in0=t_ui[:], scalar1=RMAGIC, scalar2=None, op0=ALU.subtract), inc="vv")
        s.do("vector", lambda e: e.tensor_scalar(
            out=t_pi[:], in0=t_p[:], scalar1=t_s[:, 0:1], scalar2=RMAGIC,
            op0=ALU.mult, op1=ALU.add), inc="vv")
        s.do("vector", lambda e: e.tensor_scalar(
            out=t_pi[:], in0=t_pi[:], scalar1=RMAGIC, scalar2=None, op0=ALU.subtract), inc="vv")
        s.do("vector", lambda e: e.tensor_tensor(
            out=t_O[:], in0=t_ui[:], in1=t_pi[:], op=ALU.mult), inc="vv")
        s.do("vector", lambda e: e.tensor_reduce(
            out=t_s[:, 2:3], in_=t_O[:], axis=AX.X, op=ALU.max,
            apply_absolute_value=True), inc="vv")
        s.do("vector", lambda e: e.tensor_tensor(
            out=t_mo[:], in0=t_mo[:], in1=t_s[:, 2:3], op=ALU.max), inc="vv")
        s.do("gpsimd", lambda e: e.tensor_copy(out=t_Ob[:], in_=t_O[:]), waits=("vv", "gst"), inc="gg")
        s.do("gpsimd", lambda e, mt=mt: e.dma_start(out=O_o[mt], in_=t_Ob[:]),
             waits=("gg",), inc="gst", dma=True)
    s.wait("tensor", "vv", "gst")
    s.do("tensor", lambda e: e.transpose(out=p0[0:1, 0:128], in_=t_mo[:], identity=t_idn[:]), inc="mm")
    s.do("scalar", lambda e: e.copy(out=t_mrow[:], in_=p0[0:1, 0:128]), waits=("mm",), inc="ss")
    s.do("vector", lambda e: e.tensor_reduce(
        out=t_mo[0:1, 0:1], in_=t_mrow[:], axis=AX.X, op=ALU.max), waits=("ss",), inc="vv")
    s.do("vector", lambda e: e.tensor_scalar(
        out=t_mo[0:1, 0:1], in0=t_mo[0:1, 0:1], scalar1=t_mx[0:1, 41:42], scalar2=t_mx[0:1, 44:45],
        op0=ALU.mult, op1=ALU.mult), inc="vv")
    s.do("gpsimd", lambda e: e.dma_start(out=maxo_o, in_=t_mo[0:1, 0:1]),
         waits=("vv",), inc="gst", dma=True)
    s.wait("gpsimd", "gst")
    s.finalize()
    ctx.close()
    return nc


def prep_E(d, ups, Ps, maxus, maxps, max1s, maxgs):
    cst = host_consts()
    row = np.zeros((1, 40), np.float32)
    row[0, 0:8] = np.concatenate([m.reshape(1) for m in maxus])
    row[0, 8:16] = np.concatenate([m.reshape(1) for m in maxps])
    row[0, 16:24] = np.concatenate([m.reshape(1) for m in max1s])
    row[0, 24:32] = np.concatenate([m.reshape(1) for m in maxgs])
    in_maps = []
    for c in range(NCORE):
        m = {"up": ups[c], "P": Ps[c], "scal": row, "conesr": cst["onesr"], "cidn": cst["idn128"]}
        in_maps.append({k: np.ascontiguousarray(v) for k, v in m.items()})
    return in_maps


def build_F():
    nc = bass.Bass("TRN2", target_bir_lowering=False, debug=False)
    dram = lambda n, s, d, k: nc.dram_tensor(n, list(s), d, kind=k).ap()
    NTOK = T // 2
    NSC = NTOK // SC
    O = dram("O", (24, 128, NTOK), BF16, "ExternalInput")
    x1 = dram("x1", (8, 128, NTOK), F32, "ExternalInput")
    wqdT = dram("wqdT", (24, 128, 1024), BF16, "ExternalInput")
    mods = dram("mods", (128, 48), F32, "ExternalInput")
    scal = dram("scal", (1, 24), F32, "ExternalInput")
    conesr = dram("conesr", (1, 128), F32, "ExternalInput")
    x2_o = dram("x2_o", (8, 128, NTOK), F32, "ExternalOutput")

    ctx = ExitStack()
    sb, ps = _mk(nc, ctx)
    s = Sched(nc, ctx)
    t_wd = sb("t_wd", (128, 24 * 1024), BF16)
    t_O = sb("t_O", (128, 24 * SC), BF16)
    t_oi = sb("t_oi", (128, 24 * SC), BF16)
    t_x1 = sb("t_x1", (128, SC), F32)
    t_x2 = sb("t_x2", (128, SC), F32)
    t_mx = sb("t_mx", (1, 48), F32)
    t_s = sb("t_s", (128, 4), F32)
    t_mods = sb("t_mods", (128, 48), F32)
    t_gbs = sb("t_gbs", (128, 8), F32)
    t_onesr = sb("t_onesr", (1, 128), F32)
    t_sq = sb("t_sq", (128, SC), F32)
    p0 = ps("p0", (128, 512), F32)
    p1 = ps("p1", (128, 512), F32)

    def ld(dst, src):
        s.do("sync", lambda e, d=dst, sr=src: e.dma_start(out=d, in_=sr), inc="ldr", dma=True)
    ld(t_mx[0:1, 0:24], scal); ld(t_onesr[:], conesr); ld(t_mods[:], mods)
    s.do("gpsimd", lambda e: e.dma_start(
        out=t_wd[:].rearrange("p (k m) -> p k m", k=24), in_=wqdT.rearrange("k p m -> p k m")),
        inc="gld", dma=True)
    def mks(src0, out_max, out_s, out_rs):
        s.do("vector", lambda e, src0=src0, out_max=out_max: e.tensor_reduce(
            out=t_mx[0:1, out_max:out_max + 1], in_=t_mx[0:1, src0:src0 + 8],
            axis=AX.X, op=ALU.max), waits=("ldr",), inc="vv")
        s.do("vector", lambda e, out_max=out_max, out_s=out_s: e.tensor_scalar(
            out=t_mx[0:1, out_s:out_s + 1], in0=t_mx[0:1, out_max:out_max + 1],
            scalar1=1.0 / 32767, scalar2=1e-12, op0=ALU.mult, op1=ALU.max), inc="vv")
        if out_rs is not None:
            s.do("vector", lambda e, out_s=out_s, out_rs=out_rs: e.reciprocal(
                out=t_mx[0:1, out_rs:out_rs + 1], in_=t_mx[0:1, out_s:out_s + 1]), inc="vv")
    mks(0, 32, 25, 26)
    mks(8, 33, 34, None)
    mks(16, 35, 36, None)
    s.do("vector", lambda e: e.tensor_scalar(
        out=t_mx[0:1, 37:38], in0=t_mx[0:1, 34:35], scalar1=t_mx[0:1, 36:37], scalar2=None,
        op0=ALU.mult), inc="vv")
    s.do("vector", lambda e: e.tensor_scalar(
        out=t_mx[0:1, 27:28], in0=t_mx[0:1, 26:27], scalar1=t_mx[0:1, 37:38], scalar2=None,
        op0=ALU.mult), inc="vv")
    s.wait("tensor", "vv", "ldr")
    s.do("tensor", lambda e: e.matmul(out=p0[:, 0:1], lhsT=t_onesr[:], rhs=t_mx[0:1, 27:28],
                                      start=True, stop=True), inc="mm")
    s.do("scalar", lambda e: e.copy(out=t_s[:, 0:1], in_=p0[:, 0:1]), waits=("mm",), inc="ss")
    s.do("tensor", lambda e: e.matmul(out=p0[:, 1:2], lhsT=t_onesr[:], rhs=t_mx[0:1, 25:26],
                                      start=True, stop=True), waits=("ss",), inc="mm")
    s.do("scalar", lambda e: e.copy(out=t_s[:, 1:2], in_=p0[:, 1:2]), waits=("mm",), inc="ss")
    s.do("vector", lambda e: e.tensor_scalar(
        out=t_gbs[:], in0=t_mods[:, 40:48], scalar1=t_s[:, 1:2], scalar2=None,
        op0=ALU.mult), waits=("ss",), inc="vv")
    s.wait("tensor", "gld", "vv")
    for sc in range(NSC):
        s.do("sync", lambda e, sc=sc: e.dma_start(
            out=t_O[:].rearrange("p (a t) -> p a t", a=24),
            in_=O[:, :, sc * SC:(sc + 1) * SC].rearrange("a p t -> p a t")),
            waits=("vv", "mm", "gst"), inc="ldx", dma=True)
        s.wait("vector", "ldx", "ss")
        for a in range(24):
            s.do("vector", lambda e, a=a: e.tensor_scalar(
                out=t_sq[:], in0=t_O[:, a * SC:(a + 1) * SC], scalar1=t_s[:, 0:1],
                scalar2=RMAGIC, op0=ALU.mult, op1=ALU.add), inc="vv")
            s.do("vector", lambda e, a=a: e.tensor_scalar(
                out=t_oi[:, a * SC:(a + 1) * SC], in0=t_sq[:], scalar1=RMAGIC, scalar2=None,
                op0=ALU.subtract), inc="vv")
        s.wait("tensor", "vv")
        for mt in range(8):
            pp = (p0, p1)[mt % 2]
            for kk in range(24):
                s.do("tensor", lambda e, mt=mt, kk=kk, pp=pp: e.matmul(
                    out=pp[:, 0:SC],
                    lhsT=t_wd[:, kk * 1024 + mt * 128:kk * 1024 + mt * 128 + 128],
                    rhs=t_oi[:, kk * SC:(kk + 1) * SC], start=(kk == 0), stop=(kk == 23)),
                    waits=("vv",) if kk == 0 else (), inc="mm" if kk == 23 else None)
            s.do("sync", lambda e, mt=mt, sc=sc: e.dma_start(
                out=t_x1[:], in_=x1[mt, :, sc * SC:(sc + 1) * SC]),
                waits=("vv",), inc="ldy", dma=True)
            s.wait("vector", "ldy", "mm")
            s.do("vector", lambda e, mt=mt, pp=pp: e.scalar_tensor_tensor(
                out=t_x2[:], in0=pp[:, 0:SC], scalar=t_gbs[:, mt:mt + 1], in1=t_x1[:],
                op0=ALU.mult, op1=ALU.add), waits=("gst",), inc="vv")
            s.do("gpsimd", lambda e, mt=mt, sc=sc: e.dma_start(
                out=x2_o[mt, :, sc * SC:(sc + 1) * SC], in_=t_x2[:]),
                waits=("vv",), inc="gst", dma=True)
    s.wait("gpsimd", "gst")
    s.finalize()
    ctx.close()
    return nc


def prep_F(d, Os, x1s, modss, wqd, maxos, maxus, maxps):
    cst = host_consts()
    row = np.zeros((1, 24), np.float32)
    row[0, 0:8] = np.concatenate([m.reshape(1) for m in maxos])
    row[0, 8:16] = np.concatenate([m.reshape(1) for m in maxus])
    row[0, 16:24] = np.concatenate([m.reshape(1) for m in maxps])
    wqdT = np.ascontiguousarray(wqd.T).reshape(24, 128, 1024)
    in_maps = []
    for c in range(NCORE):
        b = c // 2
        m = {"O": Os[c], "x1": x1s[c], "wqdT": wqdT, "mods": modss[2 * b],
             "scal": row, "conesr": cst["onesr"]}
        in_maps.append({k: np.ascontiguousarray(v) for k, v in m.items()})
    return in_maps



def build_G(dbg=False):
    nc = bass.Bass("TRN2", target_bir_lowering=False, debug=False)
    dram = lambda n, s, d, k: nc.dram_tensor(n, list(s), d, kind=k).ap()
    NSC = T // SC
    NCH = SC // LCH
    x2T = dram("x2T", (8, 128, T), F32, "ExternalInput")
    wmod = dram("wmod", (1024, 6144), F32, "ExternalInput")
    c_col = dram("c_col", (128, 8), F32, "ExternalInput")
    Wq = dram("Wq", (8, 128, 512), F32, "ExternalInput")
    Wk = dram("Wk", (8, 128, 512), F32, "ExternalInput")
    Wv = dram("Wv", (8, 128, 512), F32, "ExternalInput")
    Wg = dram("Wg", (8, 128, 512), F32, "ExternalInput")
    Wb = dram("Wb", (8, 128, 8), F32, "ExternalInput")
    cw = dram("cw", (128, 3 * 4 * 4), F32, "ExternalInput")
    cb = dram("cb", (128, 3 * 4), F32, "ExternalInput")
    nw_r = dram("nw_r", (1, 512), F32, "ExternalInput")
    ctr = dram("ctrl", (128, 128), F32, "ExternalInput")
    cmasks = dram("cmasks", (128, 128), F32, "ExternalInput")
    cmaski = dram("cmaski", (128, 128), F32, "ExternalInput")
    cidn = dram("cidn", (128, 128), F32, "ExternalInput")
    cidnb = dram("cidnb", (128, 64), BF16, "ExternalInput")
    cidnb8 = dram("cidnb8", (8, 8), BF16, "ExternalInput")
    cidn128b = dram("cidn128b", (128, 128), BF16, "ExternalInput")
    cones = dram("cones", (128, 1), F32, "ExternalInput")
    conesb = dram("conesb", (128, 1), BF16, "ExternalInput")
    conesr = dram("conesr", (1, 128), F32, "ExternalInput")
    conesrb = dram("conesrb", (1, 128), BF16, "ExternalInput")
    cones11 = dram("cones11", (1, 1), F32, "ExternalInput")
    yg1_o = dram("yg1_o", (4, 128, T), BF16, "ExternalOutput")
    mods1_o = dram("mods1_o", (128, 48), F32, "ExternalOutput")
    dbg_map = {}
    if dbg:
        for nm, shp, dt in (("dbg_qc", (128, 4 * SC), BF16), ("dbg_kc", (128, 4 * SC), BF16),
                            ("dbg_vs", (128, 4 * SC), BF16), ("dbg_gs", (128, 4 * SC), BF16),
                            ("dbg_beta", (128, (SC // LCH) * 8), F32), ("dbg_O", (128, 64), F32),
                            ("dbg_U", (128, 64), BF16), ("dbg_Y", (128, 128), BF16),
                            ("dbg_h3", (128, 8 * SC), BF16),
                            ("dbg_rq", (1, 8 * SC), BF16), ("dbg_sqb", (128, 4 * SC), BF16),
                            ("dbg_qs", (128, 4 * SC), BF16),
                            ("dbg_qpre", (128, 4 * (SC + 3)), F32),
                            ("dbg_S", (128, 8 * 64), F32), ("dbg_S0", (128, 8 * 64), F32)):
            dbg_map[nm] = dram(nm, list(shp), dt, "ExternalOutput")

    ctx = ExitStack()
    sb, ps = _mk(nc, ctx)
    s = Sched(nc, ctx)
    t_wq = sb("t_wq", (128, 8 * 512), BF16)
    t_wk = sb("t_wk", (128, 8 * 512), BF16)
    t_wv = sb("t_wv", (128, 8 * 512), BF16)
    t_wg = sb("t_wg", (128, 8 * 512), BF16)
    t_wb = sb("t_wb", (128, 8 * 8), BF16)
    t_wm = sb("t_wm", (128, 4096), F32)
    t_modrow = sb("t_modrow", (1, 6144), F32)
    t_ccol = sb("t_ccol", (128, 8), F32)
    t_cw = sb("t_cw", (128, 48), F32)
    t_cb = sb("t_cb", (128, 12), F32)
    t_mods = sb("t_mods", (128, 48), F32)
    t_onepca = sb("t_onepca", (128, 8), F32)
    t_masks = sb("t_masks", (128, 128), F32)
    t_maski = sb("t_maski", (128, 128), F32)
    t_idn = sb("t_idn", (128, 128), F32)
    t_idnb = sb("t_idnb", (128, 64), BF16)
    t_idnb8 = sb("t_idnb8", (8, 8), BF16)
    t_idn128b = sb("t_idn128b", (128, 128), BF16)
    t_ones = sb("t_ones", (128, 1), F32)
    t_onesb = sb("t_onesb", (128, 1), BF16)
    t_onesr = sb("t_onesr", (1, 128), F32)
    t_onesrb = sb("t_onesrb", (1, 128), BF16)
    t_ones11 = sb("t_ones11", (1, 1), F32)
    t_nwall = sb("t_nwall", (128, 512), F32)
    t_epsg = sb("t_epsg", (128, 1), F32)
    t_epsl2 = sb("t_epsl2", (128, 1), F32)
    t_epsrms = sb("t_epsrms", (128, 1), F32)
    t_x = sb("t_x", (128, 8 * SC), F32)
    t_sq = sb("t_sq", (128, SC), F32)
    t_sq2 = sb("t_sq2", (128, SC), F32)
    t_row1 = sb("t_row1", (1, SC), F32)
    t_row2 = sb("t_row2", (1, SC), F32)
    t_row3 = sb("t_row3", (1, SC), F32)
    t_mub = sb("t_mub", (128, SC), F32)
    t_rsb = sb("t_rsb", (128, SC), F32)
    t_h3 = sb("t_h3", (128, 8 * SC), BF16)
    t_qpre = sb("t_qpre", (128, 4 * (SC + 3)), F32)
    t_kpre = sb("t_kpre", (128, 4 * (SC + 3)), F32)
    t_vpre = sb("t_vpre", (128, 4 * (SC + 3)), F32)
    t_cv = sb("t_cv", (128, SC), F32)
    t_cv2 = sb("t_cv2", (128, SC), F32)
    t_qs = sb("t_qs", (128, 4 * SC), BF16)
    t_ks = sb("t_ks", (128, 4 * SC), BF16)
    t_vs = sb("t_vs", (128, 4 * SC), BF16)
    t_gs = sb("t_gs", (128, 4 * SC), BF16)
    t_sqb = sb("t_sqb", (128, 4 * SC), BF16)
    t_rq = sb("t_rq", (1, 8 * SC), BF16)
    t_rk = sb("t_rk", (1, 8 * SC), BF16)
    t_qc = sb("t_qc", (128, 4 * SC), BF16)
    t_kc = sb("t_kc", (128, 4 * SC), BF16)
    t_blog = sb("t_blog", (8, SC), F32)
    t_beta = sb("t_beta", (128, NCH * 8), F32)
    t_Y = sb("t_Y", (128, 128), F32)
    t_P = sb("t_P", (128, 128), F32)
    t_Z = sb("t_Z", (128, 7 * 128), F32)
    t_R = sb("t_R", (128, 64), BF16)
    t_R32 = sb("t_R32", (128, 64), F32)
    t_Rp = sb("t_Rp", (128, 64), F32)
    t_vT2 = sb("t_vT2", (128, 128), BF16)
    t_kT2 = sb("t_kT2", (128, 128), BF16)
    t_Gm = sb("t_Gm", (128, 128), BF16)
    t_S = sb("t_S", (128, 8 * 64), F32)
    t_Sb = sb("t_Sb", (128, 8 * 64), BF16)
    t_Oall = sb("t_Oall", (128, 512), F32)
    t_rms8 = sb("t_rms8", (128, 8), F32)
    t_Ob = sb("t_Ob", (128, 512), BF16)
    t_nwrow = sb("t_nwrow", (1, 512), F32)
    t_yg = sb("t_yg", (128, 4 * SC), BF16)
    p0 = ps("p0", (128, 512), F32)
    p1 = ps("p1", (128, 512), F32)
    p2 = ps("p2", (128, 512), F32)
    p3 = ps("p3", (128, 512), F32)
    p4 = ps("p4", (128, 512), F32)
    p6b = ps("p6b", (128, 1024), BF16)
    p7b = ps("p7b", (128, 1024), BF16)

    def ld(dst, src):
        s.do("sync", lambda e, d=dst, sr=src: e.dma_start(out=d, in_=sr), inc="ldr", dma=True)

    def gld(dst, src):
        s.do("gpsimd", lambda e, d=dst, sr=src: e.dma_start(out=d, in_=sr), inc="gld", dma=True)

    ld(t_ccol[:], c_col); ld(t_cw[:], cw); ld(t_cb[:], cb)
    ld(t_masks[:], cmasks); ld(t_maski[:], cmaski); ld(t_idn[:], cidn)
    ld(t_idnb[:], cidnb); ld(t_idnb8[:], cidnb8); ld(t_idn128b[:], cidn128b)
    ld(t_ones[:], cones); ld(t_onesb[:], conesb); ld(t_onesr[:], conesr)
    ld(t_onesrb[:], conesrb); ld(t_ones11[:], cones11); ld(t_nwrow[:], nw_r)
    gld(t_wq[:].rearrange("p (k m) -> p k m", k=8), Wq.rearrange("k p m -> p k m"))
    gld(t_wk[:].rearrange("p (k m) -> p k m", k=8), Wk.rearrange("k p m -> p k m"))
    gld(t_wv[:].rearrange("p (k m) -> p k m", k=8), Wv.rearrange("k p m -> p k m"))
    gld(t_wg[:].rearrange("p (k m) -> p k m", k=8), Wg.rearrange("k p m -> p k m"))
    gld(t_wb[:].rearrange("p (k m) -> p k m", k=8), Wb.rearrange("k p m -> p k m"))
    s.do("vector", lambda e: e.memset(t_S[:], 0.0), inc="vv")
    s.do("vector", lambda e: e.memset(t_epsg[:], EPS_LN), inc="vv")
    s.do("vector", lambda e: e.memset(t_epsl2[:], 1e-6), inc="vv")
    s.do("vector", lambda e: e.memset(t_epsrms[:], 1e-5), inc="vv")
    s.do("vector", lambda e: e.memset(t_Sb[:], 0.0), inc="vv")
    s.do("vector", lambda e: e.memset(t_qpre[:], 0.0), inc="vv")
    s.do("vector", lambda e: e.memset(t_kpre[:], 0.0), inc="vv")
    s.do("vector", lambda e: e.memset(t_vpre[:], 0.0), inc="vv")
    s.wait("tensor", "ldr")
    s.do("tensor", lambda e: e.matmul(out=p0[:, 0:512], lhsT=t_onesr[:], rhs=t_nwrow[:],
                                      start=True, stop=True), inc="mm")
    s.do("scalar", lambda e: e.copy(out=t_nwall[:], in_=p0[:, 0:512]), waits=("mm",), inc="ss")
    s.do("scalar", lambda e: e.activation(out=t_ccol[:], in_=t_ccol[:], func=AF.Silu),
         waits=("ldr",), inc="ss")
    for cchunk in range(12):
        s.do("sync", lambda e, cc=cchunk: e.dma_start(
            out=t_wm[:].rearrange("p (kk m) -> p kk m", kk=8),
            in_=wmod[:, cc * 512:(cc + 1) * 512].rearrange("(kk p) m -> p kk m", p=128)),
            waits=("mmod",) if cchunk > 0 else ("ss",), inc="ldw", dma=True)
        s.wait("tensor", "ldw", "ss")
        for kk in range(8):
            s.do("tensor", lambda e, kk=kk: e.matmul(
                out=p1[0:1, 0:512], lhsT=t_ccol[:, kk:kk + 1],
                rhs=t_wm[:, kk * 512:(kk + 1) * 512], start=(kk == 0), stop=(kk == 7)),
                inc="mm" if kk == 7 else None)
        s.do("scalar", lambda e, cc=cchunk: e.copy(
            out=t_modrow[0:1, cc * 512:(cc + 1) * 512], in_=p1[0:1, 0:512]),
            waits=("mm",), inc="mmod")
    s.wait("tensor", "mmod")
    for col in range(48):
        s.do("tensor", lambda e, col=col: e.matmul(
            out=p1[:, col:col + 1], lhsT=t_modrow[0:1, col * 128:(col + 1) * 128],
            rhs=t_ones11[:], start=True, stop=True), inc="mm" if col == 47 else None)
    s.do("scalar", lambda e: e.copy(out=t_mods[:], in_=p1[:, 0:48]), waits=("mm",), inc="ss")
    for c0 in (16, 40):
        s.do("scalar", lambda e, c0=c0: e.activation(
            out=t_sq[:, 0:8], in_=t_mods[:, c0:c0 + 8], func=AF.Abs), waits=("ss", "vv"), inc="ss")
        s.do("vector", lambda e, c0=c0: e.tensor_scalar(
            out=t_sq[:, 0:8], in0=t_sq[:, 0:8], scalar1=0.25, scalar2=None,
            op0=ALU.max), waits=("ss",), inc="vv")
        s.do("scalar", lambda e, c0=c0: e.activation(
            out=t_sq[:, 8:16], in_=t_mods[:, c0:c0 + 8], func=AF.Sign), waits=("vv",), inc="ss")
        s.do("vector", lambda e, c0=c0: e.tensor_tensor(
            out=t_mods[:, c0:c0 + 8], in0=t_sq[:, 0:8], in1=t_sq[:, 8:16],
            op=ALU.mult), waits=("ss",), inc="vv")
    s.do("gpsimd", lambda e: e.dma_start(out=mods1_o, in_=t_mods[:]),
         waits=("vv",), inc="gst", dma=True)
    s.do("vector", lambda e: e.tensor_scalar(
        out=t_onepca[:], in0=t_mods[:, 8:16], scalar1=1.0, scalar2=None, op0=ALU.add),
        waits=("vv",), inc="vv")
    s.wait("tensor", "gld", "vv", "ss")

    for sc in range(NSC):
        s.do("sync", lambda e, sc=sc: e.dma_start(
            out=t_x[:].rearrange("p (a t) -> p a t", a=8),
            in_=x2T[:, :, sc * SC:(sc + 1) * SC].rearrange("a p t -> p a t")),
            waits=("vv", "ss", "mm", "gg", "gst"), inc="ldx", dma=True)
        s.wait("tensor", "ldx")
        s.wait("scalar", "ldx")
        for a in range(8):
            s.do("tensor", lambda e, a=a: e.matmul(
                out=p0[0:1, 0:SC], lhsT=t_ones[:], rhs=t_x[:, a * SC:(a + 1) * SC],
                start=(a == 0), stop=(a == 7)), inc="mm" if a == 7 else None)
        for a in range(8):
            buf = t_sq if a % 2 == 0 else t_sq2
            s.do("scalar", lambda e, a=a, buf=buf: e.activation(
                out=buf[:], in_=t_x[:, a * SC:(a + 1) * SC], func=AF.Square),
                waits=("mm",), inc="ss")
            s.wait("tensor", "ss")
            s.do("tensor", lambda e, a=a, buf=buf: e.matmul(
                out=p1[0:1, 0:SC], lhsT=t_ones[:], rhs=buf[:],
                start=(a == 0), stop=(a == 7)), inc="mm")
        s.do("scalar", lambda e: e.activation(out=t_row1[:], in_=p0[0:1, 0:SC],
                                              func=AF.Copy, scale=1.0 / 1024), waits=("mm",), inc="ss")
        s.do("scalar", lambda e: e.activation(out=t_row2[:], in_=p1[0:1, 0:SC],
                                              func=AF.Copy, scale=1.0 / 1024), inc="ss")
        s.do("vector", lambda e: e.tensor_copy(out=t_row3[:], in_=t_row1[:]), waits=("ss",), inc="vv")
        s.do("vector", lambda e: e.tensor_tensor(
            out=t_row3[:], in0=t_row3[:], in1=t_row1[:], op=ALU.mult), inc="vv")
        s.do("vector", lambda e: e.tensor_tensor(
            out=t_row2[:], in0=t_row2[:], in1=t_row3[:], op=ALU.subtract), inc="vv")
        s.do("scalar", lambda e: e.activation(out=t_row3[:], in_=t_row2[:], func=AF.Sqrt,
                                              bias=t_epsg[0:1, :]), waits=("vv",), inc="ss")
        s.do("vector", lambda e: e.reciprocal(out=t_row2[:], in_=t_row3[:]), waits=("ss",), inc="vv")
        s.wait("tensor", "vv")
        s.do("tensor", lambda e: e.matmul(out=p2[:, 0:SC], lhsT=t_onesr[:],
                                          rhs=t_row1[:], start=True, stop=True), inc="mm")
        s.do("tensor", lambda e: e.matmul(out=p3[:, 0:SC], lhsT=t_onesr[:],
                                          rhs=t_row2[:], start=True, stop=True), inc="mm")
        s.do("scalar", lambda e: e.copy(out=t_mub[:], in_=p2[:, 0:SC]), waits=("mm",), inc="ss")
        s.do("scalar", lambda e: e.copy(out=t_rsb[:], in_=p3[:, 0:SC]), inc="ss")
        for a in range(8):
            buf = t_sq if a % 2 == 0 else t_sq2
            s.do("vector", lambda e, a=a, buf=buf: e.tensor_tensor(
                out=buf[:], in0=t_x[:, a * SC:(a + 1) * SC], in1=t_mub[:],
                op=ALU.subtract), waits=("ss",), inc="vv")
            s.do("vector", lambda e, a=a, buf=buf: e.tensor_tensor(
                out=buf[:], in0=buf[:], in1=t_rsb[:], op=ALU.mult), inc="vv")
            s.do("vector", lambda e, a=a, buf=buf: e.tensor_scalar(
                out=t_h3[:, a * SC:(a + 1) * SC], in0=buf[:],
                scalar1=t_onepca[:, a:a + 1], scalar2=t_mods[:, a:a + 1],
                op0=ALU.mult, op1=ALU.add), inc="vv")
        for tt in (t_qpre, t_kpre, t_vpre):
            for a in range(4):
                s.do("vector", lambda e, tt=tt, a=a: e.tensor_copy(
                    out=tt[:, a * (SC + 3):a * (SC + 3) + 3],
                    in_=tt[:, a * (SC + 3) + SC:a * (SC + 3) + SC + 3]), inc="vv")
        s.wait("tensor", "vv")
        projs = [(t_wq, t_qpre, 1), (t_wk, t_kpre, 1), (t_wv, t_vpre, 1)]
        for wt, dst, _ in projs:
            for mt in range(4):
                pp = (p2, p3)[mt % 2]
                for kk in range(8):
                    s.do("tensor", lambda e, wt=wt, mt=mt, kk=kk, pp=pp: e.matmul(
                        out=pp[:, 0:SC],
                        lhsT=wt[:, kk * 512 + mt * 128:kk * 512 + mt * 128 + 128],
                        rhs=t_h3[:, kk * SC:(kk + 1) * SC], start=(kk == 0), stop=(kk == 7)),
                        waits=("ss", "vv") if kk == 0 else (), inc="mm" if kk == 7 else None)
                s.do("scalar", lambda e, dst=dst, mt=mt, pp=pp: e.copy(
                    out=dst[:, mt * (SC + 3) + 3:mt * (SC + 3) + 3 + SC], in_=pp[:, 0:SC]),
                    waits=("mm",), inc="ss")
        for mt in range(4):
            pp = (p2, p3)[mt % 2]
            for kk in range(8):
                s.do("tensor", lambda e, mt=mt, kk=kk, pp=pp: e.matmul(
                    out=pp[:, 0:SC],
                    lhsT=t_wg[:, kk * 512 + mt * 128:kk * 512 + mt * 128 + 128],
                    rhs=t_h3[:, kk * SC:(kk + 1) * SC], start=(kk == 0), stop=(kk == 7)),
                    waits=("ss",) if kk == 0 else (), inc="mm" if kk == 7 else None)
            s.do("scalar", lambda e, mt=mt, pp=pp: e.activation(
                out=t_gs[:, mt * SC:(mt + 1) * SC], in_=pp[:, 0:SC], func=AF.Silu),
                waits=("mm",), inc="ss")
        for kk in range(8):
            s.do("tensor", lambda e, kk=kk: e.matmul(
                out=p4[0:8, 0:SC], lhsT=t_wb[:, kk * 8:(kk + 1) * 8],
                rhs=t_h3[:, kk * SC:(kk + 1) * SC], start=(kk == 0), stop=(kk == 7)),
                waits=("ss",) if kk == 0 else (), inc="mm" if kk == 7 else None)
        s.do("scalar", lambda e: e.copy(out=t_blog[:], in_=p4[0:8, 0:SC]), waits=("mm",), inc="ss")
        s.wait("tensor", "ss")
        for ch in range(NCH):
            s.do("tensor", lambda e, ch=ch: e.transpose(
                out=p0[:, 0:8], in_=t_blog[0:8, ch * LCH:(ch + 1) * LCH],
                identity=t_idn[0:8, 0:8]), waits=("ss", "vv"), inc="mm")
            s.do("scalar", lambda e, ch=ch: e.activation(
                out=t_beta[:, ch * 8:(ch + 1) * 8], in_=p0[:, 0:8], func=AF.Sigmoid),
                waits=("mm",), inc="ss")
        for pi, (src_t, dst_t) in enumerate(((t_qpre, t_qs), (t_kpre, t_ks), (t_vpre, t_vs))):
            for a in range(4):
                base_c = pi * 16 + a * 4
                s.do("vector", lambda e, src_t=src_t, a=a, base_c=base_c, pi=pi: e.tensor_scalar(
                    out=t_cv[:], in0=src_t[:, a * (SC + 3):a * (SC + 3) + SC],
                    scalar1=t_cw[:, base_c:base_c + 1], scalar2=t_cb[:, pi * 4 + a:pi * 4 + a + 1],
                    op0=ALU.mult, op1=ALU.add), waits=("ss",), inc="vv")
                bufs = (t_cv, t_cv2)
                for j in (1, 2, 3):
                    src_b, dst_b = bufs[(j + 1) % 2], bufs[j % 2]
                    s.do("vector", lambda e, src_t=src_t, a=a, base_c=base_c, j=j,
                         src_b=src_b, dst_b=dst_b: e.scalar_tensor_tensor(
                        out=dst_b[:], in0=src_t[:, a * (SC + 3) + j:a * (SC + 3) + j + SC],
                        scalar=t_cw[:, base_c + j:base_c + j + 1], in1=src_b[:],
                        op0=ALU.mult, op1=ALU.add), inc="vv")
                s.do("scalar", lambda e, dst_t=dst_t, a=a: e.activation(
                    out=dst_t[:, a * SC:(a + 1) * SC], in_=t_cv2[:], func=AF.Silu),
                    waits=("vv", "gg"), inc="ss")
        for src_t, rrow, scale_extra in ((t_qs, t_rq, 0.125), (t_ks, t_rk, 1.0)):
            for a in range(4):
                s.do("scalar", lambda e, src_t=src_t, a=a: e.activation(
                    out=t_sqb[:, a * SC:(a + 1) * SC], in_=src_t[:, a * SC:(a + 1) * SC],
                    func=AF.Square), waits=("ss",), inc="ss")
            s.wait("tensor", "ss")
            for h in range(8):
                a, base = h // 2, (h % 2) * 64
                s.do("tensor", lambda e, h=h, a=a, base=base: e.matmul(
                    out=p0[0:1, 0:SC], lhsT=t_onesb[base:base + 64, :],
                    rhs=t_sqb[base:base + 64, a * SC:(a + 1) * SC], start=True, stop=True),
                    waits=("vv", "ss"), inc="mm")
                s.do("scalar", lambda e, h=h: e.activation(
                    out=t_row3[0:1, 0:SC], in_=p0[0:1, 0:SC], func=AF.Sqrt,
                    bias=t_epsl2[0:1, :]), waits=("mm",), inc="ss")
                s.do("vector", lambda e, h=h: e.reciprocal(
                    out=t_row2[0:1, 0:SC], in_=t_row3[0:1, 0:SC]), waits=("ss",), inc="vv")
                s.do("vector", lambda e, h=h, rrow=rrow, se=scale_extra: e.tensor_scalar(
                    out=rrow[0:1, h * SC:(h + 1) * SC], in0=t_row2[0:1, 0:SC],
                    scalar1=se, scalar2=None, op0=ALU.mult), inc="vv")
        s.wait("tensor", "vv")
        for src_t, rrow, dst_t in ((t_qs, t_rq, t_qc), (t_ks, t_rk, t_kc)):
            for h in range(8):
                a, base = h // 2, (h % 2) * 64
                s.do("tensor", lambda e, h=h, base=base, rrow=rrow: e.matmul(
                    out=p4[base:base + 64, 0:SC],
                    lhsT=t_onesrb[0:1, 0:64], rhs=rrow[0:1, h * SC:(h + 1) * SC],
                    start=True, stop=True, tile_position=(0, base)),
                    waits=("vv",), inc="mm")
                s.do("vector", lambda e, h=h, a=a, base=base, src_t=src_t, dst_t=dst_t: e.tensor_tensor(
                    out=dst_t[base:base + 64, a * SC:(a + 1) * SC],
                    in0=src_t[base:base + 64, a * SC:(a + 1) * SC],
                    in1=p4[base:base + 64, 0:SC], op=ALU.mult),
                    waits=("mm",), inc="vv")
        if dbg and sc == 0:
            for nm, tt in (("dbg_qc", t_qc), ("dbg_kc", t_kc), ("dbg_vs", t_vs),
                           ("dbg_gs", t_gs), ("dbg_beta", t_beta), ("dbg_h3", t_h3),
                           ("dbg_rq", t_rq), ("dbg_qs", t_qs), ("dbg_qpre", t_qpre)):
                s.do("gpsimd", lambda e, nm=nm, tt=tt: e.dma_start(
                    out=dbg_map[nm], in_=tt[:]), waits=("vv", "ss"), inc="gst", dma=True)
            s.wait("vector", "gst"); s.wait("tensor", "gst"); s.wait("scalar", "gst")
        for ch in range(NCH):
            if dbg and sc == 0 and ch == 1:
                s.do("gpsimd", lambda e: e.dma_start(out=dbg_map["dbg_S"], in_=t_S[:]),
                     waits=("vv", "ss"), inc="gst", dma=True)
                s.wait("vector", "gst"); s.wait("tensor", "gst"); s.wait("scalar", "gst")
            tof = ch * LCH
            for h in range(8):
                a, base = h // 2, (h % 2) * 64
                qh = t_qc[base:base + 64, a * SC + tof:a * SC + tof + LCH]
                kh = t_kc[base:base + 64, a * SC + tof:a * SC + tof + LCH]
                vh = t_vs[base:base + 64, a * SC + tof:a * SC + tof + LCH]
                gh = t_gs[base:base + 64, a * SC + tof:a * SC + tof + LCH]
                bcol = t_beta[:, ch * 8 + h:ch * 8 + h + 1]
                Sh = (slice(base, base + 64), slice(h * 64, (h + 1) * 64))
                s.do("tensor", lambda e, kh=kh: e.matmul(
                    out=p2[:, 0:128], lhsT=kh, rhs=kh, start=True, stop=True),
                    waits=("vv", "ss"), inc="mm")
                s.do("vector", lambda e, bcol=bcol: e.scalar_tensor_tensor(
                    out=t_Y[:], in0=p2[:, 0:128], scalar=bcol, in1=t_masks[:],
                    op0=ALU.mult, op1=ALU.mult), waits=("mm",), inc="vv")
                s.wait("tensor", "vv")
                s.do("tensor", lambda e: e.transpose(
                    out=p2[:, 256:384], in_=t_Y[:], identity=t_idn[:]), inc="mm")
                s.do("scalar", lambda e: e.copy(out=t_Z[:, 0:128], in_=p2[:, 256:384]),
                     waits=("mm",), inc="ss")
                s.do("vector", lambda e: e.tensor_copy(out=t_P[:], in_=t_Y[:]), inc="vv")
                for k in range(1, 7):
                    s.do("tensor", lambda e, k=k: e.matmul(
                        out=p2[:, 256:384], lhsT=t_Z[:, (k - 1) * 128:k * 128], rhs=t_P[:],
                        start=True, stop=True), waits=("ss", "vv"), inc="mm")
                    s.do("tensor", lambda e, k=k: e.matmul(
                        out=p3[:, 256:384], lhsT=t_P[:], rhs=t_Z[:, (k - 1) * 128:k * 128],
                        start=True, stop=True), inc="mm")
                    s.do("vector", lambda e, k=k: e.tensor_copy(
                        out=t_P[:], in_=p2[:, 256:384]), waits=("mm",), inc="vv")
                    s.do("scalar", lambda e, k=k: e.copy(
                        out=t_Z[:, k * 128:(k + 1) * 128], in_=p3[:, 256:384]),
                        waits=("mm",), inc="ss")
                if h % 2 == 0:
                    s.do("tensor", lambda e, a=a, tof=tof: e.transpose(
                        out=p6b[:, 256:384],
                        in_=t_vs[:, a * SC + tof:a * SC + tof + LCH],
                        identity=t_idn128b[:]), waits=("ss", "vv"), inc="mm")
                    s.do("vector", lambda e: e.tensor_copy(out=t_vT2[:], in_=p6b[:, 256:384]),
                         waits=("mm",), inc="vv")
                    s.do("tensor", lambda e, a=a, tof=tof: e.transpose(
                        out=p7b[:, 256:384],
                        in_=t_kc[:, a * SC + tof:a * SC + tof + LCH],
                        identity=t_idn128b[:]), inc="mm")
                    s.do("vector", lambda e: e.tensor_copy(out=t_kT2[:], in_=p7b[:, 256:384]),
                         waits=("mm",), inc="vv")
                ho = (h % 2) * 64
                s.do("tensor", lambda e, kh=kh, Sh=Sh: e.matmul(
                    out=p3[:, 0:64], lhsT=kh, rhs=t_Sb[Sh[0], Sh[1]],
                    start=True, stop=True), waits=("vv",), inc="mm")
                s.do("vector", lambda e, ho=ho: e.tensor_tensor(
                    out=t_Rp[:], in0=t_vT2[:, ho:ho + 64], in1=p3[:, 0:64], op=ALU.subtract),
                    waits=("mm", "ss"), inc="vv")
                s.do("vector", lambda e, bcol=bcol: e.tensor_scalar(
                    out=t_R32[:], in0=t_Rp[:], scalar1=bcol, scalar2=None, op0=ALU.mult), inc="vv")
                for k in range(7):
                    s.do("tensor", lambda e, k=k: e.matmul(
                        out=p3[:, 64:128], lhsT=t_Z[:, k * 128:(k + 1) * 128], rhs=t_R32[:],
                        start=True, stop=True), waits=("vv",), inc="mm")
                    s.do("vector", lambda e, k=k: e.tensor_tensor(
                        out=t_R32[:], in0=t_R32[:], in1=p3[:, 64:128], op=ALU.add),
                        waits=("mm",), inc="vv")
                s.do("vector", lambda e: e.tensor_copy(out=t_R[:], in_=t_R32[:]), inc="vv")
                s.wait("tensor", "vv")
                s.do("tensor", lambda e, base=base, Sh=Sh, ho=ho: e.matmul(
                    out=p4[Sh[0], 0:64], lhsT=t_kT2[:, ho:ho + 64], rhs=t_R[:],
                    start=True, stop=True, tile_position=(0, base)), inc="mm")
                s.do("tensor", lambda e, kh=kh, qh=qh: e.matmul(
                    out=p2[:, 128:256], lhsT=kh, rhs=qh, start=True, stop=True), inc="mm")
                s.do("vector", lambda e: e.tensor_tensor(
                    out=t_Gm[:], in0=p2[:, 128:256], in1=t_maski[:], op=ALU.mult),
                    waits=("mm",), inc="vv")
                s.wait("tensor", "vv")
                s.do("tensor", lambda e: e.matmul(
                    out=p3[:, 128:192], lhsT=t_Gm[:], rhs=t_R[:], start=True, stop=False))
                s.do("tensor", lambda e, qh=qh, Sh=Sh: e.matmul(
                    out=p3[:, 128:192], lhsT=qh, rhs=t_Sb[Sh[0], Sh[1]],
                    start=False, stop=True), inc="mm")
                s.do("vector", lambda e, Sh=Sh: e.tensor_tensor(
                    out=t_S[Sh[0], Sh[1]], in0=t_S[Sh[0], Sh[1]], in1=p4[Sh[0], 0:64],
                    op=ALU.add), waits=("mm",), inc="vv")
                if dbg and sc == 0 and ch == 0 and h == 7:
                    s.do("gpsimd", lambda e: e.dma_start(out=dbg_map["dbg_S0"], in_=t_S[:]),
                         waits=("vv",), inc="gst", dma=True)
                    s.wait("vector", "gst"); s.wait("tensor", "gst"); s.wait("scalar", "gst")
                s.do("scalar", lambda e, h=h: e.copy(
                    out=t_Oall[:, h * 64:(h + 1) * 64], in_=p3[:, 128:192]),
                    waits=("mm", "gg") if h == 0 else ("mm",), inc="ss")
                if dbg and sc == 0 and ch == 0 and h == 0:
                    s.do("gpsimd", lambda e: e.dma_start(out=dbg_map["dbg_O"],
                                                         in_=t_Oall[:, 0:64]),
                         waits=("ss",), inc="gst", dma=True)
                    s.do("gpsimd", lambda e: e.dma_start(out=dbg_map["dbg_U"], in_=t_R[:]),
                         waits=("ss",), inc="gst", dma=True)
                    s.do("gpsimd", lambda e: e.dma_start(out=dbg_map["dbg_Y"], in_=t_Y[:]),
                         waits=("ss",), inc="gst", dma=True)
                    s.wait("vector", "gst"); s.wait("tensor", "gst"); s.wait("scalar", "gst")
            s.do("vector", lambda e: e.tensor_copy(out=t_Sb[:], in_=t_S[:]),
                 waits=("mm",), inc="vv")
            for h2 in range(8):
                s.do("scalar", lambda e, h2=h2: e.activation(
                    out=t_Rp[:], in_=t_Oall[:, h2 * 64:(h2 + 1) * 64], func=AF.Square,
                    accum_out=t_rms8[:, h2:h2 + 1]),
                    waits=("vv",) if h2 == 0 else (), inc="ss")
            s.do("scalar", lambda e: e.activation(
                out=t_rms8[:], in_=t_rms8[:], func=AF.Sqrt, scale=1.0 / 64,
                bias=t_epsrms[:]), waits=("vv",), inc="ss")
            s.do("vector", lambda e: e.reciprocal(out=t_rms8[:], in_=t_rms8[:]),
                 waits=("ss",), inc="vv")
            s.do("vector", lambda e: e.tensor_tensor(
                out=t_Oall[:].rearrange("p (h n) -> p h n", n=64),
                in0=t_Oall[:].rearrange("p (h n) -> p h n", n=64),
                in1=t_rms8[:].rearrange("p (h o) -> p h o", o=1).broadcast_to((128, 8, 64)),
                op=ALU.mult), inc="vv")
            s.do("vector", lambda e: e.tensor_tensor(
                out=t_Oall[:], in0=t_Oall[:], in1=t_nwall[:], op=ALU.mult), inc="vv")
            for a in range(4):
                pb = (p6b, p7b)[a % 2]
                s.do("tensor", lambda e, a=a, tof=tof, pb=pb: e.transpose(
                    out=pb[:, 384:512], in_=t_gs[:, a * SC + tof:a * SC + tof + LCH],
                    identity=t_idn128b[:]), waits=("vv",), inc="mm")
                s.do("vector", lambda e, a=a, pb=pb: e.tensor_tensor(
                    out=t_Oall[:, a * 128:(a + 1) * 128],
                    in0=t_Oall[:, a * 128:(a + 1) * 128],
                    in1=pb[:, 384:512], op=ALU.mult), waits=("mm",), inc="vv")
            s.do("gpsimd", lambda e: e.tensor_copy(out=t_Ob[:], in_=t_Oall[:]),
                 waits=("vv",), inc="gg")
            s.wait("tensor", "gg")
            for a in range(4):
                pb = (p6b, p7b)[a % 2]
                s.do("tensor", lambda e, a=a, pb=pb: e.transpose(
                    out=pb[:, 640:768], in_=t_Ob[:, a * 128:(a + 1) * 128],
                    identity=t_idn128b[:]), waits=("vv",) if a >= 2 else (), inc="mm")
                s.do("vector", lambda e, a=a, tof=tof, pb=pb: e.tensor_copy(
                    out=t_yg[:, a * SC + tof:a * SC + tof + LCH],
                    in_=pb[:, 640:768]), waits=("mm",), inc="vv")
        for a in range(4):
            s.do("gpsimd", lambda e, sc=sc, a=a: e.dma_start(
                out=yg1_o[a, :, sc * SC:(sc + 1) * SC],
                in_=t_yg[:, a * SC:(a + 1) * SC]), waits=("vv",), inc="gst", dma=True)

    s.wait("gpsimd", "gst")
    s.finalize()
    ctx.close()
    return nc


def prep_G(d, x2s):
    cst = host_consts()
    i = np.arange(LCH)
    masks = -((i[:, None] > i[None, :]).astype(np.float32))
    maski = (i[:, None] <= i[None, :]).astype(np.float32)
    idn128b = np.eye(128, dtype=np.float32).astype(BF)
    in_maps = []
    for c in range(NCORE):
        b, hg = c // 2, c % 2
        SL = slice(hg * 512, hg * 512 + 512)
        x2_full = np.concatenate([x2s[2 * b], x2s[2 * b + 1]], axis=2)
        cwm = np.zeros((128, 48), np.float32)
        cbm = np.zeros((128, 12), np.float32)
        for pi, nm in enumerate(("cq", "ck", "cv")):
            w = np.asarray(d[f"L1_attn_{nm}_w"])[SL]
            bb = np.asarray(d[f"L1_attn_{nm}_b"])[SL]
            for a in range(4):
                cwm[:, pi * 16 + a * 4: pi * 16 + (a + 1) * 4] = w[a * 128:(a + 1) * 128]
                cbm[:, pi * 4 + a] = bb[a * 128:(a + 1) * 128]
        m = {
            "x2T": x2_full,
            "wmod": d["L1_wmod"],
            "c_col": np.ascontiguousarray(np.asarray(d["L1_c"][b]).reshape(8, 128).T),
            "Wq": np.ascontiguousarray(np.asarray(d["L1_attn_Wq"])[:, SL]).reshape(8, 128, 512),
            "Wk": np.ascontiguousarray(np.asarray(d["L1_attn_Wk"])[:, SL]).reshape(8, 128, 512),
            "Wv": np.ascontiguousarray(np.asarray(d["L1_attn_Wv"])[:, SL]).reshape(8, 128, 512),
            "Wg": np.ascontiguousarray(np.asarray(d["L1_attn_Wg"])[:, SL]).reshape(8, 128, 512),
            "Wb": np.ascontiguousarray(np.asarray(d["L1_attn_Wb"])[:, hg * 8:hg * 8 + 8]).reshape(8, 128, 8),
            "cw": cwm, "cb": cbm,
            "nw_r": np.tile(np.asarray(d["L1_attn_nw"]).reshape(64), 8).reshape(1, 512),
            "ctrl": cst["tril_incl"], "cmasks": masks, "cmaski": maski,
            "cidn": cst["idn128"], "cidnb": np.vstack([cst["idn64b"], cst["idn64b"]]),
            "cidnb8": np.eye(8, dtype=np.float32).astype(BF),
            "cidn128b": idn128b,
            "cones": cst["ones128"], "conesb": cst["ones128"].astype(BF),
            "conesr": cst["onesr"], "conesrb": cst["onesr"].astype(BF),
            "cones11": cst["ones11"],
        }
        in_maps.append({k: np.ascontiguousarray(v) for k, v in m.items()})
    return in_maps



CAPO = 768


def build_H(dbg=False):
    nc = bass.Bass("TRN2", target_bir_lowering=False, debug=False)
    dram = lambda n, s, d, k: nc.dram_tensor(n, list(s), d, kind=k).ap()
    NTOK = T // 2
    NSC = NTOK // SC
    NTT = NTOK // 128
    yg1 = dram("yg1", (8, 128, NTOK), BF16, "ExternalInput")
    x2T = dram("x2T", (8, 128, NTOK), F32, "ExternalInput")
    Wo1 = dram("Wo1", (8, 128, 1024), F32, "ExternalInput")
    mods1 = dram("mods1", (128, 48), F32, "ExternalInput")
    rwT = dram("rwT", (8, 128, 8), F32, "ExternalInput")
    tokid = dram("tokid", (NTOK, 1), F32, "ExternalInput")
    iot8 = dram("iot8", (128, 8), F32, "ExternalInput")
    iotm8 = dram("iotm8", (128, 8), F32, "ExternalInput")
    ctsu = dram("ctsu", (128, 128), F32, "ExternalInput")
    cidn = dram("cidn", (128, 128), F32, "ExternalInput")
    cidnb_h = dram("cidnb_h", (128, 128), BF16, "ExternalInput")
    cones = dram("cones", (128, 1), F32, "ExternalInput")
    conesr = dram("conesr", (1, 128), F32, "ExternalInput")
    x3_o = dram("x3_o", (8, 128, NTOK), F32, "ExternalOutput")
    h4_o = dram("h4_o", (NTOK, 1024), BF16, "ExternalOutput")
    td_o = dram("td_o", (NTOK, 8), F32, "ExternalOutput")
    cnt_o = dram("cnt_o", (4, 8), F32, "ExternalOutput")
    idl_o = dram("idl_o", (8 * 2048, 1), F32, "ExternalOutput")
    if dbg:
        dbg_lg = dram("dbg_lg", (128, 8), F32, "ExternalOutput")
        dbg_pr = dram("dbg_pr", (128, 8), F32, "ExternalOutput")

    ctx = ExitStack()
    sb, ps = _mk(nc, ctx)
    s = Sched(nc, ctx)
    t_wo = sb("t_wo", (128, 8 * 1024), BF16)
    t_rw = sb("t_rw", (128, 8 * 8), F32)
    t_yg = sb("t_yg", (128, 8 * SC), BF16)
    t_x = sb("t_x", (128, 8 * SC), F32)
    t_x3 = sb("t_x3", (128, 8 * SC), F32)
    t_h4 = sb("t_h4", (128, 8 * SC), BF16)
    t_h4f = sb("t_h4f", (128, 8 * SC), F32)
    t_h4t = sb("t_h4t", (128, 1024), BF16)
    t_mods = sb("t_mods", (128, 48), F32)
    t_onepcb = sb("t_onepcb", (128, 8), F32)
    t_iot = sb("t_iot", (128, 8), F32)
    t_iotm = sb("t_iotm", (128, 8), F32)
    t_tsu = sb("t_tsu", (128, 128), F32)
    t_idn = sb("t_idn", (128, 128), F32)
    t_idnb2 = sb("t_idnb2", (128, 128), BF16)
    t_ones = sb("t_ones", (128, 1), F32)
    t_onesr = sb("t_onesr", (1, 128), F32)
    t_eps = sb("t_eps", (128, 1), F32)
    t_sq = sb("t_sq", (128, SC), F32)
    t_sq2 = sb("t_sq2", (128, SC), F32)
    t_row1 = sb("t_row1", (1, SC), F32)
    t_row2 = sb("t_row2", (1, SC), F32)
    t_row3 = sb("t_row3", (1, SC), F32)
    t_mub = sb("t_mub", (128, SC), F32)
    t_rsb = sb("t_rsb", (128, SC), F32)
    t_lg = sb("t_lg", (128, 8), F32)
    t_pr = sb("t_pr", (128, 8), F32)
    t_td = sb("t_td", (128, 8), F32)
    t_ind = sb("t_ind", (128, 3 * 8), F32)
    t_rk = sb("t_rk", (128, 3 * 8), F32)
    t_offs = sb("t_offs", (1, 3 * 8), F32)
    t_rowt = sb("t_rowt", (1, 3 * 8), F32)
    t_c1 = sb("t_c1", (128, 8), F32)
    t_c2 = sb("t_c2", (128, 8), F32)
    t_tid = sb("t_tid", (128, 1), F32)
    t_idx = sb("t_idx", (128, 1), U32)
    t_idxf = sb("t_idxf", (128, 1), F32)
    t_zro = sb("t_zro", (128, 128), F32)
    p0 = ps("p0", (128, 512), F32)
    p1 = ps("p1", (128, 512), F32)
    p2 = ps("p2", (128, 512), F32)
    p3 = ps("p3", (128, 512), F32)
    p2b = ps("p2b", (128, 1024), BF16)

    def ld(dst, src):
        s.do("sync", lambda e, d=dst, sr=src: e.dma_start(out=d, in_=sr), inc="ldr", dma=True)
    ld(t_mods[:], mods1); ld(t_iot[:], iot8); ld(t_iotm[:], iotm8)
    ld(t_tsu[:], ctsu); ld(t_idn[:], cidn)
    ld(t_idnb2[:], cidnb_h)
    ld(t_ones[:], cones); ld(t_onesr[:], conesr)
    s.do("gpsimd", lambda e: e.dma_start(
        out=t_wo[:].rearrange("p (k m) -> p k m", k=8), in_=Wo1.rearrange("k p m -> p k m")),
        inc="gld", dma=True)
    s.do("gpsimd", lambda e: e.dma_start(
        out=t_rw[:].rearrange("p (k m) -> p k m", k=8), in_=rwT.rearrange("k p m -> p k m")),
        inc="gld", dma=True)
    s.do("vector", lambda e: e.memset(t_eps[:], EPS_LN), inc="vv")
    s.do("vector", lambda e: e.memset(t_offs[:], 0.0), inc="vv")
    s.do("vector", lambda e: e.memset(t_zro[:], 0.0), inc="vv")
    s.do("gpsimd", lambda e: e.dma_start(
        out=idl_o.rearrange("(p n) o -> p (n o)", p=128), in_=t_zro[:]),
        waits=("vv",), inc="gst", dma=True)
    s.do("vector", lambda e: e.tensor_scalar(
        out=t_onepcb[:], in0=t_mods[:, 32:40], scalar1=1.0, scalar2=None, op0=ALU.add),
        waits=("ldr",), inc="vv")
    s.wait("tensor", "gld", "ldr", "vv")
    for sc in range(NSC):
        s.do("sync", lambda e, sc=sc: e.dma_start(
            out=t_yg[:].rearrange("p (a t) -> p a t", a=8),
            in_=yg1[:, :, sc * SC:(sc + 1) * SC].rearrange("a p t -> p a t")),
            waits=("vv", "mm", "gst", "ss"), inc="ldx", dma=True)
        s.do("sync", lambda e, sc=sc: e.dma_start(
            out=t_x[:].rearrange("p (a t) -> p a t", a=8),
            in_=x2T[:, :, sc * SC:(sc + 1) * SC].rearrange("a p t -> p a t")),
            inc="ldx", dma=True)
        s.wait("tensor", "ldx")
        s.wait("scalar", "ldx")
        s.wait("vector", "ldx")
        for mt in range(8):
            pp = (p0, p1)[mt % 2]
            for kk in range(8):
                s.do("tensor", lambda e, mt=mt, kk=kk, pp=pp: e.matmul(
                    out=pp[:, 0:SC], lhsT=t_wo[:, kk * 1024 + mt * 128:kk * 1024 + mt * 128 + 128],
                    rhs=t_yg[:, kk * SC:(kk + 1) * SC], start=(kk == 0), stop=(kk == 7)),
                    waits=("vv",) if kk == 0 else (), inc="mm" if kk == 7 else None)
            s.do("vector", lambda e, mt=mt, pp=pp: e.scalar_tensor_tensor(
                out=t_x3[:, mt * SC:(mt + 1) * SC], in0=pp[:, 0:SC],
                scalar=t_mods[:, 16 + mt:17 + mt], in1=t_x[:, mt * SC:(mt + 1) * SC],
                op0=ALU.mult, op1=ALU.add), waits=("mm",), inc="vv")
            s.do("gpsimd", lambda e, mt=mt, sc=sc: e.dma_start(
                out=x3_o[mt, :, sc * SC:(sc + 1) * SC], in_=t_x3[:, mt * SC:(mt + 1) * SC]),
                waits=("vv",), inc="gst", dma=True)
        for a in range(8):
            s.do("tensor", lambda e, a=a: e.matmul(
                out=p2[0:1, 0:SC], lhsT=t_ones[:], rhs=t_x3[:, a * SC:(a + 1) * SC],
                start=(a == 0), stop=(a == 7)), waits=("vv",) if a == 0 else (),
                inc="mm" if a == 7 else None)
        for a in range(8):
            buf = t_sq if a % 2 == 0 else t_sq2
            s.do("scalar", lambda e, a=a, buf=buf: e.activation(
                out=buf[:], in_=t_x3[:, a * SC:(a + 1) * SC], func=AF.Square),
                waits=("mm", "vv"), inc="ss")
            s.wait("tensor", "ss")
            s.do("tensor", lambda e, a=a, buf=buf: e.matmul(
                out=p3[0:1, 0:SC], lhsT=t_ones[:], rhs=buf[:],
                start=(a == 0), stop=(a == 7)), inc="mm")
        s.do("scalar", lambda e: e.activation(out=t_row1[:], in_=p2[0:1, 0:SC],
                                              func=AF.Copy, scale=1.0 / 1024), waits=("mm",), inc="ss")
        s.do("scalar", lambda e: e.activation(out=t_row2[:], in_=p3[0:1, 0:SC],
                                              func=AF.Copy, scale=1.0 / 1024), inc="ss")
        s.do("vector", lambda e: e.tensor_copy(out=t_row3[:], in_=t_row1[:]), waits=("ss",), inc="vv")
        s.do("vector", lambda e: e.tensor_tensor(
            out=t_row3[:], in0=t_row3[:], in1=t_row1[:], op=ALU.mult), inc="vv")
        s.do("vector", lambda e: e.tensor_tensor(
            out=t_row2[:], in0=t_row2[:], in1=t_row3[:], op=ALU.subtract), inc="vv")
        s.do("scalar", lambda e: e.activation(out=t_row3[:], in_=t_row2[:], func=AF.Sqrt,
                                              bias=t_eps[0:1, :]), waits=("vv",), inc="ss")
        s.do("vector", lambda e: e.reciprocal(out=t_row2[:], in_=t_row3[:]), waits=("ss",), inc="vv")
        s.wait("tensor", "vv")
        s.do("tensor", lambda e: e.matmul(out=p2[:, 0:SC], lhsT=t_onesr[:],
                                          rhs=t_row1[:], start=True, stop=True), inc="mm")
        s.do("tensor", lambda e: e.matmul(out=p3[:, 0:SC], lhsT=t_onesr[:],
                                          rhs=t_row2[:], start=True, stop=True), inc="mm")
        s.do("scalar", lambda e: e.copy(out=t_mub[:], in_=p2[:, 0:SC]), waits=("mm",), inc="ss")
        s.do("scalar", lambda e: e.copy(out=t_rsb[:], in_=p3[:, 0:SC]), inc="ss")
        for a in range(8):
            buf = t_sq if a % 2 == 0 else t_sq2
            s.do("vector", lambda e, a=a, buf=buf: e.tensor_tensor(
                out=buf[:], in0=t_x3[:, a * SC:(a + 1) * SC], in1=t_mub[:],
                op=ALU.subtract), waits=("ss",), inc="vv")
            s.do("vector", lambda e, a=a, buf=buf: e.tensor_tensor(
                out=buf[:], in0=buf[:], in1=t_rsb[:], op=ALU.mult), inc="vv")
            s.do("vector", lambda e, a=a, buf=buf: e.tensor_scalar(
                out=t_h4f[:, a * SC:(a + 1) * SC], in0=buf[:],
                scalar1=t_onepcb[:, a:a + 1], scalar2=t_mods[:, 24 + a:25 + a],
                op0=ALU.mult, op1=ALU.add), inc="vv")
            s.do("vector", lambda e, a=a: e.tensor_copy(
                out=t_h4[:, a * SC:(a + 1) * SC], in_=t_h4f[:, a * SC:(a + 1) * SC]),
                inc="vv")
        for tt in range(SC // 128):
            t0 = sc * SC + tt * 128
            for a in range(8):
                s.do("tensor", lambda e, a=a, tt=tt: e.transpose(
                    out=p2b[:, 0:128], in_=t_h4[:, a * SC + tt * 128:a * SC + (tt + 1) * 128],
                    identity=t_idnb2[:]), waits=("vv", "ss"), inc="mm")
                s.do("scalar", lambda e, a=a: e.copy(
                    out=t_h4t[:, a * 128:(a + 1) * 128], in_=p2b[:, 0:128]),
                    waits=("mm",), inc="ss")
            s.do("gpsimd", lambda e, t0=t0: e.dma_start(
                out=h4_o[t0:t0 + 128, :], in_=t_h4t[:]), waits=("ss",), inc="gst", dma=True)
            for a in range(8):
                s.do("tensor", lambda e, a=a, tt=tt: e.matmul(
                    out=p3[:, 0:8], lhsT=t_h4f[:, a * SC + tt * 128:a * SC + (tt + 1) * 128],
                    rhs=t_rw[:, a * 8:(a + 1) * 8], start=(a == 0), stop=(a == 7)),
                    waits=("ss", "vv") if a == 0 else (), inc="mm" if a == 7 else None)
            s.do("scalar", lambda e: e.copy(out=t_lg[:], in_=p3[:, 0:8]), waits=("mm",), inc="ss")
            if dbg and sc == 0 and tt == 0:
                s.do("gpsimd", lambda e: e.dma_start(out=dbg_lg, in_=t_lg[:]),
                     waits=("ss",), inc="gst", dma=True)
                s.wait("vector", "gst"); s.wait("scalar", "gst")
            s.do("vector", lambda e: e.tensor_reduce(
                out=t_c1[:, 0:1], in_=t_lg[:], axis=AX.X, op=ALU.max), waits=("ss",), inc="vv")
            s.do("vector", lambda e: e.tensor_scalar(
                out=t_lg[:], in0=t_lg[:], scalar1=t_c1[:, 0:1], scalar2=None,
                op0=ALU.subtract), inc="vv")
            s.do("scalar", lambda e: e.activation(out=t_pr[:], in_=t_lg[:], func=AF.Exp),
                 waits=("vv",), inc="ss")
            s.do("vector", lambda e: e.tensor_reduce(
                out=t_c1[:, 0:1], in_=t_pr[:], axis=AX.X, op=ALU.add), waits=("ss",), inc="vv")
            s.do("vector", lambda e: e.reciprocal(out=t_c1[:, 1:2], in_=t_c1[:, 0:1]), inc="vv")
            s.do("vector", lambda e: e.tensor_scalar(
                out=t_pr[:], in0=t_pr[:], scalar1=t_c1[:, 1:2], scalar2=None,
                op0=ALU.mult), inc="vv")
            if dbg and sc == 0 and tt == 0:
                s.do("gpsimd", lambda e: e.dma_start(out=dbg_pr, in_=t_pr[:]),
                     waits=("vv",), inc="gst", dma=True)
                s.wait("vector", "gst")
            s.do("vector", lambda e: e.tensor_reduce(
                out=t_c1[:, 2:3], in_=t_pr[:], axis=AX.X, op=ALU.max), inc="vv")
            s.do("vector", lambda e: e.tensor_scalar(
                out=t_c2[:], in0=t_pr[:], scalar1=t_c1[:, 2:3], scalar2=None,
                op0=ALU.is_equal), inc="vv")
            s.do("vector", lambda e: e.tensor_tensor(
                out=t_c2[:], in0=t_c2[:], in1=t_iotm[:], op=ALU.mult), inc="vv")
            s.do("vector", lambda e: e.tensor_scalar(
                out=t_c2[:], in0=t_c2[:], scalar1=1000.0, scalar2=None, op0=ALU.add), inc="vv")
            s.do("vector", lambda e: e.tensor_reduce(
                out=t_td[:, 0:1], in_=t_c2[:], axis=AX.X, op=ALU.min), inc="vv")
            s.do("vector", lambda e: e.tensor_scalar(
                out=t_c2[:], in0=t_iot[:], scalar1=t_td[:, 0:1], scalar2=None,
                op0=ALU.is_equal), inc="vv")
            s.do("vector", lambda e: e.scalar_tensor_tensor(
                out=t_c2[:], in0=t_c2[:], scalar=-1e30, in1=t_pr[:],
                op0=ALU.mult, op1=ALU.add), inc="vv")
            s.do("vector", lambda e: e.tensor_reduce(
                out=t_c1[:, 3:4], in_=t_c2[:], axis=AX.X, op=ALU.max), inc="vv")
            s.do("vector", lambda e: e.tensor_scalar(
                out=t_c2[:], in0=t_c2[:], scalar1=t_c1[:, 3:4], scalar2=None,
                op0=ALU.is_equal), inc="vv")
            s.do("vector", lambda e: e.tensor_tensor(
                out=t_c2[:], in0=t_c2[:], in1=t_iotm[:], op=ALU.mult), inc="vv")
            s.do("vector", lambda e: e.tensor_scalar(
                out=t_c2[:], in0=t_c2[:], scalar1=1000.0, scalar2=None, op0=ALU.add), inc="vv")
            s.do("vector", lambda e: e.tensor_reduce(
                out=t_td[:, 1:2], in_=t_c2[:], axis=AX.X, op=ALU.min), inc="vv")
            s.do("vector", lambda e: e.tensor_tensor(
                out=t_c1[:, 4:5], in0=t_c1[:, 2:3], in1=t_c1[:, 3:4], op=ALU.add), inc="vv")
            s.do("vector", lambda e: e.reciprocal(out=t_c1[:, 5:6], in_=t_c1[:, 4:5]), inc="vv")
            s.do("vector", lambda e: e.tensor_tensor(
                out=t_td[:, 2:3], in0=t_c1[:, 2:3], in1=t_c1[:, 5:6], op=ALU.mult), inc="vv")
            s.do("vector", lambda e: e.tensor_tensor(
                out=t_td[:, 3:4], in0=t_c1[:, 3:4], in1=t_c1[:, 5:6], op=ALU.mult), inc="vv")
            s.do("vector", lambda e: e.tensor_scalar(
                out=t_ind[:, 0:8], in0=t_iot[:], scalar1=t_td[:, 0:1], scalar2=None,
                op0=ALU.is_equal), inc="vv")
            s.do("vector", lambda e: e.tensor_scalar(
                out=t_ind[:, 8:16], in0=t_iot[:], scalar1=t_td[:, 1:2], scalar2=None,
                op0=ALU.is_equal), inc="vv")
            s.do("vector", lambda e: e.tensor_tensor(
                out=t_ind[:, 16:24], in0=t_ind[:, 0:8], in1=t_ind[:, 8:16], op=ALU.add), inc="vv")
            s.wait("tensor", "vv")
            for r3 in range(3):
                s.do("tensor", lambda e, r3=r3: e.matmul(
                    out=p2[:, 0:8], lhsT=t_tsu[:], rhs=t_ind[:, r3 * 8:(r3 + 1) * 8],
                    start=True, stop=False), waits=("ss",)),
                s.do("tensor", lambda e, r3=r3: e.matmul(
                    out=p2[:, 0:8], lhsT=t_onesr[:], rhs=t_offs[0:1, r3 * 8:(r3 + 1) * 8],
                    start=False, stop=True), inc="mm")
                s.do("scalar", lambda e, r3=r3: e.copy(
                    out=t_rk[:, r3 * 8:(r3 + 1) * 8], in_=p2[:, 0:8]), waits=("mm",), inc="ss")
                s.do("tensor", lambda e, r3=r3: e.matmul(
                    out=p3[0:1, 0:8], lhsT=t_ones[:], rhs=t_ind[:, r3 * 8:(r3 + 1) * 8],
                    start=True, stop=True), waits=("ss",), inc="mm")
                s.do("scalar", lambda e, r3=r3: e.copy(
                    out=t_rowt[0:1, r3 * 8:(r3 + 1) * 8], in_=p3[0:1, 0:8]),
                    waits=("mm",), inc="ss")
                s.do("vector", lambda e, r3=r3: e.tensor_tensor(
                    out=t_offs[0:1, r3 * 8:(r3 + 1) * 8], in0=t_offs[0:1, r3 * 8:(r3 + 1) * 8],
                    in1=t_rowt[0:1, r3 * 8:(r3 + 1) * 8], op=ALU.add), waits=("ss",), inc="vv")
            for col, rsec, isec in ((4, 0, 0), (5, 1, 1), (6, 2, 0), (7, 2, 1)):
                s.do("vector", lambda e, col=col, rsec=rsec, isec=isec: e.tensor_tensor(
                    out=t_c2[:], in0=t_rk[:, rsec * 8:(rsec + 1) * 8],
                    in1=t_ind[:, isec * 8:(isec + 1) * 8], op=ALU.mult), inc="vv")
                s.do("vector", lambda e, col=col: e.tensor_reduce(
                    out=t_td[:, col:col + 1], in_=t_c2[:], axis=AX.X, op=ALU.add), inc="vv")
            s.do("gpsimd", lambda e, t0=t0: e.dma_start(
                out=td_o[t0:t0 + 128, :], in_=t_td[:]), waits=("vv",), inc="gst", dma=True)
            s.do("sync", lambda e, t0=t0: e.dma_start(
                out=t_tid[:], in_=tokid[t0:t0 + 128, :]), waits=("gst",),
                inc="ldt", dma=True)
            s.wait("vector", "ldt")
            for kki in range(2):
                rmcol = 6 + kki
                s.do("vector", lambda e, kki=kki, rmcol=rmcol: e.scalar_tensor_tensor(
                    out=t_idxf[:], in0=t_td[:, kki:kki + 1], scalar=2048.0,
                    in1=t_td[:, rmcol:rmcol + 1], op0=ALU.mult, op1=ALU.add),
                    waits=("gst",), inc="vv")
                s.do("vector", lambda e: e.tensor_scalar(
                    out=t_idxf[:], in0=t_idxf[:], scalar1=0.0, scalar2=16383.0,
                    op0=ALU.max, op1=ALU.min), inc="vv")
                s.do("vector", lambda e: e.tensor_copy(out=t_idx[:], in_=t_idxf[:]),
                     inc="vv")
                s.do("gpsimd", lambda e: e.indirect_dma_start(
                    out=idl_o, out_offset=IndirectOffsetOnAxis(ap=t_idx[:], axis=0),
                    in_=t_tid[:], in_offset=None), waits=("vv", "gst"), inc="gst", dma=True)
    s.do("vector", lambda e: e.tensor_copy(out=t_rk[0:1, 0:24], in_=t_offs[0:1, 0:24]),
         waits=("gst",), inc="vv")
    for r3 in range(3):
        s.do("gpsimd", lambda e, r3=r3: e.dma_start(
            out=cnt_o[r3:r3 + 1, :], in_=t_rk[0:1, r3 * 8:(r3 + 1) * 8]),
            waits=("vv",), inc="gst", dma=True)
    s.do("gpsimd", lambda e: e.dma_start(out=cnt_o[3:4, :], in_=t_rk[0:1, 0:8]),
         inc="gst", dma=True)
    s.wait("gpsimd", "gst")
    s.finalize()
    ctx.close()
    return nc


def prep_H(d, yg1s, x2s, mods1s):
    cst = host_consts()
    i = np.arange(LCH)
    NTOK = T // 2
    in_maps = []
    for c in range(NCORE):
        b, half = c // 2, c % 2
        tsl = slice(half * NTOK, (half + 1) * NTOK)
        yg_full = np.concatenate([yg1s[2 * b], yg1s[2 * b + 1]], axis=0)
        x2_full = np.concatenate([x2s[2 * b], x2s[2 * b + 1]], axis=2)
        m = {
            "yg1": np.ascontiguousarray(yg_full[:, :, tsl]),
            "x2T": np.ascontiguousarray(x2_full[:, :, tsl]),
            "Wo1": np.asarray(d["L1_attn_Wo"]).reshape(8, 128, 1024),
            "mods1": mods1s[2 * b],
            "rwT": np.ascontiguousarray(np.asarray(d["L1_ffn_router_w"]).T).reshape(8, 128, 8),
            "tokid": (np.arange(NTOK, dtype=np.float32) + c * NTOK).reshape(NTOK, 1),
            "iot8": np.tile(np.arange(8, dtype=np.float32), (128, 1)),
            "iotm8": np.tile(np.arange(8, dtype=np.float32) - 1000.0, (128, 1)),
            "ctsu": (i[:, None] < i[None, :]).astype(np.float32),
            "cidn": cst["idn128"], "cidnb_h": cst["idn128"].astype(BF),
            "cones": cst["ones128"], "conesr": cst["onesr"],
        }
        in_maps.append({k: np.ascontiguousarray(v) for k, v in m.items()})
    return in_maps


def build_I():
    nc = bass.Bass("TRN2", target_bir_lowering=False, debug=False)
    dram = lambda n, s, d, k: nc.dram_tensor(n, list(s), d, kind=k).ap()
    NTOK = T // 2
    h4a = dram("h4a", (8 * NTOK, 1024), BF16, "ExternalInput")
    w1e = dram("w1e", (8, 128, 4096), F32, "ExternalInput")
    w2e = dram("w2e", (32, 128, 1024), F32, "ExternalInput")
    idla = dram("idla", (8, 8 * 2048), F32, "ExternalInput")
    eid_in = dram("eid_in", (1, 8), F32, "ExternalInput")
    iotacol = dram("iotacol", (128, 1), F32, "ExternalInput")
    conesr_i = dram("conesr_i", (1, 128), F32, "ExternalInput")
    cidnb_i = dram("cidnb_i", (128, 128), BF16, "ExternalInput")
    y_o = dram("y_o", (8 * CAPO, 1024), BF16, "ExternalOutput")

    ctx = ExitStack()
    sb, ps = _mk(nc, ctx)
    s = Sched(nc, ctx)
    t_w1 = sb("t_w1", (128, 8 * 4096), BF16)
    t_w2 = sb("t_w2", (128, 32 * 1024), BF16)
    t_rows = sb("t_rows", (128, 1024), BF16)
    t_rows2 = sb("t_rows2", (128, 1024), BF16)
    t_rT = sb("t_rT", (128, 1024), BF16)
    t_idnb_i = sb("t_idnb_i", (128, 128), BF16)
    t_h1 = sb("t_h1", (128, 4096), BF16)
    t_y = sb("t_y", (128, 1024), BF16)
    t_idxf = sb("t_idxf", (128, 1), F32)
    t_idx = sb("t_idx", (128, 1), U32)
    t_tid = sb("t_tid", (128, 1), F32)
    t_idxf2 = sb("t_idxf2", (128, 1), F32)
    t_idx2 = sb("t_idx2", (128, 1), U32)
    t_tid2 = sb("t_tid2", (128, 1), F32)
    t_eid = sb("t_eid", (1, 8), F32)
    t_iotacol = sb("t_iotacol", (128, 1), F32)
    t_eidc = sb("t_eidc", (128, 8), F32)
    t_onesr_i = sb("t_onesr_i", (1, 128), F32)
    p0 = ps("p0", (128, 512), F32)
    p1 = ps("p1", (128, 512), F32)
    p2b = ps("p2b", (128, 128), BF16)

    s.do("sync", lambda e: e.dma_start(out=t_eid[:], in_=eid_in), inc="ldr", dma=True)
    s.do("sync", lambda e: e.dma_start(out=t_iotacol[:], in_=iotacol), inc="ldr", dma=True)
    s.do("sync", lambda e: e.dma_start(out=t_onesr_i[:], in_=conesr_i), inc="ldr", dma=True)
    s.do("sync", lambda e: e.dma_start(out=t_idnb_i[:], in_=cidnb_i), inc="ldr", dma=True)
    s.do("gpsimd", lambda e: e.dma_start(
        out=t_w1[:].rearrange("p (k m) -> p k m", k=8), in_=w1e.rearrange("k p m -> p k m")),
        inc="gld", dma=True)
    s.do("gpsimd", lambda e: e.dma_start(
        out=t_w2[:].rearrange("p (k m) -> p k m", k=32), in_=w2e.rearrange("k p m -> p k m")),
        inc="gld", dma=True)
    s.do("vector", lambda e: e.memset(t_idxf[:], 0.0), inc="vv")
    s.do("vector", lambda e: e.memset(t_rows[:], 0.0), inc="vv")
    s.do("vector", lambda e: e.memset(t_rows2[:], 0.0), inc="vv")
    s.wait("tensor", "gld", "ldr")
    s.do("tensor", lambda e: e.matmul(out=p0[:, 0:8], lhsT=t_onesr_i[:], rhs=t_eid[:],
                                      start=True, stop=True), inc="mm")
    s.do("scalar", lambda e: e.copy(out=t_eidc[:], in_=p0[:, 0:8]), waits=("mm",), inc="ss")
    s.wait("vector", "ss")
    NB = 8 * CAPO // 128
    bufs = [(t_rows, t_idxf, t_idx, t_tid, "gidxA", "growA"),
            (t_rows2, t_idxf2, t_idx2, t_tid2, "gidxB", "growB")]

    def emit_gather(bt):
        rows, idxf, idx, tid, sg, sr = bufs[bt % 2]
        c_own = (bt * 128) // CAPO
        r0 = (bt * 128) % CAPO
        s.do("vector", lambda e, r0=r0, c_own=c_own, idxf=idxf: e.tensor_scalar(
            out=idxf[:], in0=t_iotacol[:], scalar1=1.0, scalar2=float(r0 + c_own * 16384),
            op0=ALU.mult, op1=ALU.add), waits=("gg", sg, sr), inc="vv")
        s.do("vector", lambda e, idxf=idxf: e.tensor_scalar(
            out=idxf[:], in0=idxf[:], scalar1=t_eidc[:, 0:1], scalar2=None,
            op0=ALU.add), inc="vv")
        s.do("vector", lambda e, idxf=idxf, idx=idx: e.tensor_copy(out=idx[:], in_=idxf[:]),
             inc="vv")
        s.do("gpsimd", lambda e, idx=idx, tid=tid: e.indirect_dma_start(
            out=tid[:], out_offset=None,
            in_=idla.rearrange("c n -> (c n)").rearrange("(n o) -> n o", o=1),
            in_offset=IndirectOffsetOnAxis(ap=idx[:], axis=0)),
            waits=("vv",), inc=sg, dma=True)
        s.do("vector", lambda e, idx=idx, tid=tid: e.tensor_copy(out=idx[:], in_=tid[:]),
             waits=(sg,), inc="vv")
        s.do("gpsimd", lambda e, rows=rows, idx=idx: e.indirect_dma_start(
            out=rows[:], out_offset=None, in_=h4a,
            in_offset=IndirectOffsetOnAxis(ap=idx[:], axis=0)),
            waits=("vv", "mm"), inc=sr, dma=True)

    emit_gather(0)
    for bt in range(NB):
        rows, _, _, _, _, sr = bufs[bt % 2]
        if bt + 1 < NB:
            emit_gather(bt + 1)
        s.wait("tensor", sr, "ss")
        for a in range(8):
            s.do("tensor", lambda e, a=a, rows=rows: e.transpose(
                out=p2b[:, 0:128], in_=rows[:, a * 128:(a + 1) * 128],
                identity=t_idnb_i[:]), waits=("ss",), inc="mm")
            s.do("scalar", lambda e, a=a: e.copy(
                out=t_rT[:, a * 128:(a + 1) * 128], in_=p2b[:, 0:128]),
                waits=("mm",), inc="ss")
        s.wait("tensor", "ss", "vv")
        for mt in range(32):
            pp = (p0, p1)[mt % 2]
            for kk in range(8):
                s.do("tensor", lambda e, mt=mt, kk=kk, pp=pp: e.matmul(
                    out=pp[:, 0:128], lhsT=t_w1[:, kk * 4096 + mt * 128:kk * 4096 + mt * 128 + 128],
                    rhs=t_rT[:, kk * 128:(kk + 1) * 128],
                    start=(kk == 0), stop=(kk == 7)),
                    waits=("ss",) if kk == 0 else (), inc="mm" if kk == 7 else None)
            s.do("scalar", lambda e, mt=mt, pp=pp: e.activation(
                out=t_h1[:, mt * 128:(mt + 1) * 128], in_=pp[:, 0:128], func=AF.Gelu),
                waits=("mm",), inc="ss")
        s.wait("tensor", "ss")
        for half in range(2):
            pp = (p0, p1)[half]
            for kk in range(32):
                s.do("tensor", lambda e, half=half, kk=kk, pp=pp: e.matmul(
                    out=pp[:, 0:512], lhsT=t_h1[:, kk * 128:(kk + 1) * 128],
                    rhs=t_w2[:, kk * 1024 + half * 512:kk * 1024 + half * 512 + 512],
                    start=(kk == 0), stop=(kk == 31)),
                    waits=("vv",) if kk == 0 and half == 0 else (),
                    inc="mm" if kk == 31 else None)
            s.do("vector", lambda e, half=half, pp=pp: e.tensor_copy(
                out=t_y[:, half * 512:(half + 1) * 512], in_=pp[:, 0:512]),
                waits=("mm", "gst") if half == 0 else ("mm",), inc="vv")
        s.do("gpsimd", lambda e, bt=bt: e.dma_start(
            out=y_o[bt * 128:(bt + 1) * 128, :], in_=t_y[:]), waits=("vv",), inc="gst", dma=True)
    s.wait("gpsimd", "gst")
    s.finalize()
    ctx.close()
    return nc


def prep_I(d, h4s, idls, core_e):
    NTOK = T // 2
    h4a = np.concatenate(h4s, axis=0)
    idla = np.stack([x.reshape(8 * 2048) for x in idls])
    iot = np.arange(128, dtype=np.float32).reshape(128, 1)
    in_maps = []
    for e in range(NCORE):
        m = {
            "h4a": h4a,
            "w1e": np.asarray(d["L1_ffn_w1"][e]).reshape(8, 128, 4096),
            "w2e": np.asarray(d["L1_ffn_w2"][e]).reshape(32, 128, 1024),
            "idla": idla,
            "eid_in": np.full((1, 8), float(e * 2048), np.float32),
            "iotacol": iot, "conesr_i": host_consts()["onesr"],
            "cidnb_i": np.eye(128, dtype=np.float32).astype(BF),
        }
        in_maps.append({k: np.ascontiguousarray(v) for k, v in m.items()})
    return in_maps


def build_J():
    nc = bass.Bass("TRN2", target_bir_lowering=False, debug=False)
    dram = lambda n, s, d, k: nc.dram_tensor(n, list(s), d, kind=k).ap()
    NTOK = T // 2
    NTT = NTOK // 128
    ysec = dram("ysec", (8 * CAPO, 1024), BF16, "ExternalInput")
    x3T = dram("x3T", (8, 128, NTOK), F32, "ExternalInput")
    td = dram("td", (NTOK, 8), F32, "ExternalInput")
    cnts = dram("cnts", (32, 8), F32, "ExternalInput")
    mods1 = dram("mods1", (128, 48), F32, "ExternalInput")
    iot8 = dram("iot8", (128, 8), F32, "ExternalInput")
    cidnb = dram("cidnb", (128, 128), BF16, "ExternalInput")
    conesr = dram("conesr", (1, 128), F32, "ExternalInput")
    coreid = dram("coreid", (1, 8), F32, "ExternalInput")
    out_o = dram("out_o", (8, 128, NTOK), F32, "ExternalOutput")

    ctx = ExitStack()
    sb, ps = _mk(nc, ctx)
    s = Sched(nc, ctx)
    t_td = sb("t_td", (128, 8), F32)
    t_y0 = sb("t_y0", (128, 1024), BF16)
    t_y1 = sb("t_y1", (128, 1024), BF16)
    t_x3 = sb("t_x3", (128, 8 * 128), F32)
    t_ctr = sb("t_ctr", (128, 1024), F32)
    t_ctrb = sb("t_ctrb", (128, 1024), BF16)
    t_cnt = sb("t_cnt", (32, 8), F32)
    t_offs = sb("t_offs", (1, 16), F32)
    t_offb = sb("t_offb", (128, 16), F32)
    t_mods = sb("t_mods", (128, 48), F32)
    t_iot = sb("t_iot", (128, 8), F32)
    t_idnb = sb("t_idnb", (128, 128), BF16)
    t_onesr = sb("t_onesr", (1, 128), F32)
    t_cid = sb("t_cid", (1, 8), F32)
    t_idxf = sb("t_idxf", (128, 1), F32)
    t_idx = sb("t_idx", (128, 1), U32)
    t_c1 = sb("t_c1", (128, 8), F32)
    t_c2 = sb("t_c2", (128, 8), F32)
    t_w01 = sb("t_w01", (128, 2), F32)
    p0 = ps("p0", (128, 512), F32)
    p2b = ps("p2b", (128, 1024), BF16)

    def ld(dst, src):
        s.do("sync", lambda e, d=dst, sr=src: e.dma_start(out=d, in_=sr), inc="ldr", dma=True)
    ld(t_cnt[:], cnts); ld(t_mods[:], mods1); ld(t_iot[:], iot8)
    ld(t_idnb[:], cidnb); ld(t_onesr[:], conesr); ld(t_cid[:], coreid)
    rowpat = dram("rowpat", (32, 2), F32, "ExternalInput")
    corein = dram("corein", (32, 1), F32, "ExternalInput")
    t_rp = sb("t_rp", (32, 2), F32)
    t_ci = sb("t_ci", (32, 1), F32)
    t_mk = sb("t_mk", (32, 2), F32)
    ld(t_rp[:], rowpat); ld(t_ci[:], corein)
    s.do("vector", lambda e: e.tensor_scalar(
        out=t_ci[:], in0=t_ci[:], scalar1=t_cid[0:1, 0:1].partition_broadcast(32) if False else 0.0,
        scalar2=None, op0=ALU.add), waits=("ldr",), inc="vv")
    s.wait("tensor", "ldr")
    s.do("tensor", lambda e: e.matmul(out=p0[0:32, 0:1], lhsT=t_onesr[0:1, 0:32],
                                      rhs=t_cid[0:1, 0:1], start=True, stop=True), inc="mm")
    s.do("scalar", lambda e: e.copy(out=t_idxf[0:32, 0:1], in_=p0[0:32, 0:1]),
         waits=("mm",), inc="ss")
    s.do("vector", lambda e: e.tensor_scalar(
        out=t_ci[:], in0=t_ci[:], scalar1=t_idxf[0:32, 0:1], scalar2=None,
        op0=ALU.is_lt), waits=("ss",), inc="vv")
    for kki in range(2):
        s.do("vector", lambda e, kki=kki: e.tensor_tensor(
            out=t_mk[:, kki:kki + 1], in0=t_rp[:, kki:kki + 1], in1=t_ci[:],
            op=ALU.mult), inc="vv")
    s.wait("tensor", "vv")
    for kki in range(2):
        s.do("tensor", lambda e, kki=kki: e.matmul(
            out=p0[0:1, 8 * kki:8 * kki + 8], lhsT=t_mk[:, kki:kki + 1], rhs=t_cnt[:],
            start=True, stop=True), inc="mm")
    s.do("scalar", lambda e: e.copy(out=t_offs[0:1, 0:16], in_=p0[0:1, 0:16]),
         waits=("mm",), inc="ss")
    s.wait("tensor", "ss")
    s.do("tensor", lambda e: e.matmul(out=p0[:, 0:16], lhsT=t_onesr[:], rhs=t_offs[0:1, 0:16],
                                      start=True, stop=True), inc="mm")
    s.do("scalar", lambda e: e.copy(out=t_offb[:], in_=p0[:, 0:16]), waits=("mm",), inc="ss")
    gb_done = False
    for tt in range(NTT):
        t0 = tt * 128
        s.do("sync", lambda e, t0=t0: e.dma_start(out=t_td[:], in_=td[t0:t0 + 128, :]),
             waits=("vv", "gst"), inc="ldx", dma=True)
        s.do("sync", lambda e, tt=tt: e.dma_start(
            out=t_x3[:].rearrange("p (a t) -> p a t", a=8),
            in_=x3T[:, :, tt * 128:(tt + 1) * 128].rearrange("a p t -> p a t")),
            inc="ldx", dma=True)
        s.wait("vector", "ldx", "ss")
        for kki in range(2):
            s.do("vector", lambda e, kki=kki: e.scalar_tensor_tensor(
                out=t_idxf[:], in0=t_td[:, kki:kki + 1], scalar=float(CAPO),
                in1=t_td[:, 6 + kki:7 + kki], op0=ALU.mult, op1=ALU.add),
                waits=("grow",), inc="vv")
            s.do("vector", lambda e: e.tensor_scalar(
                out=t_idxf[:], in0=t_idxf[:], scalar1=0.0, scalar2=float(8 * CAPO - 1),
                op0=ALU.max, op1=ALU.min), inc="vv")
            s.do("vector", lambda e: e.tensor_copy(out=t_idx[:], in_=t_idxf[:]), inc="vv")
            s.do("gpsimd", lambda e, kki=kki: e.indirect_dma_start(
                out=(t_y0 if kki == 0 else t_y1)[:], out_offset=None, in_=ysec,
                in_offset=IndirectOffsetOnAxis(ap=t_idx[:], axis=0)),
                waits=("vv", "gst"), inc="grow", dma=True)
            s.do("vector", lambda e, kki=kki: e.tensor_scalar(
                out=t_c1[:], in0=t_iot[:], scalar1=t_td[:, kki:kki + 1], scalar2=None,
                op0=ALU.is_equal), inc="vv")
            s.do("vector", lambda e, kki=kki: e.tensor_tensor(
                out=t_c1[:], in0=t_c1[:], in1=t_offb[:, 8 * kki:8 * kki + 8],
                op=ALU.mult), inc="vv")
            s.do("vector", lambda e: e.tensor_reduce(
                out=t_c2[:, 0:1], in_=t_c1[:], axis=AX.X, op=ALU.add), inc="vv")
            s.do("vector", lambda e, kki=kki: e.tensor_tensor(
                out=t_c2[:, 1:2], in0=t_td[:, 4 + kki:5 + kki], in1=t_c2[:, 0:1],
                op=ALU.add), inc="vv")
            s.do("vector", lambda e: e.tensor_scalar(
                out=t_c2[:, 2:3], in0=t_c2[:, 1:2], scalar1=float(int(1.25 * 4 * T / 8 * 2)),
                scalar2=None, op0=ALU.is_lt), inc="vv")
            s.do("vector", lambda e, kki=kki: e.tensor_scalar(
                out=t_c2[:, 3:4], in0=t_td[:, 6 + kki:7 + kki], scalar1=float(CAPO),
                scalar2=None, op0=ALU.is_lt), inc="vv")
            s.do("vector", lambda e: e.tensor_tensor(
                out=t_c2[:, 2:3], in0=t_c2[:, 2:3], in1=t_c2[:, 3:4], op=ALU.mult), inc="vv")
            s.do("vector", lambda e, kki=kki: e.tensor_tensor(
                out=t_w01[:, kki:kki + 1], in0=t_td[:, 2 + kki:3 + kki], in1=t_c2[:, 2:3],
                op=ALU.mult), inc="vv")
        s.wait("vector", "grow")
        s.do("vector", lambda e: e.tensor_scalar(
            out=t_ctr[:], in0=t_y0[:], scalar1=t_w01[:, 0:1], scalar2=None,
            op0=ALU.mult), inc="vv")
        s.do("vector", lambda e: e.scalar_tensor_tensor(
            out=t_ctr[:], in0=t_y1[:], scalar=t_w01[:, 1:2], in1=t_ctr[:],
            op0=ALU.mult, op1=ALU.add), inc="vv")
        s.do("gpsimd", lambda e: e.tensor_copy(out=t_ctrb[:], in_=t_ctr[:]),
             waits=("vv",), inc="gg")
        s.wait("tensor", "gg")
        for a in range(8):
            s.do("tensor", lambda e, a=a: e.transpose(
                out=p2b[:, 0:128], in_=t_ctrb[:, a * 128:(a + 1) * 128],
                identity=t_idnb[:]), waits=("vv",) if a == 0 else ("vv",), inc="mm")
            s.do("vector", lambda e, a=a, tt=tt: e.scalar_tensor_tensor(
                out=t_x3[:, a * 128:(a + 1) * 128], in0=p2b[:, 0:128],
                scalar=t_mods[:, 40 + a:41 + a], in1=t_x3[:, a * 128:(a + 1) * 128],
                op0=ALU.mult, op1=ALU.add), waits=("mm",), inc="vv")
            s.do("gpsimd", lambda e, a=a, tt=tt: e.dma_start(
                out=out_o[a, :, tt * 128:(tt + 1) * 128], in_=t_x3[:, a * 128:(a + 1) * 128]),
                waits=("vv",), inc="gst", dma=True)
    s.wait("gpsimd", "gst")
    s.finalize()
    ctx.close()
    return nc


def prep_J(d, ysecs_by_core, x3s, tds, cnts_all, mods1s):
    cst = host_consts()
    NTOK = T // 2
    rowpat = np.zeros((32, 2), np.float32)
    for p in range(32):
        if p % 4 == 0: rowpat[p, 0] = 1.0
        if p % 4 == 1: rowpat[p, 1] = 1.0
    corein = (np.arange(32) // 4).astype(np.float32).reshape(32, 1)
    cnts = np.concatenate([c.reshape(4, 8) for c in cnts_all], axis=0)
    in_maps = []
    for c in range(NCORE):
        b = c // 2
        m = {
            "ysec": ysecs_by_core[c],
            "x3T": x3s[c],
            "td": tds[c],
            "cnts": cnts,
            "mods1": mods1s[2 * b],
            "iot8": np.tile(np.arange(8, dtype=np.float32), (128, 1)),
            "cidnb": np.eye(128, dtype=np.float32).astype(BF),
            "conesr": cst["onesr"],
            "coreid": np.full((1, 8), float(c), np.float32),
            "rowpat": rowpat, "corein": corein,
        }
        in_maps.append({k: np.ascontiguousarray(v) for k, v in m.items()})
    return in_maps


def _run_launch(nc, in_maps):
    res = run_bass_kernel_spmd(nc, in_maps, core_ids=list(range(NCORE)))
    return [res.results[c] for c in range(NCORE)]


def kernel(**inputs):
    d = inputs
    NTOK = T // 2

    outsA = _run_launch(build_A(), prep_A(d))
    ygs = [o["yg_o"] for o in outsA]
    modss = [o["mods_o"] for o in outsA]

    outsB = _run_launch(build_B(), prep_B(d, ygs, modss))
    h2s = [o["h2_o"] for o in outsB]
    x1s = [o["x1_o"] for o in outsB]
    max1s = [o["max1_o"] for o in outsB]
    wqg = np.concatenate([o["wqg_o"].reshape(384, 1024) for o in outsB], axis=0)
    wqu = np.concatenate([o["wqu_o"].reshape(384, 1024) for o in outsB], axis=0)
    wqd = np.concatenate([o["wqd_o"] for o in outsB], axis=0)

    outsC = _run_launch(build_C(), prep_C(d, h2s, max1s, wqg, wqu))
    ups = [o["up_o"] for o in outsC]
    gates = [o["gate_o"] for o in outsC]
    maxus = [o["maxu_o"] for o in outsC]
    maxgs = [o["maxg_o"] for o in outsC]

    outsD = _run_launch(build_D(), prep_D(d, gates, maxgs, max1s))
    Ps = [o["P_o"] for o in outsD]
    maxps = [o["maxp_o"] for o in outsD]

    outsE = _run_launch(build_E(), prep_E(d, ups, Ps, maxus, maxps, max1s, maxgs))
    Os = [o["O_o"] for o in outsE]
    maxos = [o["maxo_o"] for o in outsE]

    outsF = _run_launch(build_F(), prep_F(d, Os, x1s, modss, wqd, maxos, maxus, maxps))
    x2s = [o["x2_o"] for o in outsF]

    outsG = _run_launch(build_G(), prep_G(d, x2s))
    yg1s = [o["yg1_o"] for o in outsG]
    mods1s = [o["mods1_o"] for o in outsG]

    outsH = _run_launch(build_H(), prep_H(d, yg1s, x2s, mods1s))
    x3s = [o["x3_o"] for o in outsH]
    h4s = [o["h4_o"] for o in outsH]
    tds = [o["td_o"] for o in outsH]
    cnts_all = [o["cnt_o"] for o in outsH]
    idls = [o["idl_o"] for o in outsH]

    outsI = _run_launch(build_I(), prep_I(d, h4s, idls, None))
    ysecs = [np.concatenate([outsI[e]["y_o"][c * CAPO:(c + 1) * CAPO]
                             for e in range(8)], axis=0) for c in range(NCORE)]

    outsJ = _run_launch(build_J(), prep_J(d, ysecs, x3s, tds, cnts_all, mods1s))
    out = np.concatenate([outsJ[c]["out_o"].reshape(1024, NTOK).T
                          for c in range(NCORE)], axis=0)
    return np.ascontiguousarray(out.reshape(4, T, 1024).astype(np.float32))
